# version 15
# baseline (speedup 1.0000x reference)
import contextlib
import numpy as np
import ml_dtypes
import concourse.bass as bass
import concourse.mybir as mybir
from concourse.bass_utils import run_bass_kernel_spmd

F32 = mybir.dt.float32
BF16 = mybir.dt.bfloat16
I32 = mybir.dt.int32
ALU = mybir.AluOpType
AF = mybir.ActivationFunctionType
AX = mybir.AxisListType

NCORES = 8
T = 2048
D = 1024
DCH = 8
NT = 4
FPAD = 640
FCH = 5
H0 = 3072
H1 = 4096

ALPHA = [4.89352455891786e-03, 6.37261928875436e-04, 1.48572235717979e-05,
         5.12229709037114e-08, -8.60467152213735e-11, 2.00018790482477e-13,
         -2.76076847742355e-16]
BETA = [4.89352518554385e-03, 2.26843463243900e-03, 1.18534705686654e-04,
        1.19825839466702e-06]
CLAMP = 7.99881172180175781
CUTOFF = 0.0004
SPLITC = 4097.0
TANH_SCRATCH = ["ah", "al", "bh", "bl", "e1", "e2", "tmp", "s",
                "xc", "x2", "x2h", "x2l", "p", "q", "y", "r1"]


def emit_tanh(v, sc, x, out, msk):
    ah, al, bh, bl, e1, e2, tmp, s_ = (sc[k] for k in
                                       ["ah", "al", "bh", "bl", "e1", "e2", "tmp", "s"])

    def split(src, hi, lo):
        v.tensor_scalar(tmp[:], src[:], SPLITC, None, ALU.mult)
        v.tensor_sub(lo[:], tmp[:], src[:])
        v.tensor_sub(hi[:], tmp[:], lo[:])
        v.tensor_sub(lo[:], src[:], hi[:])

    def fma(out_t, a, bhi, blo, bfull, c_ap=None, c_imm=None):
        split(a, ah, al)
        v.tensor_mul(s_[:], a[:], bfull[:])
        v.tensor_mul(e1[:], ah[:], bhi[:])
        v.tensor_sub(e1[:], e1[:], s_[:])
        v.tensor_mul(tmp[:], ah[:], blo[:])
        v.tensor_add(e1[:], e1[:], tmp[:])
        v.tensor_mul(tmp[:], al[:], bhi[:])
        v.tensor_add(e1[:], e1[:], tmp[:])
        v.tensor_mul(tmp[:], al[:], blo[:])
        v.tensor_add(e1[:], e1[:], tmp[:])
        if c_ap is None:
            v.tensor_scalar(e2[:], s_[:], float(c_imm), None, ALU.add)
        else:
            v.tensor_add(e2[:], s_[:], c_ap[:])
        v.tensor_sub(tmp[:], e2[:], s_[:])
        v.tensor_sub(ah[:], e2[:], tmp[:])
        v.tensor_sub(ah[:], s_[:], ah[:])
        if c_ap is None:
            v.tensor_scalar(tmp[:], tmp[:], -1.0, float(c_imm), ALU.mult, ALU.add)
        else:
            v.tensor_sub(tmp[:], c_ap[:], tmp[:])
        v.tensor_add(ah[:], ah[:], tmp[:])
        v.tensor_add(e1[:], e1[:], ah[:])
        v.tensor_add(out_t[:], e2[:], e1[:])

    xc, x2, x2h, x2l, p, q, y, r1 = (sc[k] for k in
                                     ["xc", "x2", "x2h", "x2l", "p", "q", "y", "r1"])
    v.tensor_scalar(xc[:], x[:], -CLAMP, CLAMP, ALU.max, ALU.min)
    v.tensor_mul(x2[:], xc[:], xc[:])
    split(x2, x2h, x2l)
    v.memset(p[:], float(np.float32(ALPHA[-1])))
    for coef in ALPHA[-2::-1]:
        fma(p, p, x2h, x2l, x2, c_imm=coef)
    v.tensor_mul(p[:], p[:], xc[:])
    v.memset(q[:], float(np.float32(BETA[-1])))
    for coef in BETA[-2::-1]:
        fma(q, q, x2h, x2l, x2, c_imm=coef)
    v.reciprocal(y[:], q[:])
    v.tensor_mul(r1[:], p[:], y[:])
    v.tensor_scalar(q[:], q[:], -1.0, None, ALU.mult)
    split(r1, bh, bl)
    v.tensor_copy(x2[:], r1[:])
    fma(out, q, bh, bl, x2, c_ap=p)
    split(y, bh, bl)
    v.tensor_copy(p[:], out[:])
    fma(out, p, bh, bl, y, c_ap=x2)
    v.tensor_scalar(tmp[:], x[:], -1.0, None, ALU.mult)
    v.tensor_max(tmp[:], tmp[:], x[:])
    v.tensor_single_scalar(msk[:], tmp[:], CUTOFF, ALU.is_lt)
    v.copy_predicated(out[:], msk[:], x[:])



class Sem:
    __slots__ = ("h", "n")

    def __init__(self, h):
        self.h = h
        self.n = 0

    def inc(self, amt=1):
        self.n += (1 if amt is None else amt)
        return (self.h, amt)


class Prog:
    ENGINES = ("sync", "tensor", "vector", "scalar", "gpsimd")

    def __init__(self, nc, stack):
        self.nc = nc
        self.stack = stack
        self.ops = []
        self._nsem = 0
        self._pending = {}

    def set_barrier(self, waits):
        for e in self.ENGINES:
            self._pending[e] = list(waits)

    def sem(self):
        h = self.stack.enter_context(self.nc.semaphore(f"s{self._nsem}"))
        self._nsem += 1
        return Sem(h)

    def op(self, eng, fn, waits=(), inc=None):
        w = list(waits)
        if eng in self._pending:
            w = self._pending.pop(eng) + w
        self.ops.append((eng, fn, [(s.h, val) for (s, val) in w if val > 0],
                         inc))

    def finalize(self):
        with self.nc.Block() as block:
            for engname in self.ENGINES:
                getattr(block, engname)(self._body(engname))

    def _body(self, engname):
        ops = [o for o in self.ops if o[0] == engname]

        def body(eng):
            for (_, fn, waits, inc) in ops:
                for (h, val) in waits:
                    if val > 0:
                        eng.wait_ge(h, val)
                ins = fn(eng)
                if inc is not None:
                    if ins is None:
                        ins = eng.engine_nop()
                    if inc[1] is None:
                        ins.then_inc(inc[0])
                    else:
                        ins.then_inc(*inc)
        return body


class SyncedEngine:

    def __init__(self, eng, sem):
        self._eng = eng
        self._sem = sem

    def wait_ge(self, *a):
        return self._eng.wait_ge(*a)

    def __getattr__(self, name):
        fn = getattr(self._eng, name)

        def wrapped(*a, **k):
            ins = fn(*a, **k)
            self._sem.n += 1
            ins.then_inc(self._sem.h, 1)
            self._eng.wait_ge(self._sem.h, self._sem.n)
            return ins
        return wrapped



def build_launch_A():
    nc = bass.Bass("TRN2", target_bir_lowering=False)
    x_in = nc.dram_tensor("x_fm", [D, T], F32, kind="ExternalInput").ap()
    gwT_in = nc.dram_tensor("gwT", [D, D], F32, kind="ExternalInput").ap()
    a_in = nc.dram_tensor("a_v", [128, 8], F32, kind="ExternalInput").ap()
    b_in = nc.dram_tensor("b_v", [128, 8], F32, kind="ExternalInput").ap()
    gb_in = nc.dram_tensor("gb_v", [128, 8], F32, kind="ExternalInput").ap()
    csel_in = nc.dram_tensor("csel_v", [128, 4], F32, kind="ExternalInput").ap()
    cb_in = nc.dram_tensor("cb_v", [128, 4], F32, kind="ExternalInput").ap()
    v_out = nc.dram_tensor("v_out", [D, T], F32, kind="ExternalOutput").ap()
    pows_out = nc.dram_tensor("pows_out", [D, T], F32, kind="ExternalOutput").ap()
    vtot_out = nc.dram_tensor("vtot_out", [128, 8], F32, kind="ExternalOutput").ap()
    dbg_out = nc.dram_tensor("dbg_out", [128, 24], F32, kind="ExternalOutput").ap()

    with contextlib.ExitStack() as stack:
        sb = lambda name, shape, dt=F32: stack.enter_context(
            nc.sbuf_tensor(name, shape, dt))
        ps = lambda name, shape: stack.enter_context(
            nc.psum_tensor(name, shape, F32))

        xf = sb("xf", [128, DCH * T])
        xb = sb("xb", [128, DCH * T], BF16)
        gw = sb("gw", [128, DCH * D], BF16)
        a_t = sb("a_t", [128, 8])
        b_t = sb("b_t", [128, 8])
        gb_t = sb("gb_t", [128, 8])
        csel_t = sb("csel_t", [128, 4])
        cb_t = sb("cb_t", [128, 4])
        sc = {n: sb("sc_" + n, [128, 8]) for n in TANH_SCRATCH}
        msk = sb("msk", [128, 8], I32)
        a_raw = sb("a_raw", [128, 8])
        asafe = sb("asafe", [128, 8])
        carry = sb("carry", [128, 8])
        zeros = sb("zeros", [128, T])
        powst = [sb(f"powst{i}", [128, T]) for i in range(2)]
        gstrip = [sb(f"gstrip{i}", [128, T]) for i in range(2)]
        vstrip = [sb(f"vstrip{i}", [128, T]) for i in range(2)]
        wk1 = sb("wk1", [128, T])
        wk2 = sb("wk2", [128, T])
        vtot = sb("vtot", [128, 8])
        psum = [ps(f"pA{i}", [128, 512]) for i in range(8)]

        prog = Prog(nc, stack)
        ld = prog.sem()
        vec = prog.sem()
        scl = prog.sem()
        mms = prog.sem()
        sto = prog.sem()
        nsync = prog.sem()

        for k in range(DCH):
            prog.op("sync", lambda s, k=k: s.dma_start(
                out=xf[:, k * T:(k + 1) * T], in_=x_in[k * 128:(k + 1) * 128, :]),
                inc=ld.inc(16))
            prog.op("gpsimd", lambda g, k=k: g.dma_start(
                out=xb[:, k * T:(k + 1) * T], in_=x_in[k * 128:(k + 1) * 128, :]),
                inc=ld.inc(16))
            prog.op("gpsimd", lambda g, k=k: g.dma_start(
                out=gw[:, k * D:(k + 1) * D], in_=gwT_in[k * 128:(k + 1) * 128, :]),
                inc=ld.inc(16))
        for (dst, src) in [(a_t, a_in), (b_t, b_in), (gb_t, gb_in),
                           (csel_t, csel_in), (cb_t, cb_in)]:
            prog.op("sync", lambda s, dst=dst, src=src: s.dma_start(
                out=dst[:], in_=src[:]), inc=ld.inc(16))
        lded = ld.n

        prog.op("vector", lambda v: v.memset(zeros[:], 0.0))
        def _tanh(v):
            emit_tanh(SyncedEngine(v, nsync), sc, a_t, a_raw, msk)
            return v.engine_nop()
        prog.op("vector", _tanh, waits=[(ld, lded)])

        def _asafe(v0):
            v = SyncedEngine(v0, nsync)
            t1, t2 = sc["tmp"], sc["s"]
            v.tensor_scalar(t1[:], a_raw[:], -1.0, None, ALU.mult)
            v.tensor_max(t1[:], t1[:], a_raw[:])
            v.tensor_scalar(t1[:], t1[:], 1e-4, None, ALU.max)
            v.tensor_single_scalar(t2[:], a_raw[:], 0.0, ALU.is_lt)
            v.tensor_scalar(t2[:], t2[:], -2.0, 1.0, ALU.mult, ALU.add)
            v.tensor_mul(asafe[:], t1[:], t2[:])
            return v0.engine_nop()
        prog.op("vector", _asafe)

        def _carries(v0):
            v = SyncedEngine(v0, nsync)
            p2, p4, p6, t1 = sc["p"], sc["q"], sc["y"], sc["tmp"]
            v.tensor_copy(p2[:], asafe[:])
            for _ in range(11):
                v.tensor_mul(p2[:], p2[:], p2[:])
            v.tensor_mul(p4[:], p2[:], p2[:])
            v.tensor_mul(p6[:], p4[:], p2[:])
            v.tensor_scalar(carry[:], zeros[:, 0:8], csel_t[:, 0:1], None, ALU.add)
            v.scalar_tensor_tensor(t1[:], p2[:], csel_t[:, 1:2], carry[:],
                                   ALU.mult, ALU.add)
            v.scalar_tensor_tensor(carry[:], p4[:], csel_t[:, 2:3], t1[:],
                                   ALU.mult, ALU.add)
            v.scalar_tensor_tensor(t1[:], p6[:], csel_t[:, 3:4], carry[:],
                                   ALU.mult, ALU.add)
            v.tensor_copy(carry[:], t1[:])
            return v0.engine_nop()
        prog.op("vector", _carries, inc=vec.inc())
        vec_init = vec.n

        vec_pow = [0] * DCH; vec_v = [0] * DCH
        scl_evac = [0] * DCH; scl_rch = [0] * DCH
        mm_done = [0] * DCH
        sto_pow = [0] * DCH; sto_v = [0] * DCH

        for m in range(DCH):
            par = m % 2
            pw, gs, vs = powst[par], gstrip[par], vstrip[par]
            xf_m = xf[:, m * T:(m + 1) * T]

            w = [(vec, vec_init)]
            if m >= 2:
                w += [(scl, scl_rch[m - 2]), (sto, sto_pow[m - 2])]
            def _powscan(v, pw=pw, m=m):
                v.tensor_scalar(wk1[:], zeros[:], asafe[:, m:m + 1], None, ALU.add)
                v.tensor_copy(pw[:, 0:1], carry[:, m:m + 1])
                return v.tensor_tensor_scan(
                    pw[:, 1:T], wk1[:, 1:T], zeros[:, 1:T],
                    carry[:, m:m + 1], ALU.mult, ALU.add)
            prog.op("vector", _powscan, waits=w, inc=vec.inc())
            vec_pow[m] = vec.n

            w = [(ld, lded)]
            if m >= 2:
                w.append((scl, scl_evac[m - 2]))
            def _gatemm(t, m=m, par=par):
                last = None
                for n in range(NT):
                    bank = psum[4 * par + n]
                    for k in range(DCH):
                        last = t.matmul(
                            bank[:],
                            gw[:, k * D + m * 128: k * D + (m + 1) * 128],
                            xb[:, k * T + n * 512: k * T + (n + 1) * 512],
                            start=(k == 0), stop=(k == DCH - 1))
                return last
            prog.op("tensor", _gatemm, waits=w, inc=mms.inc())
            mm_done[m] = mms.n

            for n in range(NT):
                w = []
                if n == 0:
                    w.append((mms, mm_done[m]))
                    if m >= 2:
                        w.append((vec, vec_v[m - 2]))
                prog.op("scalar", lambda s, n=n, par=par, gs=gs, m=m: s.activation(
                    gs[:, n * 512:(n + 1) * 512], psum[4 * par + n][:],
                    AF.Sigmoid, bias=gb_t[:, m:m + 1]),
                    waits=w, inc=scl.inc() if n == NT - 1 else None)
            scl_evac[m] = scl.n

            def _rchain(s, pw=pw):
                s.activation(wk2[:], pw[:], AF.Abs, bias=cb_t[:, 0:1])
                s.activation(wk2[:], wk2[:], AF.Ln)
                s.activation(wk2[:], wk2[:], AF.Exp, scale=-1.0)
                return s.sign(wk1[:], pw[:], bias=cb_t[:, 0:1])
            prog.op("scalar", _rchain, waits=[(vec, vec_pow[m])], inc=scl.inc())
            scl_rch[m] = scl.n

            w = [(scl, scl_rch[m])]
            if m >= 2:
                w.append((sto, sto_v[m - 2]))
            def _vchain(v, gs=gs, vs=vs, m=m, xf_m=xf_m):
                v.tensor_mul(wk2[:], wk2[:], wk1[:])
                v.scalar_tensor_tensor(wk1[:], gs[:], b_t[:, m:m + 1], xf_m,
                                       ALU.mult, ALU.mult)
                return v.scalar_tensor_tensor(
                    vs[:], wk1[:], 1.0, wk2[:], ALU.mult, ALU.mult,
                    accum_out=vtot[:, m:m + 1])
            prog.op("vector", _vchain, waits=w, inc=vec.inc())
            vec_v[m] = vec.n

            prog.op("sync", lambda s, pw=pw, m=m: s.dma_start(
                out=pows_out[m * 128:(m + 1) * 128, :], in_=pw[:]),
                waits=[(vec, vec_pow[m])], inc=sto.inc(16))
            sto_pow[m] = sto.n
            prog.op("sync", lambda s, vs=vs, m=m: s.dma_start(
                out=v_out[m * 128:(m + 1) * 128, :], in_=vs[:]),
                waits=[(vec, vec_v[m])], inc=sto.inc(16))
            sto_v[m] = sto.n

        prog.op("sync", lambda s: s.dma_start(out=vtot_out[:], in_=vtot[:]),
                waits=[(vec, vec.n)], inc=sto.inc(16))
        prog.op("sync", lambda s: s.dma_start(out=dbg_out[:, 0:8], in_=a_raw[:]),
                waits=[(vec, vec.n)], inc=sto.inc(16))
        prog.op("sync", lambda s: s.dma_start(out=dbg_out[:, 8:16], in_=asafe[:]),
                waits=[(vec, vec.n)], inc=sto.inc(16))
        prog.op("sync", lambda s: s.dma_start(out=dbg_out[:, 16:24], in_=carry[:]),
                waits=[(vec, vec.n)], inc=sto.inc(16))
        prog.op("sync", lambda s: None, waits=[(sto, sto.n)])

        prog.finalize()
    return nc



def emit_carry_combine(prog, eng_dma, agt, maskt, carryt, nsync, src_dram, nrows,
                       ld, vec, waits=()):
    n8 = nrows * 8
    for r in range(8):
        prog.op(eng_dma, lambda e, r=r: e.dma_start(
            out=agt[:, r:n8:8],
            in_=src_dram[r:r + 1, :].rearrange("one (k p) -> p (one k)", p=128),
            allow_slow_non_contiguous=True), waits=waits, inc=ld.inc(16))
    lded = ld.n

    def _comb(v0):
        v = SyncedEngine(v0, nsync)
        v.tensor_mul(agt[:], agt[:], maskt[:])
        v.tensor_reduce(carryt[:, 0:nrows],
                        agt.rearrange("p (k r) -> p k r", r=8),
                        AX.X, ALU.add)
        return v0.engine_nop()
    prog.op("vector", _comb, waits=[(ld, lded)], inc=vec.inc())
    return vec.n


def emit_layernorm(prog, st, eps_col, milestones):
    xf, xb, wkS, psum = st["xf"], st["xb"], st["wkS"], st["psum"]
    su, sq = st["rowA"], st["rowB"]
    rstd, mr = sq, su
    ones_col, ones_row, cb_t = st["ones_col"], st["ones_row"], st["cb_t"]
    vec, scl, mms, nsync = st["vec"], st["scl"], st["mms"], st["nsync"]
    zv = milestones["z_ready_vec"]
    zs = milestones.get("z_ready_scl", 0)
    banks_free_scl = milestones.get("banks_free_scl", 0)
    banks_free_vec = milestones.get("banks_free_vec", 0)

    scl_sq = [0] * DCH
    mms_stat = [0] * DCH
    for m in range(DCH):
        par = m % 2
        w = [(vec, zv)]
        if zs:
            w.append((scl, zs))
        if m >= 2:
            w.append((mms, mms_stat[m - 2]))
        prog.op("scalar", lambda s, m=m, par=par: s.activation(
            wkS[par][:], xf[:, m * T:(m + 1) * T], AF.Square),
            waits=w, inc=scl.inc())
        scl_sq[m] = scl.n
        w = [(scl, scl_sq[m])]
        if m == 0 and banks_free_scl:
            w.append((scl, banks_free_scl))
        if m == 0 and banks_free_vec:
            w.append((vec, banks_free_vec))
        def _stat(t, m=m, par=par):
            last = None
            for n in range(NT):
                last = t.matmul(psum[n][0:1, :], ones_col[:],
                                xf[:, m * T + n * 512: m * T + (n + 1) * 512],
                                start=(m == 0), stop=(m == DCH - 1))
                last = t.matmul(psum[4 + n][0:1, :], ones_col[:],
                                wkS[par][:, n * 512:(n + 1) * 512],
                                start=(m == 0), stop=(m == DCH - 1))
            return last
        prog.op("tensor", _stat, waits=w, inc=mms.inc())
        mms_stat[m] = mms.n

    def _rows1(v0):
        v = SyncedEngine(v0, nsync)
        for n in range(NT):
            v.tensor_copy(su[:, n * 512:(n + 1) * 512], psum[n][0:1, :])
            v.tensor_copy(sq[:, n * 512:(n + 1) * 512], psum[4 + n][0:1, :])
        v.tensor_scalar(su[:, :], su[:, :], 1.0 / D, None, ALU.mult)
        v.tensor_scalar(sq[:, :], sq[:, :], 1.0 / D, None, ALU.mult)
        tmp_row = wkS[0][0:1, 0:T]
        v.tensor_mul(tmp_row, su[:, :], su[:, :])
        v.tensor_sub(sq[:, :], sq[:, :], tmp_row)
        return v0.engine_nop()
    prog.op("vector", _rows1, waits=[(mms, mms_stat[DCH - 1])], inc=vec.inc())
    v_rows1 = vec.n
    prog.op("scalar", lambda s: s.activation(
        sq[:, :], sq[:, :], AF.Sqrt, bias=cb_t[0:1, eps_col:eps_col + 1]),
        waits=[(vec, v_rows1)], inc=scl.inc())
    s_sqrt = scl.n

    def _rows2(v0):
        v = SyncedEngine(v0, nsync)
        v.reciprocal(sq[:, :], sq[:, :])
        v.tensor_mul(su[:, :], su[:, :], sq[:, :])
        return v0.engine_nop()
    prog.op("vector", _rows2, waits=[(scl, s_sqrt)], inc=vec.inc())
    v_rows2 = vec.n

    def _bcast(t):
        last = None
        for n in range(NT):
            last = t.matmul(psum[n][:], ones_row[:],
                            rstd[:, n * 512:(n + 1) * 512], start=True, stop=True)
            last = t.matmul(psum[4 + n][:], ones_row[:],
                            mr[:, n * 512:(n + 1) * 512], start=True, stop=True)
        return last
    prog.op("tensor", _bcast, waits=[(vec, v_rows2)], inc=mms.inc())
    m_bcast = mms.n

    entry_sto = milestones.get("entry_sto", 0)
    vec_apply = [0] * DCH
    for m in range(DCH):
        def _apply(v, m=m):
            last = None
            for n in range(NT):
                sl = slice(m * T + n * 512, m * T + (n + 1) * 512)
                v.tensor_tensor(wkS[0][:, n * 512:(n + 1) * 512],
                                xf[:, sl], psum[n][:], ALU.mult)
                last = v.tensor_tensor(xf[:, sl],
                                       wkS[0][:, n * 512:(n + 1) * 512],
                                       psum[4 + n][:], ALU.subtract)
            return last
        w_ap = [(mms, m_bcast)]
        if m == 0 and entry_sto:
            w_ap.append((st.get("sto"), entry_sto))
        prog.op("vector", _apply, waits=w_ap, inc=vec.inc())
        vec_apply[m] = vec.n
        prog.op("scalar", lambda s, m=m: s.activation(
            xb[:, m * T:(m + 1) * T], xf[:, m * T:(m + 1) * T], AF.Copy),
            waits=[(vec, vec_apply[m])], inc=scl.inc())
    return {"vec_apply": vec.n, "scl_shadow": scl.n, "mms_bcast": m_bcast,
            "banks_busy_until_vec": vec.n}



def build_launch_B():
    nc = bass.Bass("TRN2", target_bir_lowering=False)
    dram = lambda n, sh, dt=F32, kind="ExternalInput": nc.dram_tensor(
        n, sh, dt, kind=kind).ap()
    x_in = dram("x_fm", [D, T])
    v_in = dram("v_in", [D, T])
    pows_in = dram("pows_in", [D, T])
    vta_in = dram("vtotall", [8, D])
    mask0_in = dram("mask0", [128, 64])
    c_in = dram("c_v", [128, 8])
    d_in = dram("d_v", [128, 8])
    cb_in = dram("cb_v", [128, 4])
    wg_in = dram("wgT_i", [D, H0], I32)
    wu_in = dram("wuT_i", [D, H0], I32)
    wd_in = dram("wdT_i", [H0, D], I32)
    w_qkv_in = {"q": dram("wq_r", [D, D]), "k": dram("wk_r", [D, D]),
                "v": dram("wv_r", [D, D])}
    cmat_in = dram("cmat", [D, FPAD], BF16)
    smat_in = dram("smat", [D, FPAD], BF16)
    y1_out = dram("y1_out", [D, T], F32, "ExternalOutput")

    pr_out = {"re": dram("prre_out", [FPAD, T], BF16, "ExternalOutput"),
              "im": dram("prim_out", [FPAD, T], BF16, "ExternalOutput")}
    nq_out = {"re": dram("nqre_out", [FPAD, T], BF16, "ExternalOutput"),
              "im": dram("nqim_out", [FPAD, T], BF16, "ExternalOutput")}
    ptot_out = dram("ptot_out", [128, 10], F32, "ExternalOutput")
    mscr = {}
    for nm in ["qre", "qim", "kre", "kim", "vre", "vim"]:
        mscr[nm] = nc.dram_tensor("mscr_" + nm, [D, FPAD], BF16,
                                  kind="Internal").ap()

    with contextlib.ExitStack() as stack:
        sb = lambda name, shape, dt=F32: stack.enter_context(
            nc.sbuf_tensor(name, shape, dt))
        ps = lambda name, shape: stack.enter_context(
            nc.psum_tensor(name, shape, F32))

        xb = sb("xb", [128, DCH * T], BF16)
        cb_t = sb("cb_t", [128, 4])
        c_t = sb("c_t", [128, 8])
        d1_t = sb("d1_t", [128, 8])
        ones_col = sb("ones_col", [128, 1])
        ones_row = sb("ones_row", [1, 128])
        rowA = sb("rowA", [1, T])
        rowB = sb("rowB", [1, T])
        wkS = [sb(f"wkS{i}", [128, T]) for i in range(2)]
        psum = [ps(f"pB{i}", [128, 512]) for i in range(8)]

        prog = Prog(nc, stack)
        ld = prog.sem(); vec = prog.sem(); scl = prog.sem()
        mms = prog.sem(); sto = prog.sem(); nsync = prog.sem()

        st = {"xb": xb, "wkS": wkS, "psum": psum, "rowA": rowA, "rowB": rowB,
              "ones_col": ones_col, "ones_row": ones_row,
              "cb_t": cb_t, "vec": vec, "scl": scl, "mms": mms, "nsync": nsync,
              "sto": sto}

        p_xf = contextlib.ExitStack()
        xf = p_xf.enter_context(nc.sbuf_tensor("xf", [128, DCH * T], F32))
        st["xf"] = xf

        p1 = contextlib.ExitStack()
        sb1 = lambda name, shape, dt=F32: p1.enter_context(
            nc.sbuf_tensor(name, shape, dt))
        vst = [sb1(f"vst{i}", [128, T]) for i in range(2)]
        pwst = [sb1(f"pwst{i}", [128, T]) for i in range(2)]
        agt = sb1("agt", [128, 64])
        maskt = sb1("maskt", [128, 64])
        carry0 = sb1("carry0", [128, 8])

        for k in range(DCH):
            prog.op("sync", lambda s, k=k: s.dma_start(
                out=xf[:, k * T:(k + 1) * T], in_=x_in[k * 128:(k + 1) * 128, :]),
                inc=ld.inc(16))
        for (dst, srcd) in [(cb_t, cb_in), (c_t, c_in), (d1_t, d_in),
                            (maskt, mask0_in)]:
            prog.op("sync", lambda s, dst=dst, srcd=srcd: s.dma_start(
                out=dst[:], in_=srcd[:]), inc=ld.inc(16))
        lded0 = ld.n

        prog.op("vector", lambda v: v.memset(ones_col[:], 1.0))
        prog.op("vector", lambda v: v.memset(ones_row[:], 1.0))
        prog.op("vector", lambda v: v.memset(wkS[1][:], 0.0))

        def _d1(v0):
            v = SyncedEngine(v0, nsync)
            v.tensor_scalar(d1_t[:], d1_t[:], 1.0, None, ALU.add)
            return v0.engine_nop()
        prog.op("vector", _d1, waits=[(ld, lded0)], inc=vec.inc())

        cb_vec = emit_carry_combine(prog, "gpsimd", agt, maskt, carry0, nsync,
                                    vta_in, 8, ld, vec, waits=[(ld, lded0)])

        vec_z0 = [0] * DCH
        for m in range(DCH):
            par = m % 2
            vs, pw = vst[par], pwst[par]
            w = []
            if m >= 2:
                w.append((vec, vec_z0[m - 2]))
            prog.op("gpsimd", lambda g, m=m, vs=vs: g.dma_start(
                out=vs[:], in_=v_in[m * 128:(m + 1) * 128, :]),
                waits=w, inc=ld.inc(16))
            prog.op("gpsimd", lambda g, m=m, pw=pw: g.dma_start(
                out=pw[:], in_=pows_in[m * 128:(m + 1) * 128, :]),
                inc=ld.inc(16))
            ld_m = ld.n

            def _scan_z0(v, m=m, vs=vs, pw=pw):
                v.tensor_tensor_scan(vs[:], vs[:], wkS[1][:, 0:T],
                                     carry0[:, m:m + 1], ALU.add, ALU.add)
                v.scalar_tensor_tensor(wkS[0][:], vs[:], c_t[:, m:m + 1], pw[:],
                                       ALU.mult, ALU.mult)
                return v.scalar_tensor_tensor(
                    xf[:, m * T:(m + 1) * T], xf[:, m * T:(m + 1) * T],
                    d1_t[:, m:m + 1], wkS[0][:], ALU.mult, ALU.add)
            prog.op("vector", _scan_z0, waits=[(ld, ld_m), (vec, cb_vec)],
                    inc=vec.inc())
            vec_z0[m] = vec.n

        ms0 = emit_layernorm(prog, st, 2, {"z_ready_vec": vec_z0[DCH - 1]})
        p1.close()

        p3 = contextlib.ExitStack()
        sb3 = lambda name, shape, dt=F32: p3.enter_context(
            nc.sbuf_tensor(name, shape, dt))
        wblk = sb3("wblk", [128, DCH * 1024], BF16)
        gsil = sb3("gsil", [128, 8 * 512], BF16)
        hh = sb3("hh", [128, 24 * 512], BF16)
        wdb = [sb3(f"wdb{i}", [128, 1024], BF16) for i in range(2)]

        vec_banks_free = ms0["banks_busy_until_vec"]
        scl_entry = ms0["scl_shadow"]
        mm_wblk_free = 0
        vec_z1_all = 0
        for qt in range(4):
            tk = qt * 512
            vec_hh = 0
            for blk in range(3):
                for k in range(DCH):
                    w = [(scl, scl_entry), (mms, mm_wblk_free)] if k == 0 else []
                    prog.op("gpsimd", lambda g, k=k, blk=blk: g.dma_start(
                        out=wblk[:, k * 1024:(k + 1) * 1024],
                        in_=wg_in[k * 128:(k + 1) * 128,
                                  blk * 1024:(blk + 1) * 1024]),
                        waits=[x for x in w if x[1] > 0], inc=ld.inc(16))
                ld_g = ld.n

                def _gmm(t, tk=tk):
                    last = None
                    for k in range(DCH):
                        for mh in range(8):
                            last = t.matmul(
                                psum[mh][:],
                                wblk[:, k * 1024 + mh * 128:
                                     k * 1024 + (mh + 1) * 128],
                                xb[:, k * T + tk:k * T + tk + 512],
                                start=(k == 0), stop=(k == DCH - 1))
                    return last
                prog.op("tensor", _gmm,
                        waits=[(ld, ld_g), (vec, vec_banks_free)],
                        inc=mms.inc())
                mm_g = mms.n
                for mh in range(8):
                    prog.op("scalar", lambda s, mh=mh: s.activation(
                        gsil[:, mh * 512:(mh + 1) * 512],
                        psum[mh][:], AF.Silu, scale=1.0 / 127.0),
                        waits=[(mms, mm_g)] if mh == 0 else (),
                        inc=scl.inc() if mh == 7 else None)
                scl_gate = scl.n
                for k in range(DCH):
                    w = [(mms, mm_g)] if k == 0 else []
                    prog.op("gpsimd", lambda g, k=k, blk=blk: g.dma_start(
                        out=wblk[:, k * 1024:(k + 1) * 1024],
                        in_=wu_in[k * 128:(k + 1) * 128,
                                  blk * 1024:(blk + 1) * 1024]),
                        waits=w, inc=ld.inc(16))
                ld_u = ld.n

                def _umm(t, tk=tk):
                    last = None
                    for k in range(DCH):
                        for mh in range(8):
                            last = t.matmul(
                                psum[mh][:],
                                wblk[:, k * 1024 + mh * 128:
                                     k * 1024 + (mh + 1) * 128],
                                xb[:, k * T + tk:k * T + tk + 512],
                                start=(k == 0), stop=(k == DCH - 1))
                    return last
                prog.op("tensor", _umm, waits=[(ld, ld_u), (scl, scl_gate)],
                        inc=mms.inc())
                mm_u = mms.n
                mm_wblk_free = mms.n

                def _hh(v, blk=blk):
                    last = None
                    for mh in range(8):
                        last = v.scalar_tensor_tensor(
                            hh[:, (blk * 8 + mh) * 512:(blk * 8 + mh + 1) * 512],
                            psum[mh][:], 1.0 / 127.0,
                            gsil[:, mh * 512:(mh + 1) * 512],
                            ALU.mult, ALU.mult)
                    return last
                prog.op("vector", _hh, waits=[(mms, mm_u)], inc=vec.inc())
                vec_hh = vec.n
                vec_banks_free = vec.n

            mm_wd = [0] * 24
            for kh in range(24):
                wdt = wdb[kh % 2]
                if kh >= 2:
                    w = [(mms, mm_wd[kh - 2])]
                else:
                    w = [(mms, mm_wblk_free)] if qt > 0 else []
                prog.op("gpsimd", lambda g, kh=kh, wdt=wdt: g.dma_start(
                    out=wdt[:], in_=wd_in[kh * 128:(kh + 1) * 128, :]),
                    waits=[x for x in w if x[1] > 0], inc=ld.inc(16))
                ld_kh = ld.n
                w = [(ld, ld_kh)]
                if kh == 0:
                    w.append((vec, vec_hh))

                def _dk(t, kh=kh, wdt=wdt):
                    last = None
                    for m in range(DCH):
                        last = t.matmul(
                            psum[m][:],
                            wdt[:, m * 128:(m + 1) * 128],
                            hh[:, kh * 512:(kh + 1) * 512],
                            start=(kh == 0), stop=(kh == 23))
                    return last
                prog.op("tensor", _dk, waits=w, inc=mms.inc())
                mm_wd[kh] = mms.n
            mm_d = mms.n

            def _z1(v, tk=tk):
                last = None
                for m in range(DCH):
                    sl = slice(m * T + tk, m * T + tk + 512)
                    last = v.scalar_tensor_tensor(
                        xf[:, sl], psum[m][:], 1.0 / 127.0, xf[:, sl],
                        ALU.mult, ALU.add)
                return last
            prog.op("vector", _z1, waits=[(mms, mm_d)], inc=vec.inc())
            vec_banks_free = vec.n
        vec_z1_all = vec.n
        ms1 = emit_layernorm(prog, st, 2, {"z_ready_vec": vec_z1_all,
                                           "banks_free_vec": vec_z1_all})
        sto_y1 = 0
        for m in range(DCH):
            prog.op("sync", lambda s, m=m: s.dma_start(
                out=y1_out[m * 128:(m + 1) * 128, :],
                in_=xf[:, m * T:(m + 1) * T]),
                waits=[(vec, ms1["vec_apply"])] if m == 0 else (),
                inc=sto.inc(16))
        sto_y1 = sto.n
        p3.close()
        p_xf.close()

        p4 = contextlib.ExitStack()
        sb4 = lambda name, shape, dt=F32: p4.enter_context(
            nc.sbuf_tensor(name, shape, dt))
        wfull = sb4("wfull", [128, DCH * 1024], BF16)
        cfull = sb4("cfull", [128, DCH * FPAD], BF16)
        sfull = sb4("sfull", [128, DCH * FPAD], BF16)
        mstag = sb4("mstag", [128, DCH * FPAD], BF16)

        for e in range(DCH):
            prog.op("sync", lambda s, e=e: s.dma_start(
                out=cfull[:, e * FPAD:(e + 1) * FPAD],
                in_=cmat_in[e * 128:(e + 1) * 128, :]),
                waits=[(sto, sto_y1)] if e == 0 else (), inc=ld.inc(16))
            prog.op("sync", lambda s, e=e: s.dma_start(
                out=sfull[:, e * FPAD:(e + 1) * FPAD],
                in_=smat_in[e * 128:(e + 1) * 128, :]), inc=ld.inc(16))
        ld_cs = ld.n

        evac_scl_prev = ms1["scl_shadow"]
        bank_free_vec = ms1["banks_busy_until_vec"]
        sto_mscr = 0
        for mat in ("k", "v", "q"):
            for e in range(DCH):
                prog.op("gpsimd", lambda g, e=e, mat=mat: g.dma_start(
                    out=wfull[:, e * 1024:(e + 1) * 1024],
                    in_=w_qkv_in[mat][e * 128:(e + 1) * 128, :]),
                    waits=[(mms, mms.n), (sto, sto_y1)] if e == 0 else (),
                    inc=ld.inc(16))
            ld_w = ld.n
            for (part, csrc) in (("re", cfull), ("im", sfull)):
                scr = mscr[mat + part]
                for (ftw, ft0) in ((512, 0), (128, 512)):
                    w = [(ld, ld_w), (ld, ld_cs), (scl, evac_scl_prev),
                         (vec, bank_free_vec)]
                    def _mmm(t, csrc=csrc, ftw=ftw, ft0=ft0):
                        last = None
                        for e in range(DCH):
                            for m in range(DCH):
                                last = t.matmul(
                                    psum[m][0:128, 0:ftw],
                                    wfull[:, e * 1024 + m * 128:
                                          e * 1024 + (m + 1) * 128],
                                    csrc[:, e * FPAD + ft0:e * FPAD + ft0 + ftw],
                                    start=(e == 0), stop=(e == DCH - 1))
                        return last
                    prog.op("tensor", _mmm, waits=w, inc=mms.inc())
                    mm_m = mms.n
                    for m in range(DCH):
                        w = []
                        if m == 0:
                            w = [(mms, mm_m)]
                            if sto_mscr:
                                w.append((sto, sto_mscr))
                        prog.op("scalar", lambda s, m=m, ft0=ft0, ftw=ftw:
                                s.activation(
                                    mstag[:, m * FPAD + ft0:m * FPAD + ft0 + ftw],
                                    psum[m][0:128, 0:ftw], AF.Copy),
                                waits=w,
                                inc=scl.inc() if m == DCH - 1 else None)
                    evac_scl_prev = scl.n
                    bank_free_vec = 0
                for m in range(DCH):
                    prog.op("sync", lambda s, m=m, scr=scr: s.dma_start(
                        out=scr[m * 128:(m + 1) * 128, :],
                        in_=mstag[:, m * FPAD:(m + 1) * FPAD]),
                        waits=[(scl, evac_scl_prev)] if m == 0 else (),
                        inc=sto.inc(16))
                sto_mscr = sto.n
        p4.close()

        p5 = contextlib.ExitStack()
        sb5 = lambda name, shape, dt=F32: p5.enter_context(
            nc.sbuf_tensor(name, shape, dt))
        At = sb5("At", [128, T]); Bt = sb5("Bt", [128, T])
        Ct = sb5("Ct", [128, T]); Dt = sb5("Dt", [128, T])
        wf1 = sb5("wf1", [128, T]); wf2 = sb5("wf2", [128, T])
        scr1 = sb5("scr1", [128, T]); scr2 = sb5("scr2", [128, T])
        pre = [sb5(f"pre{i}", [128, T], BF16) for i in range(2)]
        pim = [sb5(f"pim{i}", [128, T], BF16) for i in range(2)]
        mk1 = sb5("mk1", [128, DCH * 128], BF16)
        mk2 = sb5("mk2", [128, DCH * 128], BF16)
        ptot = sb5("ptot", [128, 10])

        def proj_mms(tag, c, blk_re, blk_im, bank0_wait_scl, extra_waits=()):
            w = list(extra_waits) + [(scl, bank0_wait_scl)]
            def _p(t, c=c, blk_re=blk_re, blk_im=blk_im):
                last = None
                for n in range(NT):
                    for k in range(DCH):
                        last = t.matmul(
                            psum[n][:],
                            blk_re[:, k * 128:(k + 1) * 128],
                            xb[:, k * T + n * 512:k * T + (n + 1) * 512],
                            start=(k == 0), stop=(k == DCH - 1))
                for n in range(NT):
                    for k in range(DCH):
                        last = t.matmul(
                            psum[4 + n][:],
                            blk_im[:, k * 128:(k + 1) * 128],
                            xb[:, k * T + n * 512:k * T + (n + 1) * 512],
                            start=(k == 0), stop=(k == DCH - 1))
                return last
            prog.op("tensor", _p, waits=[x for x in w if x[1] > 0], inc=mms.inc())
            return mms.n

        def evac8(dst_re, dst_im, mm_wait, extra=()):
            for n in range(NT):
                w = ([(mms, mm_wait)] + [x for x in extra if x[1] > 0]) \
                    if n == 0 else ()
                prog.op("scalar", lambda s, n=n, dst_re=dst_re: s.activation(
                    dst_re[:, n * 512:(n + 1) * 512], psum[n][:], AF.Copy),
                    waits=w)
            for n in range(NT):
                prog.op("scalar", lambda s, n=n, dst_im=dst_im: s.activation(
                    dst_im[:, n * 512:(n + 1) * 512], psum[4 + n][:], AF.Copy),
                    inc=scl.inc() if n == NT - 1 else None)
            return scl.n

        def load_mblks(names, c, dst1, dst2, waits0):
            for (nm, dstt) in ((names[0], dst1), (names[1], dst2)):
                for k in range(DCH):
                    w = [x for x in waits0 if x[1] > 0] \
                        if (k == 0 and nm == names[0]) else ()
                    prog.op("gpsimd", lambda g, nm=nm, k=k, c=c, dstt=dstt:
                            g.dma_start(
                                out=dstt[:, k * 128:(k + 1) * 128],
                                in_=mscr[nm][k * 128:(k + 1) * 128,
                                             c * 128:(c + 1) * 128]),
                            waits=w, inc=ld.inc(16))
            return ld.n

        def normalize(mm_evac_scl, vec_guard):
            w = [(scl, mm_evac_scl)]
            if vec_guard:
                w.append((vec, vec_guard))
            prog.op("scalar", lambda s: s.activation(wf1[:], At[:], AF.Square),
                    waits=w)
            prog.op("scalar", lambda s: s.activation(wf2[:], Bt[:], AF.Square),
                    inc=scl.inc())
            s_sq = scl.n
            prog.op("vector", lambda v: v.tensor_add(wf1[:], wf1[:], wf2[:]),
                    waits=[(scl, s_sq)], inc=vec.inc())
            v_add = vec.n
            prog.op("scalar", lambda s: s.activation(wf2[:], wf1[:], AF.Sqrt),
                    waits=[(vec, v_add)])
            prog.op("scalar", lambda s: s.activation(
                wf2[:], wf2[:], AF.Ln, bias=cb_t[:, 1:2]))
            prog.op("scalar", lambda s: s.activation(
                wf2[:], wf2[:], AF.Exp, scale=-1.0), inc=scl.inc())
            s_rn = scl.n
            def _nk(v):
                v.tensor_mul(At[:], At[:], wf2[:])
                return v.tensor_mul(Bt[:], Bt[:], wf2[:])
            prog.op("vector", _nk, waits=[(scl, s_rn)], inc=vec.inc())
            return vec.n

        vec_work_last = 0
        scl_evac_prev = evac_scl_prev
        sto_pr = [0, 0]
        mm_blk_last = 0
        for c in range(FCH):
            par = c % 2
            ld_k = load_mblks(("kre", "kim"), c, mk1, mk2,
                              [(sto, sto_mscr), (mms, mm_blk_last)])
            mm_k = proj_mms("k", c, mk1, mk2, scl_evac_prev,
                            extra_waits=[(ld, ld_k)])
            mm_blk_last = mm_k
            scl_evac_prev = evac8(At, Bt, mm_k,
                                  extra=[(vec, vec_work_last)])
            ld_v = load_mblks(("vre", "vim"), c, mk1, mk2,
                              [(mms, mm_k)])
            mm_v = proj_mms("v", c, mk1, mk2, scl_evac_prev,
                            extra_waits=[(ld, ld_v)])
            mm_blk_last = mm_v
            scl_evac_prev = evac8(Ct, Dt, mm_v)
            v_nk = normalize(scl_evac_prev, vec_work_last)
            def _prods(v, c=c, par=par):
                v.tensor_mul(scr1[:], At[:], Ct[:])
                v.tensor_mul(scr2[:], At[:], Dt[:])
                v.tensor_mul(Ct[:], Bt[:], Ct[:])
                v.tensor_mul(Dt[:], Bt[:], Dt[:])
                v.scalar_tensor_tensor(pre[par][:], scr1[:], 1.0, Dt[:],
                                       ALU.mult, ALU.subtract,
                                       accum_out=ptot[:, 2 * c:2 * c + 1])
                return v.scalar_tensor_tensor(
                    pim[par][:], scr2[:], 1.0, Ct[:], ALU.mult, ALU.add,
                    accum_out=ptot[:, 2 * c + 1:2 * c + 2])
            w = [(vec, v_nk)]
            if sto_pr[par]:
                w.append((sto, sto_pr[par]))
            prog.op("vector", _prods, waits=w, inc=vec.inc())
            vec_work_last = vec.n
            prog.op("sync", lambda s, c=c, par=par: s.dma_start(
                out=pr_out["re"][c * 128:(c + 1) * 128, :], in_=pre[par][:]),
                waits=[(vec, vec_work_last)], inc=sto.inc(16))
            prog.op("sync", lambda s, c=c, par=par: s.dma_start(
                out=pr_out["im"][c * 128:(c + 1) * 128, :], in_=pim[par][:]),
                inc=sto.inc(16))
            sto_pr[par] = sto.n

        for c in range(FCH):
            par = c % 2
            ld_q = load_mblks(("qre", "qim"), c, mk1, mk2,
                              [(mms, mm_blk_last)])
            mm_q = proj_mms("q", c, mk1, mk2, scl_evac_prev,
                            extra_waits=[(ld, ld_q)])
            mm_blk_last = mm_q
            scl_evac_prev = evac8(At, Bt, mm_q,
                                  extra=[(vec, vec_work_last)])
            v_nq = normalize(scl_evac_prev, vec_work_last)
            def _cast(v, par=par):
                v.tensor_copy(pre[par][:], At[:])
                return v.tensor_copy(pim[par][:], Bt[:])
            w = [(vec, v_nq)]
            if sto_pr[par]:
                w.append((sto, sto_pr[par]))
            prog.op("vector", _cast, waits=w, inc=vec.inc())
            vec_work_last = vec.n
            prog.op("sync", lambda s, c=c, par=par: s.dma_start(
                out=nq_out["re"][c * 128:(c + 1) * 128, :], in_=pre[par][:]),
                waits=[(vec, vec_work_last)], inc=sto.inc(16))
            prog.op("sync", lambda s, c=c, par=par: s.dma_start(
                out=nq_out["im"][c * 128:(c + 1) * 128, :], in_=pim[par][:]),
                inc=sto.inc(16))
            sto_pr[par] = sto.n

        prog.op("sync", lambda s: s.dma_start(out=ptot_out[:], in_=ptot[:]),
                waits=[(vec, vec.n)], inc=sto.inc(16))
        prog.op("sync", lambda s: None, waits=[(sto, sto.n)])
        p5.close()
        p4.close()
        prog.finalize()
    return nc



def build_launch_C():
    nc = bass.Bass("TRN2", target_bir_lowering=False)
    dram = lambda n, sh, dt=F32, kind="ExternalInput": nc.dram_tensor(
        n, sh, dt, kind=kind).ap()
    y1_in = dram("y1_in", [D, T])
    pr_in = {"re": dram("prre_in", [FPAD, T], BF16),
             "im": dram("prim_in", [FPAD, T], BF16)}
    nq_in = {"re": dram("nqre_in", [FPAD, T], BF16),
             "im": dram("nqim_in", [FPAD, T], BF16)}
    pta_in = dram("ptotall", [8, 1280])
    mask1_in = dram("mask1", [128, 80])
    cb_in = dram("cb_v", [128, 4])
    ic_in = dram("icmat", [FPAD, D], BF16)
    is_in = dram("ismat", [FPAD, D], BF16)
    w1_in = dram("w1T", [D, H1])
    w2_in = dram("w2T", [D, H1])
    h1_in = dram("h1T", [H1, H1])
    h2_in = dram("h2T", [H1, H1])
    w3_in = dram("w3T", [H1, D])
    s1_in = dram("s1_v", [128, 32])
    s2_in = dram("s2_v", [128, 32])
    h1s_in = dram("h1s_v", [128, 32])
    h2s_in = dram("h2s_v", [128, 32])
    w3s_in = dram("w3s_v", [128, 8])
    y_out = dram("y_out", [D, T], F32, "ExternalOutput")

    with contextlib.ExitStack() as stack:
        sb = lambda name, shape, dt=F32: stack.enter_context(
            nc.sbuf_tensor(name, shape, dt))
        ps = lambda name, shape: stack.enter_context(
            nc.psum_tensor(name, shape, F32))

        xb = sb("xb", [128, DCH * T], BF16)
        cb_t = sb("cb_t", [128, 4])
        s1_t = sb("s1_t", [128, 32]); s2_t = sb("s2_t", [128, 32])
        h1s_t = sb("h1s_t", [128, 32]); h2s_t = sb("h2s_t", [128, 32])
        w3s_t = sb("w3s_t", [128, 8])
        ones_col = sb("ones_col", [128, 1])
        ones_row = sb("ones_row", [1, 128])
        rowA = sb("rowA", [1, T]); rowB = sb("rowB", [1, T])
        wkS = [sb(f"wkS{i}", [128, T]) for i in range(2)]
        psum = [ps(f"pC{i}", [128, 512]) for i in range(8)]

        prog = Prog(nc, stack)
        ld = prog.sem(); vec = prog.sem(); scl = prog.sem()
        mms = prog.sem(); sto = prog.sem(); nsync = prog.sem()

        st = {"xb": xb, "wkS": wkS, "psum": psum, "rowA": rowA, "rowB": rowB,
              "ones_col": ones_col, "ones_row": ones_row,
              "cb_t": cb_t, "vec": vec, "scl": scl, "mms": mms, "nsync": nsync,
              "sto": sto}

        xf = stack.enter_context(nc.sbuf_tensor("xf", [128, DCH * T], F32))
        st["xf"] = xf

        p1 = contextlib.ExitStack()
        sb1 = lambda name, shape, dt=F32: p1.enter_context(
            nc.sbuf_tensor(name, shape, dt))
        agt1 = sb1("agt1", [128, 80])
        maskt1 = sb1("maskt1", [128, 80])
        carry1 = sb1("carry1", [128, 10])
        fkv = {"re": sb1("fkv_re", [128, FCH * T], BF16),
               "im": sb1("fkv_im", [128, FCH * T], BF16)}
        p2 = contextlib.ExitStack()
        sb2 = lambda name, shape, dt=F32: p2.enter_context(
            nc.sbuf_tensor(name, shape, dt))
        nqt = [sb2(f"nqt{i}", [128, T], BF16) for i in range(2)]
        sc1 = sb2("sc1", [128, T]); sc2 = sb2("sc2", [128, T])
        sc3 = sb2("sc3", [128, T], BF16)

        for k in range(DCH):
            prog.op("sync", lambda s, k=k: s.dma_start(
                out=xf[:, k * T:(k + 1) * T], in_=y1_in[k * 128:(k + 1) * 128, :]),
                inc=ld.inc(16))
        for (dst, srcd) in [(cb_t, cb_in), (maskt1, mask1_in), (s1_t, s1_in),
                            (s2_t, s2_in), (h1s_t, h1s_in), (h2s_t, h2s_in),
                            (w3s_t, w3s_in)]:
            prog.op("sync", lambda s, dst=dst, srcd=srcd: s.dma_start(
                out=dst[:], in_=srcd[:]), inc=ld.inc(16))
        lded0 = ld.n
        prog.op("vector", lambda v: v.memset(ones_col[:], 1.0))
        prog.op("vector", lambda v: v.memset(ones_row[:], 1.0))
        prog.op("vector", lambda v: v.memset(wkS[1][:], 0.0))
        cb1_vec = emit_carry_combine(prog, "gpsimd", agt1, maskt1, carry1, nsync,
                                     pta_in, 10, ld, vec, waits=[(ld, lded0)])

        vec_fkvq = [0] * FCH
        for c in range(FCH):
            par = c % 2
            for (part, dstc) in (("re", 0), ("im", 1)):
                prog.op("gpsimd", lambda g, c=c, part=part: g.dma_start(
                    out=fkv[part][:, c * T:(c + 1) * T],
                    in_=pr_in[part][c * 128:(c + 1) * 128, :]), inc=ld.inc(16))
            prog.op("gpsimd", lambda g, c=c: g.dma_start(
                out=nqt[0][:, :], in_=nq_in["re"][c * 128:(c + 1) * 128, :]),
                waits=[(vec, vec_fkvq[c - 1])] if c >= 1 else (), inc=ld.inc(16))
            prog.op("gpsimd", lambda g, c=c: g.dma_start(
                out=nqt[1][:, :], in_=nq_in["im"][c * 128:(c + 1) * 128, :]),
                inc=ld.inc(16))
            ld_c = ld.n
            def _scanq(v, c=c):
                fre = fkv["re"][:, c * T:(c + 1) * T]
                fim = fkv["im"][:, c * T:(c + 1) * T]
                v.tensor_tensor_scan(fre, fre, wkS[1][:, 0:T],
                                     carry1[:, 2 * c:2 * c + 1],
                                     ALU.add, ALU.add)
                v.tensor_tensor_scan(fim, fim, wkS[1][:, 0:T],
                                     carry1[:, 2 * c + 1:2 * c + 2],
                                     ALU.add, ALU.add)
                v.tensor_mul(sc1[:], fre, nqt[0][:, :])
                v.tensor_mul(sc2[:], fim, nqt[1][:, :])
                v.tensor_mul(sc3[:], fre, nqt[1][:, :])
                v.tensor_tensor(fre, sc1[:], sc2[:], ALU.add)
                v.tensor_mul(sc1[:], fim, nqt[0][:, :])
                return v.tensor_tensor(fim, sc1[:], sc3[:], ALU.subtract)
            w = [(ld, ld_c), (vec, cb1_vec)]
            prog.op("vector", _scanq, waits=w, inc=vec.inc())
            vec_fkvq[c] = vec.n
        vec_fkvq_all = vec.n

        p2.close()
        icb = sb1("icb", [128, FCH * D], BF16)
        isb = sb1("isb", [128, FCH * D], BF16)
        for c in range(FCH):
            prog.op("sync", lambda s, c=c: s.dma_start(
                out=icb[:, c * D:(c + 1) * D], in_=ic_in[c * 128:(c + 1) * 128, :]),
                waits=[(vec, vec_fkvq_all)] if c == 0 else (),
                inc=ld.inc(16))
            prog.op("sync", lambda s, c=c: s.dma_start(
                out=isb[:, c * D:(c + 1) * D], in_=is_in[c * 128:(c + 1) * 128, :]),
                inc=ld.inc(16))
        ld_ic = ld.n

        vec_z2 = 0
        for n in range(NT):
            w = [(ld, ld_ic), (vec, vec_fkvq_all)]
            if n > 0:
                w.append((vec, vec_z2))
            def _irf(t, n=n):
                last = None
                for c in range(FCH):
                    for (blkt, part) in ((icb, "re"), (isb, "im")):
                        for m in range(DCH):
                            last = t.matmul(
                                psum[m][:],
                                blkt[:, c * D + m * 128:c * D + (m + 1) * 128],
                                fkv[part][:, c * T + n * 512:c * T + (n + 1) * 512],
                                start=(c == 0 and part == "re"),
                                stop=(c == FCH - 1 and part == "im"))
                return last
            prog.op("tensor", _irf, waits=w, inc=mms.inc())
            mm_ir = mms.n

            def _z2(v, n=n):
                last = None
                for m in range(DCH):
                    sl = slice(m * T + n * 512, m * T + (n + 1) * 512)
                    last = v.scalar_tensor_tensor(
                        xf[:, sl], xf[:, sl], 2.0, psum[m][:],
                        ALU.mult, ALU.add)
                return last
            prog.op("vector", _z2, waits=[(mms, mm_ir)], inc=vec.inc())
            vec_z2 = vec.n

        ms2 = emit_layernorm(prog, st, 2, {"z_ready_vec": vec_z2,
                                           "banks_free_vec": vec_z2})
        p1.close()

        p5 = contextlib.ExitStack()
        sb5 = lambda name, shape, dt=F32: p5.enter_context(
            nc.sbuf_tensor(name, shape, dt))
        habf = sb5("habf", [128, 32 * 512], BF16)
        habf2 = sb5("habf2", [128, 32 * 512], BF16)
        wcb = sb5("wcb", [128, 4 * 1024], BF16)

        bfv = ms2["banks_busy_until_vec"]
        bfs = ms2["scl_shadow"]
        mm_wcb_free = ms2["mms_bcast"]

        def load_w4(win, row0, hb):
            for k4 in range(4):
                w = [(mms, mm_wcb_free)] if k4 == 0 else []
                prog.op("gpsimd", lambda g, k4=k4, hb=hb, win=win, row0=row0:
                        g.dma_start(
                            out=wcb[:, k4 * 1024:(k4 + 1) * 1024],
                            in_=win[(row0 + k4) * 128:(row0 + k4 + 1) * 128,
                                    hb * 1024:(hb + 1) * 1024]),
                        waits=[x for x in w if x[1] > 0], inc=ld.inc(16))
            return ld.n

        def bank_waits(extra=()):
            w = list(extra)
            if bfv:
                w.append((vec, bfv))
            if bfs:
                w.append((scl, bfs))
            return w

        for qt in range(4):
            tk = qt * 512
            for hb in range(4):
                mm_1 = 0
                for ks in range(2):
                    ld_w = load_w4(w1_in, ks * 4, hb)
                    w = [(ld, ld_w)]
                    if ks == 0:
                        w = bank_waits(w)
                    def _am(t, ks=ks, tk=tk):
                        last = None
                        for k4 in range(4):
                            k8 = ks * 4 + k4
                            for mh in range(8):
                                last = t.matmul(
                                    psum[mh][:],
                                    wcb[:, k4 * 1024 + mh * 128:
                                        k4 * 1024 + (mh + 1) * 128],
                                    xb[:, k8 * T + tk:k8 * T + tk + 512],
                                    start=(k8 == 0), stop=(k8 == 7))
                        return last
                    prog.op("tensor", _am, waits=w, inc=mms.inc())
                    mm_wcb_free = mms.n
                    mm_1 = mms.n
                for mh in range(8):
                    prog.op("scalar", lambda s, mh=mh, hb=hb: s.activation(
                        habf[:, (hb * 8 + mh) * 512:(hb * 8 + mh + 1) * 512],
                        psum[mh][:],
                        AF.Silu, scale=s1_t[:, hb * 8 + mh:hb * 8 + mh + 1]),
                        waits=[(mms, mm_1)] if mh == 0 else (),
                        inc=scl.inc() if mh == 7 else None)
                bfs = scl.n
                bfv = 0
                mm_2 = 0
                for ks in range(2):
                    ld_w = load_w4(w2_in, ks * 4, hb)
                    w = [(ld, ld_w)]
                    if ks == 0:
                        w = bank_waits(w)
                    def _bm(t, ks=ks, tk=tk):
                        last = None
                        for k4 in range(4):
                            k8 = ks * 4 + k4
                            for mh in range(8):
                                last = t.matmul(
                                    psum[mh][:],
                                    wcb[:, k4 * 1024 + mh * 128:
                                        k4 * 1024 + (mh + 1) * 128],
                                    xb[:, k8 * T + tk:k8 * T + tk + 512],
                                    start=(k8 == 0), stop=(k8 == 7))
                        return last
                    prog.op("tensor", _bm, waits=w, inc=mms.inc())
                    mm_wcb_free = mms.n
                    mm_2 = mms.n
                def _ha(v, hb=hb):
                    last = None
                    for mh in range(8):
                        sl = slice((hb * 8 + mh) * 512, (hb * 8 + mh + 1) * 512)
                        last = v.scalar_tensor_tensor(
                            habf[:, sl], psum[mh][:],
                            s2_t[:, hb * 8 + mh:hb * 8 + mh + 1],
                            habf[:, sl], ALU.mult, ALU.mult)
                    return last
                prog.op("vector", _ha, waits=[(mms, mm_2), (scl, bfs)],
                        inc=vec.inc())
                bfv = vec.n
                bfs = 0

            for (win, sclt, src_a, dst_a) in ((h1_in, h1s_t, habf, habf2),
                                              (h2_in, h2s_t, habf2, habf)):
                for hb in range(4):
                    mm_last = 0
                    for ks in range(8):
                        ld_w = load_w4(win, ks * 4, hb)
                        w = [(ld, ld_w)]
                        if ks == 0:
                            w = bank_waits(w)
                        def _hm(t, ks=ks, src_a=src_a):
                            last = None
                            for k4 in range(4):
                                kk = ks * 4 + k4
                                for mh in range(8):
                                    last = t.matmul(
                                        psum[mh][:],
                                        wcb[:, k4 * 1024 + mh * 128:
                                            k4 * 1024 + (mh + 1) * 128],
                                        src_a[:, kk * 512:(kk + 1) * 512],
                                        start=(kk == 0), stop=(kk == 31))
                            return last
                        prog.op("tensor", _hm, waits=w, inc=mms.inc())
                        mm_wcb_free = mms.n
                        mm_last = mms.n
                    for mh in range(8):
                        w_ev = [(mms, mm_last)] if mh == 0 else []
                        prog.op("scalar", lambda s, mh=mh, hb=hb, sclt=sclt,
                                dst_a=dst_a: s.activation(
                                    dst_a[:, (hb * 8 + mh) * 512:
                                          (hb * 8 + mh + 1) * 512],
                                    psum[mh][:], AF.Gelu,
                                    scale=sclt[:, hb * 8 + mh:hb * 8 + mh + 1]),
                                waits=w_ev,
                                inc=scl.inc() if mh == 7 else None)
                    bfs = scl.n
                    bfv = 0

            mm_wd_last = 0
            for ks in range(8):
                ld_w = load_w4(w3_in, ks * 4, 0)
                w = [(ld, ld_w)]
                if ks == 0:
                    w = bank_waits(w)
                def _dm(t, ks=ks):
                    last = None
                    for k4 in range(4):
                        kk = ks * 4 + k4
                        for m in range(8):
                            last = t.matmul(
                                psum[m][:],
                                wcb[:, k4 * 1024 + m * 128:
                                    k4 * 1024 + (m + 1) * 128],
                                habf[:, kk * 512:(kk + 1) * 512],
                                start=(kk == 0), stop=(kk == 31))
                    return last
                prog.op("tensor", _dm, waits=w, inc=mms.inc())
                mm_wcb_free = mms.n
                mm_wd_last = mms.n
            def _z3(v, tk=tk):
                last = None
                for m in range(8):
                    sl = slice(m * T + tk, m * T + tk + 512)
                    last = v.scalar_tensor_tensor(
                        xf[:, sl], psum[m][:], w3s_t[:, m:m + 1], xf[:, sl],
                        ALU.mult, ALU.add)
                return last
            prog.op("vector", _z3, waits=[(mms, mm_wd_last)], inc=vec.inc())
            bfv = vec.n
            bfs = 0
        vec_z3_all = vec.n

        ms3 = emit_layernorm(prog, st, 2, {"z_ready_vec": vec_z3_all,
                                           "banks_free_vec": vec_z3_all})
        for m in range(DCH):
            prog.op("sync", lambda s, m=m: s.dma_start(
                out=y_out[m * 128:(m + 1) * 128, :],
                in_=xf[:, m * T:(m + 1) * T]),
                waits=[(vec, ms3["vec_apply"])] if m == 0 else (),
                inc=sto.inc(16))
        prog.op("sync", lambda s: None, waits=[(sto, sto.n)])

        p5.close()
        prog.finalize()
    return nc



def build_fused():
    nc = bass.Bass("TRN2", target_bir_lowering=False)
    dram = lambda n, sh, dt=F32, kind="ExternalInput": nc.dram_tensor(
        n, sh, dt, kind=kind).ap()
    x_in = dram("x_fm", [D, T])
    gwT_in = dram("gwT", [D, D])
    a_in = dram("a_v", [128, 8])
    b_in = dram("b_v", [128, 8])
    gb_in = dram("gb_v", [128, 8])
    csel_in = dram("csel_v", [128, 4])
    cb_in = dram("cb_v", [128, 4])
    mask0_in = dram("mask0", [128, 64])
    mask1_in = dram("mask1", [128, 80])
    c_in = dram("c_v", [128, 8])
    d_in = dram("d_v", [128, 8])
    wg_in = dram("wgT_i", [D, H0], I32)
    wu_in = dram("wuT_i", [D, H0], I32)
    wd_in = dram("wdT_i", [H0, D], I32)
    w_qkv_in = {"q": dram("wq_r", [D, D]), "k": dram("wk_r", [D, D]),
                "v": dram("wv_r", [D, D])}
    cmat_in = dram("cmat", [D, FPAD], BF16)
    smat_in = dram("smat", [D, FPAD], BF16)
    ic_in = dram("icmat", [FPAD, D], BF16)
    is_in = dram("ismat", [FPAD, D], BF16)
    w1_in = dram("w1T", [D, H1])
    w2_in = dram("w2T", [D, H1])
    h1_in = dram("h1T", [H1, H1])
    h2_in = dram("h2T", [H1, H1])
    w3_in = dram("w3T", [H1, D])
    s1_in = dram("s1_v", [128, 32])
    s2_in = dram("s2_v", [128, 32])
    h1s_in = dram("h1s_v", [128, 32])
    h2s_in = dram("h2s_v", [128, 32])
    w3s_in = dram("w3s_v", [128, 8])
    y_out = dram("y_out", [D, T], F32, "ExternalOutput")
    intl = lambda n, sh, dt=F32: nc.dram_tensor(n, sh, dt, kind="Internal").ap()
    v_out = intl("v_scr", [D, T])
    pows_out = intl("pows_scr", [D, T])
    v_in = v_out
    pows_in = pows_out
    y1_out = intl("y1_scr", [D, T])
    y1_in = y1_out
    pr_out = {"re": intl("prre_scr", [FPAD, T], BF16),
              "im": intl("prim_scr", [FPAD, T], BF16)}
    nq_out = {"re": intl("nqre_scr", [FPAD, T], BF16),
              "im": intl("nqim_scr", [FPAD, T], BF16)}
    pr_in = pr_out
    nq_in = nq_out
    ccin0 = intl("ccin0", [1, D])
    ccout0 = intl("ccout0", [8, D])
    ccin1 = intl("ccin1", [1, 1280])
    ccout1 = intl("ccout1", [8, 1280])
    vta_in = ccout0
    pta_in = ccout1
    mscr = {}
    for nm in ["qre", "qim", "kre", "kim", "vre", "vim"]:
        mscr[nm] = nc.dram_tensor("mscr_" + nm, [D, FPAD], BF16,
                                  kind="Internal").ap()
    wbf0 = {
        "wg": nc.dram_tensor("wgbf", [3, 8, 128, 1024], BF16, kind="Internal").ap(),
        "wu": nc.dram_tensor("wubf", [3, 8, 128, 1024], BF16, kind="Internal").ap(),
        "wd": nc.dram_tensor("wdbf", [1, 24, 128, 1024], BF16, kind="Internal").ap(),
    }
    wbf = {
        "w1": nc.dram_tensor("w1bf", [4, 8, 128, 1024], BF16, kind="Internal").ap(),
        "w2": nc.dram_tensor("w2bf", [4, 8, 128, 1024], BF16, kind="Internal").ap(),
        "h1": nc.dram_tensor("h1bf", [4, 32, 128, 1024], BF16, kind="Internal").ap(),
        "h2": nc.dram_tensor("h2bf", [4, 32, 128, 1024], BF16, kind="Internal").ap(),
        "w3": nc.dram_tensor("w3bf", [1, 32, 128, 1024], BF16, kind="Internal").ap(),
    }

    with contextlib.ExitStack() as stack:
        psb = lambda name, shape, dt=F32: stack.enter_context(
            nc.sbuf_tensor(name, shape, dt))
        ps = lambda name, shape: stack.enter_context(
            nc.psum_tensor(name, shape, F32))
        xb = psb("xb", [128, DCH * T], BF16)
        cb_t = psb("cb_t", [128, 4])
        ones_col = psb("ones_col", [128, 1])
        ones_row = psb("ones_row", [1, 128])
        rowA = psb("rowA", [1, T])
        rowB = psb("rowB", [1, T])
        wkS = [psb(f"wkS{i}", [128, T]) for i in range(2)]
        c_t = psb("c_t", [128, 8])
        d1_t = psb("d1_t", [128, 8])
        a_t = psb("a_t", [128, 8])
        b_t = psb("b_t", [128, 8])
        gb_t = psb("gb_t", [128, 8])
        csel_t = psb("csel_t", [128, 4])
        scA = {n: psb("sc_" + n, [128, 8]) for n in TANH_SCRATCH}
        mskA = psb("msk", [128, 8], I32)
        a_raw = psb("a_raw", [128, 8])
        asafe = psb("asafe", [128, 8])
        carry = psb("carry", [128, 8])
        vtot = psb("vtot", [128, 8])
        psum = [ps(f"pF{i}", [128, 512]) for i in range(8)]

        prog = Prog(nc, stack)
        ld = prog.sem(); vec = prog.sem(); scl = prog.sem()
        mms = prog.sem(); sto = prog.sem(); nsync = prog.sem()
        ccs = prog.sem()
        ccs1 = prog.sem()
        wconv = prog.sem()
        wl0 = prog.sem()
        wl1 = prog.sem()
        vpld = prog.sem()
        fkld = prog.sem()
        mkld = prog.sem()
        wfl = prog.sem()

        conv_ops = []
        for (key, win_, nhb, nk) in (("w1", w1_in, 4, 8), ("w2", w2_in, 4, 8),
                                     ("h1", h1_in, 4, 32), ("h2", h2_in, 4, 32),
                                     ("w3", w3_in, 1, 32)):
            for hb_ in range(nhb):
                for k4_ in range(0, nk, 4):
                    def _cv(g, key=key, win_=win_, hb_=hb_, k4_=k4_):
                        return g.dma_start(
                            out=wbf[key][hb_, k4_:k4_ + 4].rearrange(
                                "k p c -> (k p) c"),
                            in_=win_[k4_ * 128:(k4_ + 4) * 128,
                                     hb_ * 1024:(hb_ + 1) * 1024])
                    conv_ops.append(_cv)
        N_CONV = len(conv_ops)
        wconv0 = prog.sem()
        conv0_ops = []
        for (key0, win0, nhb, nk) in (("wg", wg_in, 3, 8), ("wu", wu_in, 3, 8),
                                      ("wd", wd_in, 1, 24)):
            for hb_ in range(nhb):
                for k4_ in range(0, nk, 4):
                    def _cv0(g, key0=key0, win0=win0, hb_=hb_, k4_=k4_):
                        return g.dma_start(
                            out=wbf0[key0][hb_, k4_:k4_ + 4].rearrange(
                                "k p c -> (k p) c"),
                            in_=win0[k4_ * 128:(k4_ + 4) * 128,
                                     hb_ * 1024:(hb_ + 1) * 1024])
                    conv0_ops.append(_cv0)
        N_CONV0 = len(conv0_ops)

        def drip(n):
            for _ in range(min(n, len(conv_ops))):
                prog.op("gpsimd", conv_ops.pop(0), inc=wconv.inc(16))

        st = {"xb": xb, "wkS": wkS, "psum": psum, "rowA": rowA, "rowB": rowB,
              "ones_col": ones_col, "ones_row": ones_row,
              "cb_t": cb_t, "vec": vec, "scl": scl, "mms": mms, "nsync": nsync,
              "sto": sto}

        p_xf = contextlib.ExitStack()
        xf = p_xf.enter_context(nc.sbuf_tensor("xf", [128, DCH * T], F32))
        st["xf"] = xf

        for k in range(DCH):
            prog.op("sync", lambda s, k=k: s.dma_start(
                out=xf[:, k * T:(k + 1) * T], in_=x_in[k * 128:(k + 1) * 128, :]),
                inc=ld.inc(16))
        prog.op("sync", lambda s: s.dma_start(out=cb_t[:], in_=cb_in[:]),
                inc=ld.inc(16))

        pA = contextlib.ExitStack()
        sb = lambda name, shape, dt=F32: pA.enter_context(
            nc.sbuf_tensor(name, shape, dt))
        gw = sb("gw", [128, DCH * D], BF16)
        sc = scA
        msk = mskA
        zeros = sb("zeros", [128, T])
        _pw0 = sb("powstA", [128, T])
        _gs0 = sb("gstripA", [128, T])
        _vs0 = sb("vstripA", [128, T])
        powst = [_pw0, _pw0]
        gstrip = [_gs0, _gs0]
        vstrip = [_vs0, _vs0]
        wk1 = sb("wk1A", [128, T])
        wk2 = sb("wk2A", [128, T])

        for k in range(DCH):
            prog.op("gpsimd", lambda g, k=k: g.dma_start(
                out=xb[:, k * T:(k + 1) * T], in_=x_in[k * 128:(k + 1) * 128, :]),
                inc=ld.inc(16))
            prog.op("gpsimd", lambda g, k=k: g.dma_start(
                out=gw[:, k * D:(k + 1) * D], in_=gwT_in[k * 128:(k + 1) * 128, :]),
                inc=ld.inc(16))
        for (dst, src) in [(a_t, a_in), (b_t, b_in), (gb_t, gb_in),
                           (csel_t, csel_in)]:
            prog.op("sync", lambda s, dst=dst, src=src: s.dma_start(
                out=dst[:], in_=src[:]), inc=ld.inc(16))
        lded = ld.n
        for _cv0 in conv0_ops:
            prog.op("gpsimd", _cv0, inc=wconv0.inc(16))

        prog.op("vector", lambda v: v.memset(zeros[:], 0.0))
        def _tanh(v):
            emit_tanh(SyncedEngine(v, nsync), sc, a_t, a_raw, msk)
            return v.engine_nop()
        prog.op("vector", _tanh, waits=[(ld, lded)])

        def _asafe(v0):
            v = SyncedEngine(v0, nsync)
            t1, t2 = sc["tmp"], sc["s"]
            v.tensor_scalar(t1[:], a_raw[:], -1.0, None, ALU.mult)
            v.tensor_max(t1[:], t1[:], a_raw[:])
            v.tensor_scalar(t1[:], t1[:], 1e-4, None, ALU.max)
            v.tensor_single_scalar(t2[:], a_raw[:], 0.0, ALU.is_lt)
            v.tensor_scalar(t2[:], t2[:], -2.0, 1.0, ALU.mult, ALU.add)
            v.tensor_mul(asafe[:], t1[:], t2[:])
            return v0.engine_nop()
        prog.op("vector", _asafe)

        def _carries(v0):
            v = SyncedEngine(v0, nsync)
            p2, p4, p6, t1 = sc["p"], sc["q"], sc["y"], sc["tmp"]
            v.tensor_copy(p2[:], asafe[:])
            for _ in range(11):
                v.tensor_mul(p2[:], p2[:], p2[:])
            v.tensor_mul(p4[:], p2[:], p2[:])
            v.tensor_mul(p6[:], p4[:], p2[:])
            v.tensor_scalar(carry[:], zeros[:, 0:8], csel_t[:, 0:1], None, ALU.add)
            v.scalar_tensor_tensor(t1[:], p2[:], csel_t[:, 1:2], carry[:],
                                   ALU.mult, ALU.add)
            v.scalar_tensor_tensor(carry[:], p4[:], csel_t[:, 2:3], t1[:],
                                   ALU.mult, ALU.add)
            v.scalar_tensor_tensor(t1[:], p6[:], csel_t[:, 3:4], carry[:],
                                   ALU.mult, ALU.add)
            v.tensor_copy(carry[:], t1[:])
            return v0.engine_nop()
        prog.op("vector", _carries, inc=vec.inc())
        vec_init = vec.n

        vec_pow = [0] * DCH; vec_v = [0] * DCH
        scl_evac = [0] * DCH; scl_rch = [0] * DCH
        mm_done = [0] * DCH
        sto_pow = [0] * DCH; sto_v = [0] * DCH

        for m in range(DCH):
            par = m % 2
            pw, gs, vs = powst[par], gstrip[par], vstrip[par]
            xf_m = xf[:, m * T:(m + 1) * T]

            w = [(vec, vec_init)]
            if m >= 1:
                w += [(scl, scl_rch[m - 1]), (sto, sto_pow[m - 1])]
            def _powscan(v, pw=pw, m=m):
                v.tensor_scalar(wk1[:], zeros[:], asafe[:, m:m + 1], None, ALU.add)
                v.tensor_copy(pw[:, 0:1], carry[:, m:m + 1])
                return v.tensor_tensor_scan(
                    pw[:, 1:T], wk1[:, 1:T], zeros[:, 1:T],
                    carry[:, m:m + 1], ALU.mult, ALU.add)
            prog.op("vector", _powscan, waits=w, inc=vec.inc())
            vec_pow[m] = vec.n

            w = [(ld, lded)]
            if m >= 1:
                w.append((scl, scl_evac[m - 1]))
                w.append((vec, vec_v[m - 1]))
            def _gatemm(t, m=m, par=par):
                last = None
                for n in range(NT):
                    bank = psum[4 * par + n]
                    for k in range(DCH):
                        last = t.matmul(
                            bank[:],
                            gw[:, k * D + m * 128: k * D + (m + 1) * 128],
                            xb[:, k * T + n * 512: k * T + (n + 1) * 512],
                            start=(k == 0), stop=(k == DCH - 1))
                return last
            prog.op("tensor", _gatemm, waits=w, inc=mms.inc())
            mm_done[m] = mms.n

            for n in range(NT):
                w = []
                if n == 0:
                    w.append((mms, mm_done[m]))
                    if m >= 1:
                        w.append((vec, vec_v[m - 1]))
                prog.op("scalar", lambda s, n=n, par=par, gs=gs, m=m: s.activation(
                    gs[:, n * 512:(n + 1) * 512], psum[4 * par + n][:],
                    AF.Sigmoid, bias=gb_t[:, m:m + 1]),
                    waits=w, inc=scl.inc() if n == NT - 1 else None)
            scl_evac[m] = scl.n

            def _rchain(s, pw=pw):
                s.activation(wk2[:], pw[:], AF.Abs, bias=cb_t[:, 0:1])
                s.activation(wk2[:], wk2[:], AF.Ln)
                s.activation(wk2[:], wk2[:], AF.Exp, scale=-1.0)
                return s.sign(wk1[:], pw[:], bias=cb_t[:, 0:1])
            prog.op("scalar", _rchain, waits=[(vec, vec_pow[m])], inc=scl.inc())
            scl_rch[m] = scl.n

            w = [(scl, scl_rch[m])]
            if m >= 1:
                w.append((sto, sto_v[m - 1]))
            def _vchain(v, gs=gs, vs=vs, m=m, xf_m=xf_m):
                v.tensor_mul(wk2[:], wk2[:], wk1[:])
                v.scalar_tensor_tensor(wk1[:], gs[:], b_t[:, m:m + 1], xf_m,
                                       ALU.mult, ALU.mult)
                return v.scalar_tensor_tensor(
                    vs[:], wk1[:], 1.0, wk2[:], ALU.mult, ALU.mult,
                    accum_out=vtot[:, m:m + 1])
            prog.op("vector", _vchain, waits=w, inc=vec.inc())
            vec_v[m] = vec.n

            prog.op("sync", lambda s, pw=pw, m=m: s.dma_start(
                out=pows_out[m * 128:(m + 1) * 128, :], in_=pw[:]),
                waits=[(vec, vec_pow[m])], inc=sto.inc(16))
            sto_pow[m] = sto.n
            prog.op("sync", lambda s, vs=vs, m=m: s.dma_start(
                out=v_out[m * 128:(m + 1) * 128, :], in_=vs[:]),
                waits=[(vec, vec_v[m])], inc=sto.inc(16))
            sto_v[m] = sto.n

        A_vec_end = vec.n
        prog.op("gpsimd", lambda g: g.dma_start(
            out=ccin0.rearrange("one (k p) -> p (one k)", p=128), in_=vtot[:],
            allow_slow_non_contiguous=True),
            waits=[(vec, A_vec_end)], inc=ld.inc(16))
        ld_cc0 = ld.n
        prog.op("gpsimd", lambda g: g.collective_compute(
            "AllGather", ALU.bypass,
            replica_groups=[[0, 1, 2, 3, 4, 5, 6, 7]],
            ins=[ccin0.opt()], outs=[ccout0.opt()]),
            waits=[(ld, ld_cc0)], inc=ccs.inc(None))
        A_sto_end = sto.n
        A_mm_end = mms.n
        A_scl_end = scl.n
        A_ld_end = ld.n
        prog.set_barrier([(ld, ld.n), (vec, vec.n), (scl, scl.n),
                          (mms, mms.n), (sto, sto.n)])
        prog.set_barrier([(ld, ld.n), (vec, vec.n), (scl, scl.n),
                          (mms, mms.n), (sto, sto.n), (ccs, 1)])

        pA.close()

        p1 = contextlib.ExitStack()
        sb1 = lambda name, shape, dt=F32: p1.enter_context(
            nc.sbuf_tensor(name, shape, dt))
        vst = [sb1(f"vst{i}", [128, T]) for i in range(2)]
        pwst = [sb1(f"pwst{i}", [128, T]) for i in range(2)]
        agt = sb1("agt", [128, 64])
        maskt = sb1("maskt", [128, 64])
        carry0 = sb1("carry0", [128, 8])

        first_small = [True]
        for (dst, srcd) in [(c_t, c_in), (d1_t, d_in),
                            (maskt, mask0_in)]:
            w = [(vec, A_vec_end)] if first_small[0] else []
            first_small[0] = False
            prog.op("sync", lambda s, dst=dst, srcd=srcd: s.dma_start(
                out=dst[:], in_=srcd[:]), waits=w, inc=ld.inc(16))
        lded0 = ld.n

        prog.op("vector", lambda v: v.memset(ones_col[:], 1.0))
        prog.op("vector", lambda v: v.memset(ones_row[:], 1.0))
        prog.op("vector", lambda v: v.memset(wkS[1][:], 0.0))

        def _d1(v0):
            v = SyncedEngine(v0, nsync)
            v.tensor_scalar(d1_t[:], d1_t[:], 1.0, None, ALU.add)
            return v0.engine_nop()
        prog.op("vector", _d1, waits=[(ld, lded0)], inc=vec.inc())

        cb_vec = emit_carry_combine(prog, "gpsimd", agt, maskt, carry0, nsync,
                                    vta_in, 8, ld, vec,
                                    waits=[(ld, lded0), (ccs, 1),
                                           (ld, ld_cc0)])

        vec_z0 = [0] * DCH
        for m in range(DCH):
            par = m % 2
            vs, pw = vst[par], pwst[par]
            w = []
            if m >= 2:
                w.append((vec, vec_z0[m - 2]))
            if m == 0:
                w += [(vec, A_vec_end), (sto, A_sto_end), (ld, ld_cc0),
                      (scl, A_scl_end), (mms, A_mm_end)]
            prog.op("gpsimd", lambda g, m=m, vs=vs: g.dma_start(
                out=vs[:], in_=v_in[m * 128:(m + 1) * 128, :]),
                waits=w, inc=vpld.inc(16))
            prog.op("gpsimd", lambda g, m=m, pw=pw: g.dma_start(
                out=pw[:], in_=pows_in[m * 128:(m + 1) * 128, :]),
                inc=vpld.inc(16))
            ld_m = vpld.n

            def _scan_z0(v, m=m, vs=vs, pw=pw):
                v.tensor_tensor_scan(vs[:], vs[:], wkS[1][:, 0:T],
                                     carry0[:, m:m + 1], ALU.add, ALU.add)
                v.scalar_tensor_tensor(wkS[0][:], vs[:], c_t[:, m:m + 1], pw[:],
                                       ALU.mult, ALU.mult)
                return v.scalar_tensor_tensor(
                    xf[:, m * T:(m + 1) * T], xf[:, m * T:(m + 1) * T],
                    d1_t[:, m:m + 1], wkS[0][:], ALU.mult, ALU.add)
            prog.op("vector", _scan_z0, waits=[(vpld, ld_m), (vec, cb_vec)],
                    inc=vec.inc())
            vec_z0[m] = vec.n

        prog.set_barrier([(ld, ld.n), (vec, vec.n), (scl, scl.n),
                          (mms, mms.n), (sto, sto.n)])
        ms0 = emit_layernorm(prog, st, 2, {"z_ready_vec": vec_z0[DCH - 1]})
        p1.close()

        prog.set_barrier([(ld, ld.n), (vec, vec.n), (scl, scl.n),
                          (mms, mms.n), (sto, sto.n)])
        p3 = contextlib.ExitStack()
        sb3 = lambda name, shape, dt=F32: p3.enter_context(
            nc.sbuf_tensor(name, shape, dt))
        wcb0 = [sb3("wcb0A", [128, 2048], BF16), sb3("wcb0B", [128, 2048], BF16)]
        gsil = sb3("gsil", [128, 8 * 512], BF16)
        hh = sb3("hh", [128, 24 * 512], BF16)

        bfv0 = ms0["banks_busy_until_vec"]
        bfs0 = ms0["scl_shadow"]
        wcb0_state = {"par": 0, "guard": [0, 0], "first": True}

        def bank_waits0(extra=()):
            w = list(extra)
            if bfv0:
                w.append((vec, bfv0))
            if bfs0:
                w.append((scl, bfs0))
            return w

        def ffn0_stage(key0, hb, in_ch, src_fn, entry=False):
            mm_last = 0
            for g2 in range(in_ch // 2):
                par = wcb0_state["par"] % 2
                wcb0_state["par"] += 1
                w = [(mms, wcb0_state["guard"][par])]
                if wcb0_state["first"]:
                    w.append((wconv0, 16 * N_CONV0))
                    w.append((scl, ms0["scl_shadow"]))
                    wcb0_state["first"] = False
                for j in range(2):
                    prog.op("gpsimd", lambda g, key0=key0, hb=hb, g2=g2, j=j,
                            par=par: g.dma_start(
                                out=wcb0[par][:, j * 1024:(j + 1) * 1024],
                                in_=wbf0[key0][hb, g2 * 2 + j]),
                            waits=[x for x in w if x[1] > 0] if j == 0 else (),
                            inc=wl0.inc(16))
                ld_g = wl0.n
                w = [(wl0, ld_g)]
                if g2 == 0 and entry:
                    w = bank_waits0(w)

                def _mm(t, g2=g2, par=par, in_ch=in_ch, src_fn=src_fn):
                    last = None
                    for j in range(2):
                        kk = g2 * 2 + j
                        for mh in range(8):
                            last = t.matmul(
                                psum[mh][:],
                                wcb0[par][:, j * 1024 + mh * 128:
                                          j * 1024 + (mh + 1) * 128],
                                src_fn(kk),
                                start=(kk == 0), stop=(kk == in_ch - 1))
                    return last
                prog.op("tensor", _mm, waits=w, inc=mms.inc())
                wcb0_state["guard"][0] = mms.n
                wcb0_state["guard"][1] = mms.n
                mm_last = mms.n
            return mm_last

        for qt in range(4):
            tk = qt * 512
            for blk in range(3):
                mm_g = ffn0_stage("wg", blk, 8,
                                  lambda kk, tk=tk: xb[:, kk * T + tk:
                                                       kk * T + tk + 512],
                                  entry=True)
                for mh in range(8):
                    prog.op("scalar", lambda s, mh=mh: s.activation(
                        gsil[:, mh * 512:(mh + 1) * 512],
                        psum[mh][:], AF.Silu, scale=1.0 / 127.0),
                        waits=[(mms, mm_g)] if mh == 0 else (),
                        inc=scl.inc() if mh == 7 else None)
                bfs0 = scl.n
                bfv0 = 0
                mm_u = ffn0_stage("wu", blk, 8,
                                  lambda kk, tk=tk: xb[:, kk * T + tk:
                                                       kk * T + tk + 512],
                                  entry=True)

                def _hh(v, blk=blk):
                    last = None
                    for mh in range(8):
                        last = v.scalar_tensor_tensor(
                            hh[:, (blk * 8 + mh) * 512:(blk * 8 + mh + 1) * 512],
                            psum[mh][:], 1.0 / 127.0,
                            gsil[:, mh * 512:(mh + 1) * 512],
                            ALU.mult, ALU.mult)
                    return last
                prog.op("vector", _hh, waits=[(mms, mm_u), (scl, bfs0)],
                        inc=vec.inc())
                bfv0 = vec.n
                bfs0 = 0

            mm_d = ffn0_stage("wd", 0, 24,
                              lambda kk: hh[:, kk * 512:(kk + 1) * 512],
                              entry=True)

            def _z1(v, tk=tk):
                last = None
                for m in range(DCH):
                    sl = slice(m * T + tk, m * T + tk + 512)
                    last = v.scalar_tensor_tensor(
                        xf[:, sl], psum[m][:], 1.0 / 127.0, xf[:, sl],
                        ALU.mult, ALU.add)
                return last
            prog.op("vector", _z1, waits=[(mms, mm_d)], inc=vec.inc())
            bfv0 = vec.n
            bfs0 = 0
            drip(22)
        vec_z1_all = vec.n
        prog.set_barrier([(ld, ld.n), (vec, vec.n), (scl, scl.n),
                          (mms, mms.n), (sto, sto.n)])
        ms1 = emit_layernorm(prog, st, 2, {"z_ready_vec": vec_z1_all,
                                           "banks_free_vec": vec_z1_all})
        sto_y1 = 0
        for m in range(DCH):
            prog.op("sync", lambda s, m=m: s.dma_start(
                out=y1_out[m * 128:(m + 1) * 128, :],
                in_=xf[:, m * T:(m + 1) * T]),
                waits=[(vec, ms1["vec_apply"])] if m == 0 else (),
                inc=sto.inc(16))
        sto_y1 = sto.n
        p3.close()
        p_xf.close()

        prog.set_barrier([(ld, ld.n), (vec, vec.n), (scl, scl.n),
                          (mms, mms.n), (sto, sto.n)])
        p4 = contextlib.ExitStack()
        sb4 = lambda name, shape, dt=F32: p4.enter_context(
            nc.sbuf_tensor(name, shape, dt))
        wfull = sb4("wfull", [128, DCH * 1024], BF16)
        cfull = sb4("cfull", [128, DCH * FPAD], BF16)
        sfull = sb4("sfull", [128, DCH * FPAD], BF16)
        mstag = sb4("mstag", [128, DCH * FPAD], BF16)

        for e in range(DCH):
            prog.op("sync", lambda s, e=e: s.dma_start(
                out=cfull[:, e * FPAD:(e + 1) * FPAD],
                in_=cmat_in[e * 128:(e + 1) * 128, :]),
                waits=[(sto, sto_y1)] if e == 0 else (), inc=ld.inc(16))
            prog.op("sync", lambda s, e=e: s.dma_start(
                out=sfull[:, e * FPAD:(e + 1) * FPAD],
                in_=smat_in[e * 128:(e + 1) * 128, :]), inc=ld.inc(16))
        ld_cs = ld.n

        evac_scl_prev = ms1["scl_shadow"]
        bank_free_vec = ms1["banks_busy_until_vec"]
        sto_mscr = 0
        for mat in ("k", "v", "q"):
            for e in range(DCH):
                prog.op("gpsimd", lambda g, e=e, mat=mat: g.dma_start(
                    out=wfull[:, e * 1024:(e + 1) * 1024],
                    in_=w_qkv_in[mat][e * 128:(e + 1) * 128, :]),
                    waits=[(mms, mms.n), (sto, sto_y1)] if e == 0 else (),
                    inc=wfl.inc(16))
            ld_w = wfl.n
            drip(8)
            for (part, csrc) in (("re", cfull), ("im", sfull)):
                scr = mscr[mat + part]
                for (ftw, ft0) in ((512, 0), (128, 512)):
                    w = [(wfl, ld_w), (ld, ld_cs), (scl, evac_scl_prev),
                         (vec, bank_free_vec)]
                    def _mmm(t, csrc=csrc, ftw=ftw, ft0=ft0):
                        last = None
                        for e in range(DCH):
                            for m in range(DCH):
                                last = t.matmul(
                                    psum[m][0:128, 0:ftw],
                                    wfull[:, e * 1024 + m * 128:
                                          e * 1024 + (m + 1) * 128],
                                    csrc[:, e * FPAD + ft0:e * FPAD + ft0 + ftw],
                                    start=(e == 0), stop=(e == DCH - 1))
                        return last
                    prog.op("tensor", _mmm, waits=w, inc=mms.inc())
                    mm_m = mms.n
                    for m in range(DCH):
                        w = []
                        if m == 0:
                            w = [(mms, mm_m)]
                            if sto_mscr:
                                w.append((sto, sto_mscr))
                        prog.op("scalar", lambda s, m=m, ft0=ft0, ftw=ftw:
                                s.activation(
                                    mstag[:, m * FPAD + ft0:m * FPAD + ft0 + ftw],
                                    psum[m][0:128, 0:ftw], AF.Copy),
                                waits=w,
                                inc=scl.inc() if m == DCH - 1 else None)
                    evac_scl_prev = scl.n
                    bank_free_vec = 0
                for m in range(DCH):
                    prog.op("sync", lambda s, m=m, scr=scr: s.dma_start(
                        out=scr[m * 128:(m + 1) * 128, :],
                        in_=mstag[:, m * FPAD:(m + 1) * FPAD]),
                        waits=[(scl, evac_scl_prev)] if m == 0 else (),
                        inc=sto.inc(16))
                sto_mscr = sto.n
        p4.close()

        prog.set_barrier([(ld, ld.n), (vec, vec.n), (scl, scl.n),
                          (mms, mms.n), (sto, sto.n)])
        p5 = contextlib.ExitStack()
        sb5 = lambda name, shape, dt=F32: p5.enter_context(
            nc.sbuf_tensor(name, shape, dt))
        At = sb5("At", [128, T]); Bt = sb5("Bt", [128, T])
        Ct = sb5("Ct", [128, T]); Dt = sb5("Dt", [128, T])
        wf1 = sb5("wf1", [128, T]); wf2 = sb5("wf2", [128, T])
        scr1 = sb5("scr1", [128, T]); scr2 = sb5("scr2", [128, T])
        pre = [sb5(f"pre{i}", [128, T], BF16) for i in range(2)]
        pim = [sb5(f"pim{i}", [128, T], BF16) for i in range(2)]
        mk1 = sb5("mk1", [128, DCH * 128], BF16)
        mk2 = sb5("mk2", [128, DCH * 128], BF16)
        ptot = sb5("ptot", [128, 10])

        def proj_mms(tag, c, blk_re, blk_im, bank0_wait_scl, extra_waits=()):
            w = list(extra_waits) + [(scl, bank0_wait_scl)]
            def _p(t, c=c, blk_re=blk_re, blk_im=blk_im):
                last = None
                for n in range(NT):
                    for k in range(DCH):
                        last = t.matmul(
                            psum[n][:],
                            blk_re[:, k * 128:(k + 1) * 128],
                            xb[:, k * T + n * 512:k * T + (n + 1) * 512],
                            start=(k == 0), stop=(k == DCH - 1))
                for n in range(NT):
                    for k in range(DCH):
                        last = t.matmul(
                            psum[4 + n][:],
                            blk_im[:, k * 128:(k + 1) * 128],
                            xb[:, k * T + n * 512:k * T + (n + 1) * 512],
                            start=(k == 0), stop=(k == DCH - 1))
                return last
            prog.op("tensor", _p, waits=[x for x in w if x[1] > 0], inc=mms.inc())
            return mms.n

        def evac8(dst_re, dst_im, mm_wait, extra=()):
            for n in range(NT):
                w = ([(mms, mm_wait)] + [x for x in extra if x[1] > 0]) \
                    if n == 0 else ()
                prog.op("scalar", lambda s, n=n, dst_re=dst_re: s.activation(
                    dst_re[:, n * 512:(n + 1) * 512], psum[n][:], AF.Copy),
                    waits=w)
            for n in range(NT):
                prog.op("scalar", lambda s, n=n, dst_im=dst_im: s.activation(
                    dst_im[:, n * 512:(n + 1) * 512], psum[4 + n][:], AF.Copy),
                    inc=scl.inc() if n == NT - 1 else None)
            return scl.n

        def load_mblks(names, c, dst1, dst2, waits0):
            for (nm, dstt) in ((names[0], dst1), (names[1], dst2)):
                for k in range(DCH):
                    w = [x for x in waits0 if x[1] > 0] \
                        if (k == 0 and nm == names[0]) else ()
                    prog.op("gpsimd", lambda g, nm=nm, k=k, c=c, dstt=dstt:
                            g.dma_start(
                                out=dstt[:, k * 128:(k + 1) * 128],
                                in_=mscr[nm][k * 128:(k + 1) * 128,
                                             c * 128:(c + 1) * 128]),
                            waits=w, inc=mkld.inc(16))
            return mkld.n

        def normalize(mm_evac_scl, vec_guard):
            w = [(scl, mm_evac_scl)]
            if vec_guard:
                w.append((vec, vec_guard))
            prog.op("scalar", lambda s: s.activation(wf1[:], At[:], AF.Square),
                    waits=w)
            prog.op("scalar", lambda s: s.activation(wf2[:], Bt[:], AF.Square),
                    inc=scl.inc())
            s_sq = scl.n
            prog.op("vector", lambda v: v.tensor_add(wf1[:], wf1[:], wf2[:]),
                    waits=[(scl, s_sq)], inc=vec.inc())
            v_add = vec.n
            prog.op("scalar", lambda s: s.activation(wf2[:], wf1[:], AF.Sqrt),
                    waits=[(vec, v_add)])
            prog.op("scalar", lambda s: s.activation(
                wf2[:], wf2[:], AF.Ln, bias=cb_t[:, 1:2]))
            prog.op("scalar", lambda s: s.activation(
                wf2[:], wf2[:], AF.Exp, scale=-1.0), inc=scl.inc())
            s_rn = scl.n
            def _nk(v):
                v.tensor_mul(At[:], At[:], wf2[:])
                return v.tensor_mul(Bt[:], Bt[:], wf2[:])
            prog.op("vector", _nk, waits=[(scl, s_rn)], inc=vec.inc())
            return vec.n

        vec_work_last = 0
        scl_evac_prev = evac_scl_prev
        sto_pr = [0, 0]
        mm_blk_last = 0
        for c in range(FCH):
            par = c % 2
            ld_k = load_mblks(("kre", "kim"), c, mk1, mk2,
                              [(sto, sto_mscr), (mms, mm_blk_last)])
            mm_k = proj_mms("k", c, mk1, mk2, scl_evac_prev,
                            extra_waits=[(mkld, ld_k)])
            mm_blk_last = mm_k
            scl_evac_prev = evac8(At, Bt, mm_k,
                                  extra=[(vec, vec_work_last)])
            ld_v = load_mblks(("vre", "vim"), c, mk1, mk2,
                              [(mms, mm_k)])
            mm_v = proj_mms("v", c, mk1, mk2, scl_evac_prev,
                            extra_waits=[(mkld, ld_v)])
            mm_blk_last = mm_v
            scl_evac_prev = evac8(Ct, Dt, mm_v)
            v_nk = normalize(scl_evac_prev, vec_work_last)
            def _prods(v, c=c, par=par):
                v.tensor_mul(scr1[:], At[:], Ct[:])
                v.tensor_mul(scr2[:], At[:], Dt[:])
                v.tensor_mul(Ct[:], Bt[:], Ct[:])
                v.tensor_mul(Dt[:], Bt[:], Dt[:])
                v.scalar_tensor_tensor(pre[par][:], scr1[:], 1.0, Dt[:],
                                       ALU.mult, ALU.subtract,
                                       accum_out=ptot[:, 2 * c:2 * c + 1])
                return v.scalar_tensor_tensor(
                    pim[par][:], scr2[:], 1.0, Ct[:], ALU.mult, ALU.add,
                    accum_out=ptot[:, 2 * c + 1:2 * c + 2])
            w = [(vec, v_nk)]
            if sto_pr[par]:
                w.append((sto, sto_pr[par]))
            prog.op("vector", _prods, waits=w, inc=vec.inc())
            vec_work_last = vec.n
            prog.op("sync", lambda s, c=c, par=par: s.dma_start(
                out=pr_out["re"][c * 128:(c + 1) * 128, :], in_=pre[par][:]),
                waits=[(vec, vec_work_last)], inc=sto.inc(16))
            prog.op("sync", lambda s, c=c, par=par: s.dma_start(
                out=pr_out["im"][c * 128:(c + 1) * 128, :], in_=pim[par][:]),
                inc=sto.inc(16))
            sto_pr[par] = sto.n

        for c in range(FCH):
            par = c % 2
            ld_q = load_mblks(("qre", "qim"), c, mk1, mk2,
                              [(mms, mm_blk_last)])
            mm_q = proj_mms("q", c, mk1, mk2, scl_evac_prev,
                            extra_waits=[(mkld, ld_q)])
            mm_blk_last = mm_q
            scl_evac_prev = evac8(At, Bt, mm_q,
                                  extra=[(vec, vec_work_last)])
            v_nq = normalize(scl_evac_prev, vec_work_last)
            def _cast(v, par=par):
                v.tensor_copy(pre[par][:], At[:])
                return v.tensor_copy(pim[par][:], Bt[:])
            w = [(vec, v_nq)]
            if sto_pr[par]:
                w.append((sto, sto_pr[par]))
            prog.op("vector", _cast, waits=w, inc=vec.inc())
            vec_work_last = vec.n
            prog.op("sync", lambda s, c=c, par=par: s.dma_start(
                out=nq_out["re"][c * 128:(c + 1) * 128, :], in_=pre[par][:]),
                waits=[(vec, vec_work_last)], inc=sto.inc(16))
            prog.op("sync", lambda s, c=c, par=par: s.dma_start(
                out=nq_out["im"][c * 128:(c + 1) * 128, :], in_=pim[par][:]),
                inc=sto.inc(16))
            sto_pr[par] = sto.n

        B_vec_end = vec.n
        prog.op("gpsimd", lambda g: g.dma_start(
            out=ccin1.rearrange("one (k p) -> p (one k)", p=128), in_=ptot[:],
            allow_slow_non_contiguous=True),
            waits=[(vec, B_vec_end)], inc=ld.inc(16))
        ld_cc1 = ld.n
        prog.op("gpsimd", lambda g: g.collective_compute(
            "AllGather", ALU.bypass,
            replica_groups=[[0, 1, 2, 3, 4, 5, 6, 7]],
            ins=[ccin1.opt()], outs=[ccout1.opt()]),
            waits=[(ld, ld_cc1)], inc=ccs1.inc(None))
        sto_after_B = sto.n
        mm_after_B = mms.n
        p5.close()
        p4.close()

        sb = lambda name, shape, dt=F32: stack.enter_context(
            nc.sbuf_tensor(name, shape, dt))
        s1_t = sb("s1_t", [128, 32]); s2_t = sb("s2_t", [128, 32])
        h1s_t = sb("h1s_t", [128, 32]); h2s_t = sb("h2s_t", [128, 32])
        w3s_t = sb("w3s_t", [128, 8])

        xfc = stack.enter_context(nc.sbuf_tensor("xf2", [128, DCH * T], F32))
        st["xf"] = xfc

        prog.set_barrier([(ld, ld.n), (vec, vec.n), (scl, scl.n),
                          (mms, mms.n), (sto, sto.n), (ccs1, 1)])
        p1 = contextlib.ExitStack()
        sb1 = lambda name, shape, dt=F32: p1.enter_context(
            nc.sbuf_tensor(name, shape, dt))
        agt1 = sb1("agt1", [128, 80])
        maskt1 = sb1("maskt1", [128, 80])
        carry1 = sb1("carry1", [128, 10])
        fkv = {"re": sb1("fkv_re", [128, FCH * T], BF16),
               "im": sb1("fkv_im", [128, FCH * T], BF16)}
        p2 = contextlib.ExitStack()
        sb2 = lambda name, shape, dt=F32: p2.enter_context(
            nc.sbuf_tensor(name, shape, dt))
        nqt = [sb2(f"nqt{i}", [128, T], BF16) for i in range(2)]
        sc1 = sb2("sc1", [128, T]); sc2 = sb2("sc2", [128, T])
        sc3 = sb2("sc3", [128, T], BF16)

        for k in range(DCH):
            prog.op("sync", lambda s, k=k: s.dma_start(
                out=xfc[:, k * T:(k + 1) * T], in_=y1_in[k * 128:(k + 1) * 128, :]),
                waits=[(sto, sto_y1), (mms, mm_after_B)] if k == 0 else (),
                inc=ld.inc(16))
        for (dst, srcd) in [(maskt1, mask1_in), (s1_t, s1_in),
                            (s2_t, s2_in), (h1s_t, h1s_in), (h2s_t, h2s_in),
                            (w3s_t, w3s_in)]:
            prog.op("sync", lambda s, dst=dst, srcd=srcd: s.dma_start(
                out=dst[:], in_=srcd[:]), inc=ld.inc(16))
        lded0 = ld.n
        prog.op("vector", lambda v: v.memset(wkS[1][:], 0.0),
                waits=[(mms, mm_after_B)])
        cb1_vec = emit_carry_combine(prog, "gpsimd", agt1, maskt1, carry1, nsync,
                                     pta_in, 10, ld, vec,
                                     waits=[(ld, lded0), (ccs1, 1),
                                            (ld, ld_cc1)])

        vec_fkvq = [0] * FCH
        for c in range(FCH):
            par = c % 2
            for (part, dstc) in (("re", 0), ("im", 1)):
                prog.op("gpsimd", lambda g, c=c, part=part: g.dma_start(
                    out=fkv[part][:, c * T:(c + 1) * T],
                    in_=pr_in[part][c * 128:(c + 1) * 128, :]),
                    waits=[(sto, sto_after_B)] if (c == 0 and part == "re")
                    else (), inc=fkld.inc(16))
            prog.op("gpsimd", lambda g, c=c: g.dma_start(
                out=nqt[0][:, :], in_=nq_in["re"][c * 128:(c + 1) * 128, :]),
                waits=[(vec, vec_fkvq[c - 1])] if c >= 1 else (),
                inc=fkld.inc(16))
            prog.op("gpsimd", lambda g, c=c: g.dma_start(
                out=nqt[1][:, :], in_=nq_in["im"][c * 128:(c + 1) * 128, :]),
                inc=fkld.inc(16))
            ld_c = fkld.n
            def _scanq(v, c=c):
                fre = fkv["re"][:, c * T:(c + 1) * T]
                fim = fkv["im"][:, c * T:(c + 1) * T]
                v.tensor_tensor_scan(fre, fre, wkS[1][:, 0:T],
                                     carry1[:, 2 * c:2 * c + 1],
                                     ALU.add, ALU.add)
                v.tensor_tensor_scan(fim, fim, wkS[1][:, 0:T],
                                     carry1[:, 2 * c + 1:2 * c + 2],
                                     ALU.add, ALU.add)
                v.tensor_mul(sc1[:], fre, nqt[0][:, :])
                v.tensor_mul(sc2[:], fim, nqt[1][:, :])
                v.tensor_mul(sc3[:], fre, nqt[1][:, :])
                v.tensor_tensor(fre, sc1[:], sc2[:], ALU.add)
                v.tensor_mul(sc1[:], fim, nqt[0][:, :])
                return v.tensor_tensor(fim, sc1[:], sc3[:], ALU.subtract)
            w = [(fkld, ld_c), (vec, cb1_vec)]
            prog.op("vector", _scanq, waits=w, inc=vec.inc())
            vec_fkvq[c] = vec.n
        vec_fkvq_all = vec.n

        prog.set_barrier([(ld, ld.n), (vec, vec.n), (scl, scl.n),
                          (mms, mms.n), (sto, sto.n)])
        p2.close()
        icb = sb1("icb", [128, FCH * D], BF16)
        isb = sb1("isb", [128, FCH * D], BF16)
        for c in range(FCH):
            prog.op("sync", lambda s, c=c: s.dma_start(
                out=icb[:, c * D:(c + 1) * D], in_=ic_in[c * 128:(c + 1) * 128, :]),
                waits=[(vec, vec_fkvq_all)] if c == 0 else (),
                inc=ld.inc(16))
            prog.op("sync", lambda s, c=c: s.dma_start(
                out=isb[:, c * D:(c + 1) * D], in_=is_in[c * 128:(c + 1) * 128, :]),
                inc=ld.inc(16))
        ld_ic = ld.n

        vec_z2 = 0
        for n in range(NT):
            w = [(ld, ld_ic), (vec, vec_fkvq_all)]
            if n > 0:
                w.append((vec, vec_z2))
            def _irf(t, n=n):
                last = None
                for c in range(FCH):
                    for (blkt, part) in ((icb, "re"), (isb, "im")):
                        for m in range(DCH):
                            last = t.matmul(
                                psum[m][:],
                                blkt[:, c * D + m * 128:c * D + (m + 1) * 128],
                                fkv[part][:, c * T + n * 512:c * T + (n + 1) * 512],
                                start=(c == 0 and part == "re"),
                                stop=(c == FCH - 1 and part == "im"))
                return last
            prog.op("tensor", _irf, waits=w, inc=mms.inc())
            mm_ir = mms.n

            def _z2(v, n=n):
                last = None
                for m in range(DCH):
                    sl = slice(m * T + n * 512, m * T + (n + 1) * 512)
                    last = v.scalar_tensor_tensor(
                        xfc[:, sl], xfc[:, sl], 2.0, psum[m][:],
                        ALU.mult, ALU.add)
                return last
            prog.op("vector", _z2, waits=[(mms, mm_ir)], inc=vec.inc())
            vec_z2 = vec.n

        prog.set_barrier([(ld, ld.n), (vec, vec.n), (scl, scl.n),
                          (mms, mms.n), (sto, sto.n)])
        ms2 = emit_layernorm(prog, st, 2, {"z_ready_vec": vec_z2,
                                           "banks_free_vec": vec_z2})
        p1.close()

        prog.set_barrier([(ld, ld.n), (vec, vec.n), (scl, scl.n),
                          (mms, mms.n), (sto, sto.n)])
        p5 = contextlib.ExitStack()
        sb5 = lambda name, shape, dt=F32: p5.enter_context(
            nc.sbuf_tensor(name, shape, dt))
        habf = sb5("habf", [128, 32 * 512], BF16)
        habf2 = sb5("habf2", [128, 32 * 512], BF16)
        wcb2 = [sb5("wcbA", [128, 2048], BF16), sb5("wcbB", [128, 2048], BF16)]

        drip(99)
        bfv = ms2["banks_busy_until_vec"]
        bfs = ms2["scl_shadow"]
        wcb_state = {"par": 0,
                     "guard": [ms2["mms_bcast"], ms2["mms_bcast"]],
                     "first": True}

        def bank_waits(extra=()):
            w = list(extra)
            if bfv:
                w.append((vec, bfv))
            if bfs:
                w.append((scl, bfs))
            return w

        def ffn1_stage(key, hb, in_ch, src_fn, entry=False):
            mm_last = 0
            for g2 in range(in_ch // 2):
                par = wcb_state["par"] % 2
                wcb_state["par"] += 1
                w = [(mms, wcb_state["guard"][par])]
                if wcb_state["first"]:
                    w.append((wconv, 16 * N_CONV))
                    wcb_state["first"] = False
                for j in range(2):
                    prog.op("gpsimd", lambda g, key=key, hb=hb, g2=g2, j=j,
                            par=par: g.dma_start(
                                out=wcb2[par][:, j * 1024:(j + 1) * 1024],
                                in_=wbf[key][hb, g2 * 2 + j]),
                            waits=[x for x in w if x[1] > 0] if j == 0 else (),
                            inc=wl1.inc(16))
                ld_g = wl1.n
                w = [(wl1, ld_g)]
                if g2 == 0 and entry:
                    w = bank_waits(w)

                def _mm(t, g2=g2, par=par, in_ch=in_ch, src_fn=src_fn):
                    last = None
                    for j in range(2):
                        kk = g2 * 2 + j
                        for mh in range(8):
                            last = t.matmul(
                                psum[mh][:],
                                wcb2[par][:, j * 1024 + mh * 128:
                                          j * 1024 + (mh + 1) * 128],
                                src_fn(kk),
                                start=(kk == 0), stop=(kk == in_ch - 1))
                    return last
                prog.op("tensor", _mm, waits=w, inc=mms.inc())
                wcb_state["guard"][par] = mms.n
                mm_last = mms.n
            return mm_last

        for qt in range(4):
            tk = qt * 512
            for hb in range(4):
                mm_1 = ffn1_stage("w1", hb, 8,
                                  lambda kk, tk=tk: xb[:, kk * T + tk:
                                                       kk * T + tk + 512],
                                  entry=True)
                for mh in range(8):
                    prog.op("scalar", lambda s, mh=mh, hb=hb: s.activation(
                        habf[:, (hb * 8 + mh) * 512:(hb * 8 + mh + 1) * 512],
                        psum[mh][:],
                        AF.Silu, scale=s1_t[:, hb * 8 + mh:hb * 8 + mh + 1]),
                        waits=[(mms, mm_1)] if mh == 0 else (),
                        inc=scl.inc() if mh == 7 else None)
                bfs = scl.n
                bfv = 0
                mm_2 = ffn1_stage("w2", hb, 8,
                                  lambda kk, tk=tk: xb[:, kk * T + tk:
                                                       kk * T + tk + 512],
                                  entry=True)

                def _ha(v, hb=hb):
                    last = None
                    for mh in range(8):
                        sl = slice((hb * 8 + mh) * 512, (hb * 8 + mh + 1) * 512)
                        last = v.scalar_tensor_tensor(
                            habf[:, sl], psum[mh][:],
                            s2_t[:, hb * 8 + mh:hb * 8 + mh + 1],
                            habf[:, sl], ALU.mult, ALU.mult)
                    return last
                prog.op("vector", _ha, waits=[(mms, mm_2), (scl, bfs)],
                        inc=vec.inc())
                bfv = vec.n
                bfs = 0

            for (key, sclt, src_a, dst_a) in (("h1", h1s_t, habf, habf2),
                                              ("h2", h2s_t, habf2, habf)):
                for hb in range(4):
                    mm_last = ffn1_stage(key, hb, 32,
                                         lambda kk, src_a=src_a: src_a[
                                             :, kk * 512:(kk + 1) * 512],
                                         entry=True)
                    for mh in range(8):
                        prog.op("scalar", lambda s, mh=mh, hb=hb, sclt=sclt,
                                dst_a=dst_a: s.activation(
                                    dst_a[:, (hb * 8 + mh) * 512:
                                          (hb * 8 + mh + 1) * 512],
                                    psum[mh][:], AF.Gelu,
                                    scale=sclt[:, hb * 8 + mh:hb * 8 + mh + 1]),
                                waits=[(mms, mm_last)] if mh == 0 else (),
                                inc=scl.inc() if mh == 7 else None)
                    bfs = scl.n
                    bfv = 0

            mm_wd_last = ffn1_stage("w3", 0, 32,
                                    lambda kk: habf[:, kk * 512:(kk + 1) * 512],
                                    entry=True)

            def _z3(v, tk=tk):
                last = None
                for m in range(8):
                    sl = slice(m * T + tk, m * T + tk + 512)
                    last = v.scalar_tensor_tensor(
                        xfc[:, sl], psum[m][:], w3s_t[:, m:m + 1], xfc[:, sl],
                        ALU.mult, ALU.add)
                return last
            prog.op("vector", _z3, waits=[(mms, mm_wd_last)], inc=vec.inc())
            bfv = vec.n
            bfs = 0
        vec_z3_all = vec.n

        prog.set_barrier([(ld, ld.n), (vec, vec.n), (scl, scl.n),
                          (mms, mms.n), (sto, sto.n)])
        ms3 = emit_layernorm(prog, st, 2, {"z_ready_vec": vec_z3_all,
                                           "banks_free_vec": vec_z3_all})
        for m in range(DCH):
            prog.op("sync", lambda s, m=m: s.dma_start(
                out=y_out[m * 128:(m + 1) * 128, :],
                in_=xfc[:, m * T:(m + 1) * T]),
                waits=[(vec, ms3["vec_apply"])] if m == 0 else (),
                inc=sto.inc(16))
        prog.op("sync", lambda s: None, waits=[(sto, sto.n)])

        p5.close()

        prog.finalize()
    return nc



def _pack128(vec):
    return np.ascontiguousarray(np.asarray(vec).reshape(-1, 128).T).astype(np.float32)


def _dft_consts():
    e = np.arange(D, dtype=np.float64)[:, None]
    f = np.arange(FPAD, dtype=np.float64)[None, :]
    ang = 2.0 * np.pi * e * f / D
    C = np.cos(ang)
    S = -np.sin(ang)
    C[:, 513:] = 0.0
    S[:, 513:] = 0.0
    return (C.astype(ml_dtypes.bfloat16), S.astype(ml_dtypes.bfloat16))


def _idft_consts():
    f = np.arange(FPAD, dtype=np.float64)[:, None]
    d = np.arange(D, dtype=np.float64)[None, :]
    w = np.where((f == 0) | (f == 512), 1.0, 2.0)
    w[f >= 513] = 0.0
    ang = 2.0 * np.pi * f * d / D
    IC = w * np.cos(ang) / D
    IS = -w * np.sin(ang) / D
    return (IC.astype(ml_dtypes.bfloat16), IS.astype(ml_dtypes.bfloat16))


def _prep_inputs(inputs):
    x = inputs["L0_attn_x"]
    cmat, smat = _dft_consts()
    icmat, ismat = _idft_consts()
    shared = {
        "gwT": np.ascontiguousarray(inputs["L0_attn_gate_W"].T),
        "a_v": _pack128(inputs["L0_attn_a_logit"]),
        "b_v": _pack128(inputs["L0_attn_b"]),
        "c_v": _pack128(inputs["L0_attn_c"]),
        "d_v": _pack128(inputs["L0_attn_d"]),
        "gb_v": _pack128(inputs["L0_attn_gate_b"]),
        "cb_v": np.ascontiguousarray(
            np.broadcast_to(np.array([1e-12, 1e-8, 1e-5, 0.0], np.float32),
                            (128, 4))),
        "wgT_i": np.ascontiguousarray(inputs["L0_ffn_w_gate"].T),
        "wuT_i": np.ascontiguousarray(inputs["L0_ffn_w_up"].T),
        "wdT_i": np.ascontiguousarray(inputs["L0_ffn_w_down"].T),
        "wq_r": np.ascontiguousarray(inputs["L1_attn_Wq"]),
        "wk_r": np.ascontiguousarray(inputs["L1_attn_Wk"]),
        "wv_r": np.ascontiguousarray(inputs["L1_attn_Wv"]),
        "cmat": cmat,
        "smat": smat,
        "icmat": icmat,
        "ismat": ismat,
        "w1T": np.ascontiguousarray(inputs["L1_ffn_w1_w"].T),
        "w2T": np.ascontiguousarray(inputs["L1_ffn_w2_w"].T),
        "h1T": np.ascontiguousarray(inputs["L1_ffn_h1_w"].T),
        "h2T": np.ascontiguousarray(inputs["L1_ffn_h2_w"].T),
        "w3T": np.ascontiguousarray(inputs["L1_ffn_w3_w"].T),
        "s1_v": _pack128(inputs["L1_ffn_w1_s"]),
        "s2_v": _pack128(inputs["L1_ffn_w2_s"]),
        "h1s_v": _pack128(inputs["L1_ffn_h1_s"]),
        "h2s_v": _pack128(inputs["L1_ffn_h2_s"]),
        "w3s_v": _pack128(inputs["L1_ffn_w3_s"]),
    }
    per_core = []
    for c in range(NCORES):
        bidx, t0 = c // 4, 2048 * (c % 4)
        csel = np.zeros((128, 4), np.float32)
        csel[:, c % 4] = 1.0
        mask8 = np.zeros(8, np.float32)
        mask8[bidx * 4:c] = 1.0
        per_core.append({
            "x_fm": np.ascontiguousarray(x[bidx, t0:t0 + T, :].T),
            "csel_v": csel,
            "mask0": np.ascontiguousarray(
                np.broadcast_to(np.tile(mask8, 8), (128, 64))).astype(np.float32),
            "mask1": np.ascontiguousarray(
                np.broadcast_to(np.tile(mask8, 10), (128, 80))).astype(np.float32),
        })
    return shared, per_core


_A_KEYS = ["x_fm", "gwT", "a_v", "b_v", "gb_v", "csel_v", "cb_v"]
_B_KEYS = ["x_fm", "c_v", "d_v", "cb_v", "mask0", "wgT_i", "wuT_i", "wdT_i",
           "wq_r", "wk_r", "wv_r", "cmat", "smat"]
_C_KEYS = ["cb_v", "mask1", "icmat", "ismat", "w1T", "w2T", "h1T", "h2T", "w3T",
           "s1_v", "s2_v", "h1s_v", "h2s_v", "w3s_v"]


def _maps(shared, per_core, keys):
    out = []
    for c in range(NCORES):
        m = {}
        for k in keys:
            m[k] = shared[k] if k in shared else per_core[c][k]
        out.append(m)
    return out


_F_KEYS = ["x_fm", "gwT", "a_v", "b_v", "gb_v", "csel_v", "cb_v", "mask0",
           "mask1", "c_v", "d_v", "wgT_i", "wuT_i", "wdT_i", "wq_r", "wk_r",
           "wv_r", "cmat", "smat", "icmat", "ismat", "w1T", "w2T", "h1T",
           "h2T", "w3T", "s1_v", "s2_v", "h1s_v", "h2s_v", "w3s_v"]

FUSED = True


def kernel(**inputs):
    shared, per_core = _prep_inputs(inputs)
    if FUSED:
        ncF = build_fused()
        mapsF = _maps(shared, per_core, _F_KEYS)
        resF = run_bass_kernel_spmd(ncF, mapsF, core_ids=list(range(NCORES)))
        out = np.empty((2, 8192, D), np.float32)
        for c in range(NCORES):
            bidx, t0 = c // 4, 2048 * (c % 4)
            out[bidx, t0:t0 + T, :] = np.asarray(resF.results[c]["y_out"]).T
        return out
    ncA = build_launch_A()
    mapsA = _maps(shared, per_core, _A_KEYS)
    resA = run_bass_kernel_spmd(ncA, mapsA, core_ids=list(range(NCORES)))
    vta = np.stack([np.asarray(resA.results[c]["vtot_out"]).T.reshape(-1)
                    for c in range(NCORES)])
    ncB = build_launch_B()
    mapsB = _maps(shared, per_core, _B_KEYS)
    for c in range(NCORES):
        mapsB[c]["vtotall"] = vta
        mapsB[c]["v_in"] = np.asarray(resA.results[c]["v_out"])
        mapsB[c]["pows_in"] = np.asarray(resA.results[c]["pows_out"])
    resB = run_bass_kernel_spmd(ncB, mapsB, core_ids=list(range(NCORES)))
    pta = np.stack([np.asarray(resB.results[c]["ptot_out"]).T.reshape(-1)
                    for c in range(NCORES)])
    ncC = build_launch_C()
    mapsC = _maps(shared, per_core, _C_KEYS)
    for c in range(NCORES):
        mapsC[c]["ptotall"] = pta
        mapsC[c]["y1_in"] = np.asarray(resB.results[c]["y1_out"])
        for (kc, kb) in [("prre_in", "prre_out"), ("prim_in", "prim_out"),
                         ("nqre_in", "nqre_out"), ("nqim_in", "nqim_out")]:
            arr = np.asarray(resB.results[c][kb])
            if arr.dtype != ml_dtypes.bfloat16:
                arr = arr.view(ml_dtypes.bfloat16)
            mapsC[c][kc] = arr
    resC = run_bass_kernel_spmd(ncC, mapsC, core_ids=list(range(NCORES)))
    out = np.empty((2, 8192, D), np.float32)
    for c in range(NCORES):
        bidx, t0 = c // 4, 2048 * (c % 4)
        out[bidx, t0:t0 + T, :] = np.asarray(resC.results[c]["y_out"]).T
    return out


# version 21
# speedup vs baseline: 1.0755x; 1.0755x over previous
import contextlib
import numpy as np
import ml_dtypes
import concourse.bass as bass
import concourse.mybir as mybir
from concourse.bass_utils import run_bass_kernel_spmd

F32 = mybir.dt.float32
BF16 = mybir.dt.bfloat16
I32 = mybir.dt.int32
ALU = mybir.AluOpType
AF = mybir.ActivationFunctionType
AX = mybir.AxisListType

NCORES = 8
T = 2048
D = 1024
DCH = 8
NT = 4
FPAD = 640
FCH = 5
H0 = 3072
H1 = 4096

ALPHA = [4.89352455891786e-03, 6.37261928875436e-04, 1.48572235717979e-05,
         5.12229709037114e-08, -8.60467152213735e-11, 2.00018790482477e-13,
         -2.76076847742355e-16]
BETA = [4.89352518554385e-03, 2.26843463243900e-03, 1.18534705686654e-04,
        1.19825839466702e-06]
CLAMP = 7.99881172180175781
CUTOFF = 0.0004
SPLITC = 4097.0
TANH_SCRATCH = ["ah", "al", "bh", "bl", "e1", "e2", "tmp", "s",
                "xc", "x2", "x2h", "x2l", "p", "q", "y", "r1"]


def emit_tanh(v, sc, x, out, msk):
    ah, al, bh, bl, e1, e2, tmp, s_ = (sc[k] for k in
                                       ["ah", "al", "bh", "bl", "e1", "e2", "tmp", "s"])

    def split(src, hi, lo):
        v.tensor_scalar(tmp[:], src[:], SPLITC, None, ALU.mult)
        v.tensor_sub(lo[:], tmp[:], src[:])
        v.tensor_sub(hi[:], tmp[:], lo[:])
        v.tensor_sub(lo[:], src[:], hi[:])

    def fma(out_t, a, bhi, blo, bfull, c_ap=None, c_imm=None):
        split(a, ah, al)
        v.tensor_mul(s_[:], a[:], bfull[:])
        v.tensor_mul(e1[:], ah[:], bhi[:])
        v.tensor_sub(e1[:], e1[:], s_[:])
        v.tensor_mul(tmp[:], ah[:], blo[:])
        v.tensor_add(e1[:], e1[:], tmp[:])
        v.tensor_mul(tmp[:], al[:], bhi[:])
        v.tensor_add(e1[:], e1[:], tmp[:])
        v.tensor_mul(tmp[:], al[:], blo[:])
        v.tensor_add(e1[:], e1[:], tmp[:])
        if c_ap is None:
            v.tensor_scalar(e2[:], s_[:], float(c_imm), None, ALU.add)
        else:
            v.tensor_add(e2[:], s_[:], c_ap[:])
        v.tensor_sub(tmp[:], e2[:], s_[:])
        v.tensor_sub(ah[:], e2[:], tmp[:])
        v.tensor_sub(ah[:], s_[:], ah[:])
        if c_ap is None:
            v.tensor_scalar(tmp[:], tmp[:], -1.0, float(c_imm), ALU.mult, ALU.add)
        else:
            v.tensor_sub(tmp[:], c_ap[:], tmp[:])
        v.tensor_add(ah[:], ah[:], tmp[:])
        v.tensor_add(e1[:], e1[:], ah[:])
        v.tensor_add(out_t[:], e2[:], e1[:])

    xc, x2, x2h, x2l, p, q, y, r1 = (sc[k] for k in
                                     ["xc", "x2", "x2h", "x2l", "p", "q", "y", "r1"])
    v.tensor_scalar(xc[:], x[:], -CLAMP, CLAMP, ALU.max, ALU.min)
    v.tensor_mul(x2[:], xc[:], xc[:])
    split(x2, x2h, x2l)
    v.memset(p[:], float(np.float32(ALPHA[-1])))
    for coef in ALPHA[-2::-1]:
        fma(p, p, x2h, x2l, x2, c_imm=coef)
    v.tensor_mul(p[:], p[:], xc[:])
    v.memset(q[:], float(np.float32(BETA[-1])))
    for coef in BETA[-2::-1]:
        fma(q, q, x2h, x2l, x2, c_imm=coef)
    v.reciprocal(y[:], q[:])
    v.tensor_mul(r1[:], p[:], y[:])
    v.tensor_scalar(q[:], q[:], -1.0, None, ALU.mult)
    split(r1, bh, bl)
    v.tensor_copy(x2[:], r1[:])
    fma(out, q, bh, bl, x2, c_ap=p)
    split(y, bh, bl)
    v.tensor_copy(p[:], out[:])
    fma(out, p, bh, bl, y, c_ap=x2)
    v.tensor_scalar(tmp[:], x[:], -1.0, None, ALU.mult)
    v.tensor_max(tmp[:], tmp[:], x[:])
    v.tensor_single_scalar(msk[:], tmp[:], CUTOFF, ALU.is_lt)
    v.copy_predicated(out[:], msk[:], x[:])



class Sem:
    __slots__ = ("h", "n")

    def __init__(self, h):
        self.h = h
        self.n = 0

    def inc(self, amt=1):
        self.n += (1 if amt is None else amt)
        return (self.h, amt)


class Prog:
    ENGINES = ("sync", "tensor", "vector", "scalar", "gpsimd")

    def __init__(self, nc, stack):
        self.nc = nc
        self.stack = stack
        self.ops = []
        self._nsem = 0
        self._pending = {}

    def set_barrier(self, waits):
        for e in self.ENGINES:
            self._pending[e] = list(waits)

    def sem(self):
        h = self.stack.enter_context(self.nc.semaphore(f"s{self._nsem}"))
        self._nsem += 1
        return Sem(h)

    def op(self, eng, fn, waits=(), inc=None):
        w = list(waits)
        if eng in self._pending:
            w = self._pending.pop(eng) + w
        self.ops.append((eng, fn, [(s.h, val) for (s, val) in w if val > 0],
                         inc))

    def finalize(self):
        with self.nc.Block() as block:
            for engname in self.ENGINES:
                getattr(block, engname)(self._body(engname))

    def _body(self, engname):
        ops = [o for o in self.ops if o[0] == engname]

        def body(eng):
            for (_, fn, waits, inc) in ops:
                for (h, val) in waits:
                    if val > 0:
                        eng.wait_ge(h, val)
                ins = fn(eng)
                if inc is not None:
                    if ins is None:
                        ins = eng.engine_nop()
                    if inc[1] is None:
                        ins.then_inc(inc[0])
                    else:
                        ins.then_inc(*inc)
        return body


class SyncedEngine:

    def __init__(self, eng, sem):
        self._eng = eng
        self._sem = sem

    def wait_ge(self, *a):
        return self._eng.wait_ge(*a)

    def __getattr__(self, name):
        fn = getattr(self._eng, name)

        def wrapped(*a, **k):
            ins = fn(*a, **k)
            self._sem.n += 1
            ins.then_inc(self._sem.h, 1)
            self._eng.wait_ge(self._sem.h, self._sem.n)
            return ins
        return wrapped



def build_launch_A():
    nc = bass.Bass("TRN2", target_bir_lowering=False)
    x_in = nc.dram_tensor("x_fm", [D, T], F32, kind="ExternalInput").ap()
    gwT_in = nc.dram_tensor("gwT", [D, D], F32, kind="ExternalInput").ap()
    a_in = nc.dram_tensor("a_v", [128, 8], F32, kind="ExternalInput").ap()
    b_in = nc.dram_tensor("b_v", [128, 8], F32, kind="ExternalInput").ap()
    gb_in = nc.dram_tensor("gb_v", [128, 8], F32, kind="ExternalInput").ap()
    csel_in = nc.dram_tensor("csel_v", [128, 4], F32, kind="ExternalInput").ap()
    cb_in = nc.dram_tensor("cb_v", [128, 4], F32, kind="ExternalInput").ap()
    v_out = nc.dram_tensor("v_out", [D, T], F32, kind="ExternalOutput").ap()
    pows_out = nc.dram_tensor("pows_out", [D, T], F32, kind="ExternalOutput").ap()
    vtot_out = nc.dram_tensor("vtot_out", [128, 8], F32, kind="ExternalOutput").ap()
    dbg_out = nc.dram_tensor("dbg_out", [128, 24], F32, kind="ExternalOutput").ap()

    with contextlib.ExitStack() as stack:
        sb = lambda name, shape, dt=F32: stack.enter_context(
            nc.sbuf_tensor(name, shape, dt))
        ps = lambda name, shape: stack.enter_context(
            nc.psum_tensor(name, shape, F32))

        xf = sb("xf", [128, DCH * T])
        xb = sb("xb", [128, DCH * T], BF16)
        gw = sb("gw", [128, DCH * D], BF16)
        a_t = sb("a_t", [128, 8])
        b_t = sb("b_t", [128, 8])
        gb_t = sb("gb_t", [128, 8])
        csel_t = sb("csel_t", [128, 4])
        cb_t = sb("cb_t", [128, 4])
        sc = {n: sb("sc_" + n, [128, 8]) for n in TANH_SCRATCH}
        msk = sb("msk", [128, 8], I32)
        a_raw = sb("a_raw", [128, 8])
        asafe = sb("asafe", [128, 8])
        carry = sb("carry", [128, 8])
        zeros = sb("zeros", [128, T])
        powst = [sb(f"powst{i}", [128, T]) for i in range(2)]
        gstrip = [sb(f"gstrip{i}", [128, T]) for i in range(2)]
        vstrip = [sb(f"vstrip{i}", [128, T]) for i in range(2)]
        wk1 = sb("wk1", [128, T])
        wk2 = sb("wk2", [128, T])
        vtot = sb("vtot", [128, 8])
        psum = [ps(f"pA{i}", [128, 512]) for i in range(8)]

        prog = Prog(nc, stack)
        ld = prog.sem()
        vec = prog.sem()
        scl = prog.sem()
        mms = prog.sem()
        sto = prog.sem()
        nsync = prog.sem()

        for k in range(DCH):
            prog.op("sync", lambda s, k=k: s.dma_start(
                out=xf[:, k * T:(k + 1) * T], in_=x_in[k * 128:(k + 1) * 128, :]),
                inc=ld.inc(16))
            prog.op("gpsimd", lambda g, k=k: g.dma_start(
                out=xb[:, k * T:(k + 1) * T], in_=x_in[k * 128:(k + 1) * 128, :]),
                inc=ld.inc(16))
            prog.op("gpsimd", lambda g, k=k: g.dma_start(
                out=gw[:, k * D:(k + 1) * D], in_=gwT_in[k * 128:(k + 1) * 128, :]),
                inc=ld.inc(16))
        for (dst, src) in [(a_t, a_in), (b_t, b_in), (gb_t, gb_in),
                           (csel_t, csel_in), (cb_t, cb_in)]:
            prog.op("sync", lambda s, dst=dst, src=src: s.dma_start(
                out=dst[:], in_=src[:]), inc=ld.inc(16))
        lded = ld.n

        prog.op("vector", lambda v: v.memset(zeros[:], 0.0))
        def _tanh(v):
            emit_tanh(SyncedEngine(v, nsync), sc, a_t, a_raw, msk)
            return v.engine_nop()
        prog.op("vector", _tanh, waits=[(ld, lded)])

        def _asafe(v0):
            v = SyncedEngine(v0, nsync)
            t1, t2 = sc["tmp"], sc["s"]
            v.tensor_scalar(t1[:], a_raw[:], -1.0, None, ALU.mult)
            v.tensor_max(t1[:], t1[:], a_raw[:])
            v.tensor_scalar(t1[:], t1[:], 1e-4, None, ALU.max)
            v.tensor_single_scalar(t2[:], a_raw[:], 0.0, ALU.is_lt)
            v.tensor_scalar(t2[:], t2[:], -2.0, 1.0, ALU.mult, ALU.add)
            v.tensor_mul(asafe[:], t1[:], t2[:])
            return v0.engine_nop()
        prog.op("vector", _asafe)

        def _carries(v0):
            v = SyncedEngine(v0, nsync)
            p2, p4, p6, t1 = sc["p"], sc["q"], sc["y"], sc["tmp"]
            v.tensor_copy(p2[:], asafe[:])
            for _ in range(11):
                v.tensor_mul(p2[:], p2[:], p2[:])
            v.tensor_mul(p4[:], p2[:], p2[:])
            v.tensor_mul(p6[:], p4[:], p2[:])
            v.tensor_scalar(carry[:], zeros[:, 0:8], csel_t[:, 0:1], None, ALU.add)
            v.scalar_tensor_tensor(t1[:], p2[:], csel_t[:, 1:2], carry[:],
                                   ALU.mult, ALU.add)
            v.scalar_tensor_tensor(carry[:], p4[:], csel_t[:, 2:3], t1[:],
                                   ALU.mult, ALU.add)
            v.scalar_tensor_tensor(t1[:], p6[:], csel_t[:, 3:4], carry[:],
                                   ALU.mult, ALU.add)
            v.tensor_copy(carry[:], t1[:])
            return v0.engine_nop()
        prog.op("vector", _carries, inc=vec.inc())
        vec_init = vec.n

        vec_pow = [0] * DCH; vec_v = [0] * DCH
        scl_evac = [0] * DCH; scl_rch = [0] * DCH
        mm_done = [0] * DCH
        sto_pow = [0] * DCH; sto_v = [0] * DCH

        for m in range(DCH):
            par = m % 2
            pw, gs, vs = powst[par], gstrip[par], vstrip[par]
            xf_m = xf[:, m * T:(m + 1) * T]

            w = [(vec, vec_init)]
            if m >= 2:
                w += [(scl, scl_rch[m - 2]), (sto, sto_pow[m - 2])]
            def _powscan(v, pw=pw, m=m):
                v.tensor_scalar(wk1[:], zeros[:], asafe[:, m:m + 1], None, ALU.add)
                v.tensor_copy(pw[:, 0:1], carry[:, m:m + 1])
                return v.tensor_tensor_scan(
                    pw[:, 1:T], wk1[:, 1:T], zeros[:, 1:T],
                    carry[:, m:m + 1], ALU.mult, ALU.add)
            prog.op("vector", _powscan, waits=w, inc=vec.inc())
            vec_pow[m] = vec.n

            w = [(ld, lded)]
            if m >= 2:
                w.append((scl, scl_evac[m - 2]))
            def _gatemm(t, m=m, par=par):
                last = None
                for n in range(NT):
                    bank = psum[4 * par + n]
                    for k in range(DCH):
                        last = t.matmul(
                            bank[:],
                            gw[:, k * D + m * 128: k * D + (m + 1) * 128],
                            xb[:, k * T + n * 512: k * T + (n + 1) * 512],
                            start=(k == 0), stop=(k == DCH - 1))
                return last
            prog.op("tensor", _gatemm, waits=w, inc=mms.inc())
            mm_done[m] = mms.n

            for n in range(NT):
                w = []
                if n == 0:
                    w.append((mms, mm_done[m]))
                    if m >= 2:
                        w.append((vec, vec_v[m - 2]))
                prog.op("scalar", lambda s, n=n, par=par, gs=gs, m=m: s.activation(
                    gs[:, n * 512:(n + 1) * 512], psum[4 * par + n][:],
                    AF.Sigmoid, bias=gb_t[:, m:m + 1]),
                    waits=w, inc=scl.inc() if n == NT - 1 else None)
            scl_evac[m] = scl.n

            def _rchain(s, pw=pw):
                s.activation(wk2[:], pw[:], AF.Abs, bias=cb_t[:, 0:1])
                s.activation(wk2[:], wk2[:], AF.Ln)
                s.activation(wk2[:], wk2[:], AF.Exp, scale=-1.0)
                return s.sign(wk1[:], pw[:], bias=cb_t[:, 0:1])
            prog.op("scalar", _rchain, waits=[(vec, vec_pow[m])], inc=scl.inc())
            scl_rch[m] = scl.n

            w = [(scl, scl_rch[m])]
            if m >= 2:
                w.append((sto, sto_v[m - 2]))
            def _vchain(v, gs=gs, vs=vs, m=m, xf_m=xf_m):
                v.tensor_mul(wk2[:], wk2[:], wk1[:])
                v.scalar_tensor_tensor(wk1[:], gs[:], b_t[:, m:m + 1], xf_m,
                                       ALU.mult, ALU.mult)
                return v.scalar_tensor_tensor(
                    vs[:], wk1[:], 1.0, wk2[:], ALU.mult, ALU.mult,
                    accum_out=vtot[:, m:m + 1])
            prog.op("vector", _vchain, waits=w, inc=vec.inc())
            vec_v[m] = vec.n

            prog.op("sync", lambda s, pw=pw, m=m: s.dma_start(
                out=pows_out[m * 128:(m + 1) * 128, :], in_=pw[:]),
                waits=[(vec, vec_pow[m])], inc=sto.inc(16))
            sto_pow[m] = sto.n
            prog.op("sync", lambda s, vs=vs, m=m: s.dma_start(
                out=v_out[m * 128:(m + 1) * 128, :], in_=vs[:]),
                waits=[(vec, vec_v[m])], inc=sto.inc(16))
            sto_v[m] = sto.n

        prog.op("sync", lambda s: s.dma_start(out=vtot_out[:], in_=vtot[:]),
                waits=[(vec, vec.n)], inc=sto.inc(16))
        prog.op("sync", lambda s: s.dma_start(out=dbg_out[:, 0:8], in_=a_raw[:]),
                waits=[(vec, vec.n)], inc=sto.inc(16))
        prog.op("sync", lambda s: s.dma_start(out=dbg_out[:, 8:16], in_=asafe[:]),
                waits=[(vec, vec.n)], inc=sto.inc(16))
        prog.op("sync", lambda s: s.dma_start(out=dbg_out[:, 16:24], in_=carry[:]),
                waits=[(vec, vec.n)], inc=sto.inc(16))
        prog.op("sync", lambda s: None, waits=[(sto, sto.n)])

        prog.finalize()
    return nc



def emit_carry_combine(prog, eng_dma, agt, maskt, carryt, nsync, src_dram, nrows,
                       ld, vec, waits=()):
    n8 = nrows * 8
    for r in range(8):
        prog.op(eng_dma, lambda e, r=r: e.dma_start(
            out=agt[:, r:n8:8],
            in_=src_dram[r:r + 1, :].rearrange("one (k p) -> p (one k)", p=128),
            allow_slow_non_contiguous=True), waits=waits, inc=ld.inc(16))
    lded = ld.n

    def _comb(v0):
        v = SyncedEngine(v0, nsync)
        v.tensor_mul(agt[:], agt[:], maskt[:])
        v.tensor_reduce(carryt[:, 0:nrows],
                        agt.rearrange("p (k r) -> p k r", r=8),
                        AX.X, ALU.add)
        return v0.engine_nop()
    prog.op("vector", _comb, waits=[(ld, lded)], inc=vec.inc())
    return vec.n


def emit_layernorm(prog, st, eps_col, milestones):
    xf, xb, wkS, psum = st["xf"], st["xb"], st["wkS"], st["psum"]
    su, sq = st["rowA"], st["rowB"]
    rstd, mr = sq, su
    ones_col, ones_row, cb_t = st["ones_col"], st["ones_row"], st["cb_t"]
    vec, scl, mms, nsync = st["vec"], st["scl"], st["mms"], st["nsync"]
    zv = milestones["z_ready_vec"]
    zs = milestones.get("z_ready_scl", 0)
    banks_free_scl = milestones.get("banks_free_scl", 0)
    banks_free_vec = milestones.get("banks_free_vec", 0)

    scl_sq = [0] * DCH
    mms_stat = [0] * DCH
    for m in range(DCH):
        par = m % 2
        w = [(vec, zv)]
        if zs:
            w.append((scl, zs))
        if m >= 2:
            w.append((mms, mms_stat[m - 2]))
        prog.op("scalar", lambda s, m=m, par=par: s.activation(
            wkS[par][:], xf[:, m * T:(m + 1) * T], AF.Square),
            waits=w, inc=scl.inc())
        scl_sq[m] = scl.n
        w = [(scl, scl_sq[m])]
        if m == 0 and banks_free_scl:
            w.append((scl, banks_free_scl))
        if m == 0 and banks_free_vec:
            w.append((vec, banks_free_vec))
        def _stat(t, m=m, par=par):
            last = None
            for n in range(NT):
                last = t.matmul(psum[n][0:1, :], ones_col[:],
                                xf[:, m * T + n * 512: m * T + (n + 1) * 512],
                                start=(m == 0), stop=(m == DCH - 1))
                last = t.matmul(psum[4 + n][0:1, :], ones_col[:],
                                wkS[par][:, n * 512:(n + 1) * 512],
                                start=(m == 0), stop=(m == DCH - 1))
            return last
        prog.op("tensor", _stat, waits=w, inc=mms.inc())
        mms_stat[m] = mms.n

    def _rows1(v0):
        v = SyncedEngine(v0, nsync)
        for n in range(NT):
            v.tensor_copy(su[:, n * 512:(n + 1) * 512], psum[n][0:1, :])
            v.tensor_copy(sq[:, n * 512:(n + 1) * 512], psum[4 + n][0:1, :])
        v.tensor_scalar(su[:, :], su[:, :], 1.0 / D, None, ALU.mult)
        v.tensor_scalar(sq[:, :], sq[:, :], 1.0 / D, None, ALU.mult)
        tmp_row = wkS[0][0:1, 0:T]
        v.tensor_mul(tmp_row, su[:, :], su[:, :])
        v.tensor_sub(sq[:, :], sq[:, :], tmp_row)
        return v0.engine_nop()
    prog.op("vector", _rows1, waits=[(mms, mms_stat[DCH - 1])], inc=vec.inc())
    v_rows1 = vec.n
    prog.op("scalar", lambda s: s.activation(
        sq[:, :], sq[:, :], AF.Sqrt, bias=cb_t[0:1, eps_col:eps_col + 1]),
        waits=[(vec, v_rows1)], inc=scl.inc())
    s_sqrt = scl.n

    def _rows2(v0):
        v = SyncedEngine(v0, nsync)
        v.reciprocal(sq[:, :], sq[:, :])
        v.tensor_mul(su[:, :], su[:, :], sq[:, :])
        return v0.engine_nop()
    prog.op("vector", _rows2, waits=[(scl, s_sqrt)], inc=vec.inc())
    v_rows2 = vec.n

    def _bcast(t):
        last = None
        for n in range(NT):
            last = t.matmul(psum[n][:], ones_row[:],
                            rstd[:, n * 512:(n + 1) * 512], start=True, stop=True)
            last = t.matmul(psum[4 + n][:], ones_row[:],
                            mr[:, n * 512:(n + 1) * 512], start=True, stop=True)
        return last
    prog.op("tensor", _bcast, waits=[(vec, v_rows2)], inc=mms.inc())
    m_bcast = mms.n

    entry_sto = milestones.get("entry_sto", 0)
    vec_apply = [0] * DCH
    for m in range(DCH):
        def _apply(v, m=m):
            last = None
            for n in range(NT):
                sl = slice(m * T + n * 512, m * T + (n + 1) * 512)
                v.tensor_tensor(wkS[0][:, n * 512:(n + 1) * 512],
                                xf[:, sl], psum[n][:], ALU.mult)
                last = v.tensor_tensor(xf[:, sl],
                                       wkS[0][:, n * 512:(n + 1) * 512],
                                       psum[4 + n][:], ALU.subtract)
            return last
        w_ap = [(mms, m_bcast)]
        if m == 0 and entry_sto:
            w_ap.append((st.get("sto"), entry_sto))
        prog.op("vector", _apply, waits=w_ap, inc=vec.inc())
        vec_apply[m] = vec.n
        prog.op("scalar", lambda s, m=m: s.activation(
            xb[:, m * T:(m + 1) * T], xf[:, m * T:(m + 1) * T], AF.Copy),
            waits=[(vec, vec_apply[m])], inc=scl.inc())
    return {"vec_apply": vec.n, "scl_shadow": scl.n, "mms_bcast": m_bcast,
            "banks_busy_until_vec": vec.n}



def build_launch_B():
    nc = bass.Bass("TRN2", target_bir_lowering=False)
    dram = lambda n, sh, dt=F32, kind="ExternalInput": nc.dram_tensor(
        n, sh, dt, kind=kind).ap()
    x_in = dram("x_fm", [D, T])
    v_in = dram("v_in", [D, T])
    pows_in = dram("pows_in", [D, T])
    vta_in = dram("vtotall", [8, D])
    mask0_in = dram("mask0", [128, 64])
    c_in = dram("c_v", [128, 8])
    d_in = dram("d_v", [128, 8])
    cb_in = dram("cb_v", [128, 4])
    wg_in = dram("wgT_i", [D, H0], I32)
    wu_in = dram("wuT_i", [D, H0], I32)
    wd_in = dram("wdT_i", [H0, D], I32)
    w_qkv_in = {"q": dram("wq_r", [D, D]), "k": dram("wk_r", [D, D]),
                "v": dram("wv_r", [D, D])}
    cmat_in = dram("cmat", [D, FPAD], BF16)
    smat_in = dram("smat", [D, FPAD], BF16)
    y1_out = dram("y1_out", [D, T], F32, "ExternalOutput")

    pr_out = {"re": dram("prre_out", [FPAD, T], BF16, "ExternalOutput"),
              "im": dram("prim_out", [FPAD, T], BF16, "ExternalOutput")}
    nq_out = {"re": dram("nqre_out", [FPAD, T], BF16, "ExternalOutput"),
              "im": dram("nqim_out", [FPAD, T], BF16, "ExternalOutput")}
    ptot_out = dram("ptot_out", [128, 10], F32, "ExternalOutput")
    mscr = {}
    for nm in ["qre", "qim", "kre", "kim", "vre", "vim"]:
        mscr[nm] = nc.dram_tensor("mscr_" + nm, [D, FPAD], BF16,
                                  kind="Internal").ap()

    with contextlib.ExitStack() as stack:
        sb = lambda name, shape, dt=F32: stack.enter_context(
            nc.sbuf_tensor(name, shape, dt))
        ps = lambda name, shape: stack.enter_context(
            nc.psum_tensor(name, shape, F32))

        xb = sb("xb", [128, DCH * T], BF16)
        cb_t = sb("cb_t", [128, 4])
        c_t = sb("c_t", [128, 8])
        d1_t = sb("d1_t", [128, 8])
        ones_col = sb("ones_col", [128, 1])
        ones_row = sb("ones_row", [1, 128])
        rowA = sb("rowA", [1, T])
        rowB = sb("rowB", [1, T])
        wkS = [sb(f"wkS{i}", [128, T]) for i in range(2)]
        psum = [ps(f"pB{i}", [128, 512]) for i in range(8)]

        prog = Prog(nc, stack)
        ld = prog.sem(); vec = prog.sem(); scl = prog.sem()
        mms = prog.sem(); sto = prog.sem(); nsync = prog.sem()

        st = {"xb": xb, "wkS": wkS, "psum": psum, "rowA": rowA, "rowB": rowB,
              "ones_col": ones_col, "ones_row": ones_row,
              "cb_t": cb_t, "vec": vec, "scl": scl, "mms": mms, "nsync": nsync,
              "sto": sto}

        p_xf = contextlib.ExitStack()
        xf = p_xf.enter_context(nc.sbuf_tensor("xf", [128, DCH * T], F32))
        st["xf"] = xf

        p1 = contextlib.ExitStack()
        sb1 = lambda name, shape, dt=F32: p1.enter_context(
            nc.sbuf_tensor(name, shape, dt))
        vst = [sb1(f"vst{i}", [128, T]) for i in range(2)]
        pwst = [sb1(f"pwst{i}", [128, T]) for i in range(2)]
        agt = sb1("agt", [128, 64])
        maskt = sb1("maskt", [128, 64])
        carry0 = sb1("carry0", [128, 8])

        for k in range(DCH):
            prog.op("sync", lambda s, k=k: s.dma_start(
                out=xf[:, k * T:(k + 1) * T], in_=x_in[k * 128:(k + 1) * 128, :]),
                inc=ld.inc(16))
        for (dst, srcd) in [(cb_t, cb_in), (c_t, c_in), (d1_t, d_in),
                            (maskt, mask0_in)]:
            prog.op("sync", lambda s, dst=dst, srcd=srcd: s.dma_start(
                out=dst[:], in_=srcd[:]), inc=ld.inc(16))
        lded0 = ld.n

        prog.op("vector", lambda v: v.memset(ones_col[:], 1.0))
        prog.op("vector", lambda v: v.memset(ones_row[:], 1.0))
        prog.op("vector", lambda v: v.memset(wkS[1][:], 0.0))

        def _d1(v0):
            v = SyncedEngine(v0, nsync)
            v.tensor_scalar(d1_t[:], d1_t[:], 1.0, None, ALU.add)
            return v0.engine_nop()
        prog.op("vector", _d1, waits=[(ld, lded0)], inc=vec.inc())

        cb_vec = emit_carry_combine(prog, "gpsimd", agt, maskt, carry0, nsync,
                                    vta_in, 8, ld, vec, waits=[(ld, lded0)])

        vec_z0 = [0] * DCH
        for m in range(DCH):
            par = m % 2
            vs, pw = vst[par], pwst[par]
            w = []
            if m >= 2:
                w.append((vec, vec_z0[m - 2]))
            prog.op("gpsimd", lambda g, m=m, vs=vs: g.dma_start(
                out=vs[:], in_=v_in[m * 128:(m + 1) * 128, :]),
                waits=w, inc=ld.inc(16))
            prog.op("gpsimd", lambda g, m=m, pw=pw: g.dma_start(
                out=pw[:], in_=pows_in[m * 128:(m + 1) * 128, :]),
                inc=ld.inc(16))
            ld_m = ld.n

            def _scan_z0(v, m=m, vs=vs, pw=pw):
                v.tensor_tensor_scan(vs[:], vs[:], wkS[1][:, 0:T],
                                     carry0[:, m:m + 1], ALU.add, ALU.add)
                v.scalar_tensor_tensor(wkS[0][:], vs[:], c_t[:, m:m + 1], pw[:],
                                       ALU.mult, ALU.mult)
                return v.scalar_tensor_tensor(
                    xf[:, m * T:(m + 1) * T], xf[:, m * T:(m + 1) * T],
                    d1_t[:, m:m + 1], wkS[0][:], ALU.mult, ALU.add)
            prog.op("vector", _scan_z0, waits=[(ld, ld_m), (vec, cb_vec)],
                    inc=vec.inc())
            vec_z0[m] = vec.n

        ms0 = emit_layernorm(prog, st, 2, {"z_ready_vec": vec_z0[DCH - 1]})
        p1.close()

        p3 = contextlib.ExitStack()
        sb3 = lambda name, shape, dt=F32: p3.enter_context(
            nc.sbuf_tensor(name, shape, dt))
        wblk = sb3("wblk", [128, DCH * 1024], BF16)
        gsil = sb3("gsil", [128, 8 * 512], BF16)
        hh = sb3("hh", [128, 24 * 512], BF16)
        wdb = [sb3(f"wdb{i}", [128, 1024], BF16) for i in range(2)]

        vec_banks_free = ms0["banks_busy_until_vec"]
        scl_entry = ms0["scl_shadow"]
        mm_wblk_free = 0
        vec_z1_all = 0
        for qt in range(4):
            tk = qt * 512
            vec_hh = 0
            for blk in range(3):
                for k in range(DCH):
                    w = [(scl, scl_entry), (mms, mm_wblk_free)] if k == 0 else []
                    prog.op("gpsimd", lambda g, k=k, blk=blk: g.dma_start(
                        out=wblk[:, k * 1024:(k + 1) * 1024],
                        in_=wg_in[k * 128:(k + 1) * 128,
                                  blk * 1024:(blk + 1) * 1024]),
                        waits=[x for x in w if x[1] > 0], inc=ld.inc(16))
                ld_g = ld.n

                def _gmm(t, tk=tk):
                    last = None
                    for k in range(DCH):
                        for mh in range(8):
                            last = t.matmul(
                                psum[mh][:],
                                wblk[:, k * 1024 + mh * 128:
                                     k * 1024 + (mh + 1) * 128],
                                xb[:, k * T + tk:k * T + tk + 512],
                                start=(k == 0), stop=(k == DCH - 1))
                    return last
                prog.op("tensor", _gmm,
                        waits=[(ld, ld_g), (vec, vec_banks_free)],
                        inc=mms.inc())
                mm_g = mms.n
                for mh in range(8):
                    prog.op("scalar", lambda s, mh=mh: s.activation(
                        gsil[:, mh * 512:(mh + 1) * 512],
                        psum[mh][:], AF.Silu, scale=1.0 / 127.0),
                        waits=[(mms, mm_g)] if mh == 0 else (),
                        inc=scl.inc() if mh == 7 else None)
                scl_gate = scl.n
                for k in range(DCH):
                    w = [(mms, mm_g)] if k == 0 else []
                    prog.op("gpsimd", lambda g, k=k, blk=blk: g.dma_start(
                        out=wblk[:, k * 1024:(k + 1) * 1024],
                        in_=wu_in[k * 128:(k + 1) * 128,
                                  blk * 1024:(blk + 1) * 1024]),
                        waits=w, inc=ld.inc(16))
                ld_u = ld.n

                def _umm(t, tk=tk):
                    last = None
                    for k in range(DCH):
                        for mh in range(8):
                            last = t.matmul(
                                psum[mh][:],
                                wblk[:, k * 1024 + mh * 128:
                                     k * 1024 + (mh + 1) * 128],
                                xb[:, k * T + tk:k * T + tk + 512],
                                start=(k == 0), stop=(k == DCH - 1))
                    return last
                prog.op("tensor", _umm, waits=[(ld, ld_u), (scl, scl_gate)],
                        inc=mms.inc())
                mm_u = mms.n
                mm_wblk_free = mms.n

                def _hh(v, blk=blk):
                    last = None
                    for mh in range(8):
                        last = v.scalar_tensor_tensor(
                            hh[:, (blk * 8 + mh) * 512:(blk * 8 + mh + 1) * 512],
                            psum[mh][:], 1.0 / 127.0,
                            gsil[:, mh * 512:(mh + 1) * 512],
                            ALU.mult, ALU.mult)
                    return last
                prog.op("vector", _hh, waits=[(mms, mm_u)], inc=vec.inc())
                vec_hh = vec.n
                vec_banks_free = vec.n

            mm_wd = [0] * 24
            for kh in range(24):
                wdt = wdb[kh % 2]
                if kh >= 2:
                    w = [(mms, mm_wd[kh - 2])]
                else:
                    w = [(mms, mm_wblk_free)] if qt > 0 else []
                prog.op("gpsimd", lambda g, kh=kh, wdt=wdt: g.dma_start(
                    out=wdt[:], in_=wd_in[kh * 128:(kh + 1) * 128, :]),
                    waits=[x for x in w if x[1] > 0], inc=ld.inc(16))
                ld_kh = ld.n
                w = [(ld, ld_kh)]
                if kh == 0:
                    w.append((vec, vec_hh))

                def _dk(t, kh=kh, wdt=wdt):
                    last = None
                    for m in range(DCH):
                        last = t.matmul(
                            psum[m][:],
                            wdt[:, m * 128:(m + 1) * 128],
                            hh[:, kh * 512:(kh + 1) * 512],
                            start=(kh == 0), stop=(kh == 23))
                    return last
                prog.op("tensor", _dk, waits=w, inc=mms.inc())
                mm_wd[kh] = mms.n
            mm_d = mms.n

            def _z1(v, tk=tk):
                last = None
                for m in range(DCH):
                    sl = slice(m * T + tk, m * T + tk + 512)
                    last = v.scalar_tensor_tensor(
                        xf[:, sl], psum[m][:], 1.0 / 127.0, xf[:, sl],
                        ALU.mult, ALU.add)
                return last
            prog.op("vector", _z1, waits=[(mms, mm_d)], inc=vec.inc())
            vec_banks_free = vec.n
        vec_z1_all = vec.n
        ms1 = emit_layernorm(prog, st, 2, {"z_ready_vec": vec_z1_all,
                                           "banks_free_vec": vec_z1_all})
        sto_y1 = 0
        for m in range(DCH):
            prog.op("sync", lambda s, m=m: s.dma_start(
                out=y1_out[m * 128:(m + 1) * 128, :],
                in_=xf[:, m * T:(m + 1) * T]),
                waits=[(vec, ms1["vec_apply"])] if m == 0 else (),
                inc=sto.inc(16))
        sto_y1 = sto.n
        p3.close()
        p_xf.close()

        p4 = contextlib.ExitStack()
        sb4 = lambda name, shape, dt=F32: p4.enter_context(
            nc.sbuf_tensor(name, shape, dt))
        wfull = sb4("wfull", [128, DCH * 1024], BF16)
        cfull = sb4("cfull", [128, DCH * FPAD], BF16)
        sfull = sb4("sfull", [128, DCH * FPAD], BF16)
        mstag = sb4("mstag", [128, DCH * FPAD], BF16)

        for e in range(DCH):
            prog.op("sync", lambda s, e=e: s.dma_start(
                out=cfull[:, e * FPAD:(e + 1) * FPAD],
                in_=cmat_in[e * 128:(e + 1) * 128, :]),
                waits=[(sto, sto_y1)] if e == 0 else (), inc=ld.inc(16))
            prog.op("sync", lambda s, e=e: s.dma_start(
                out=sfull[:, e * FPAD:(e + 1) * FPAD],
                in_=smat_in[e * 128:(e + 1) * 128, :]), inc=ld.inc(16))
        ld_cs = ld.n

        evac_scl_prev = ms1["scl_shadow"]
        bank_free_vec = ms1["banks_busy_until_vec"]
        sto_mscr = 0
        for mat in ("k", "v", "q"):
            for e in range(DCH):
                prog.op("gpsimd", lambda g, e=e, mat=mat: g.dma_start(
                    out=wfull[:, e * 1024:(e + 1) * 1024],
                    in_=w_qkv_in[mat][e * 128:(e + 1) * 128, :]),
                    waits=[(mms, mms.n), (sto, sto_y1)] if e == 0 else (),
                    inc=ld.inc(16))
            ld_w = ld.n
            for (part, csrc) in (("re", cfull), ("im", sfull)):
                scr = mscr[mat + part]
                for (ftw, ft0) in ((512, 0), (128, 512)):
                    w = [(ld, ld_w), (ld, ld_cs), (scl, evac_scl_prev),
                         (vec, bank_free_vec)]
                    def _mmm(t, csrc=csrc, ftw=ftw, ft0=ft0):
                        last = None
                        for e in range(DCH):
                            for m in range(DCH):
                                last = t.matmul(
                                    psum[m][0:128, 0:ftw],
                                    wfull[:, e * 1024 + m * 128:
                                          e * 1024 + (m + 1) * 128],
                                    csrc[:, e * FPAD + ft0:e * FPAD + ft0 + ftw],
                                    start=(e == 0), stop=(e == DCH - 1))
                        return last
                    prog.op("tensor", _mmm, waits=w, inc=mms.inc())
                    mm_m = mms.n
                    for m in range(DCH):
                        w = []
                        if m == 0:
                            w = [(mms, mm_m)]
                            if sto_mscr:
                                w.append((sto, sto_mscr))
                        prog.op("scalar", lambda s, m=m, ft0=ft0, ftw=ftw:
                                s.activation(
                                    mstag[:, m * FPAD + ft0:m * FPAD + ft0 + ftw],
                                    psum[m][0:128, 0:ftw], AF.Copy),
                                waits=w,
                                inc=scl.inc() if m == DCH - 1 else None)
                    evac_scl_prev = scl.n
                    bank_free_vec = 0
                for m in range(DCH):
                    prog.op("sync", lambda s, m=m, scr=scr: s.dma_start(
                        out=scr[m * 128:(m + 1) * 128, :],
                        in_=mstag[:, m * FPAD:(m + 1) * FPAD]),
                        waits=[(scl, evac_scl_prev)] if m == 0 else (),
                        inc=sto.inc(16))
                sto_mscr = sto.n
        p4.close()

        p5 = contextlib.ExitStack()
        sb5 = lambda name, shape, dt=F32: p5.enter_context(
            nc.sbuf_tensor(name, shape, dt))
        At = sb5("At", [128, T]); Bt = sb5("Bt", [128, T])
        Ct = sb5("Ct", [128, T]); Dt = sb5("Dt", [128, T])
        wf1 = sb5("wf1", [128, T]); wf2 = sb5("wf2", [128, T])
        scr1 = sb5("scr1", [128, T]); scr2 = sb5("scr2", [128, T])
        pre = [sb5(f"pre{i}", [128, T], BF16) for i in range(2)]
        pim = [sb5(f"pim{i}", [128, T], BF16) for i in range(2)]
        mk1 = sb5("mk1", [128, DCH * 128], BF16)
        mk2 = sb5("mk2", [128, DCH * 128], BF16)
        ptot = sb5("ptot", [128, 10])

        def proj_mms(tag, c, blk_re, blk_im, bank0_wait_scl, extra_waits=()):
            w = list(extra_waits) + [(scl, bank0_wait_scl)]
            def _p(t, c=c, blk_re=blk_re, blk_im=blk_im):
                last = None
                for n in range(NT):
                    for k in range(DCH):
                        last = t.matmul(
                            psum[n][:],
                            blk_re[:, k * 128:(k + 1) * 128],
                            xb[:, k * T + n * 512:k * T + (n + 1) * 512],
                            start=(k == 0), stop=(k == DCH - 1))
                for n in range(NT):
                    for k in range(DCH):
                        last = t.matmul(
                            psum[4 + n][:],
                            blk_im[:, k * 128:(k + 1) * 128],
                            xb[:, k * T + n * 512:k * T + (n + 1) * 512],
                            start=(k == 0), stop=(k == DCH - 1))
                return last
            prog.op("tensor", _p, waits=[x for x in w if x[1] > 0], inc=mms.inc())
            return mms.n

        def evac8(dst_re, dst_im, mm_wait, extra=()):
            for n in range(NT):
                w = ([(mms, mm_wait)] + [x for x in extra if x[1] > 0]) \
                    if n == 0 else ()
                prog.op("scalar", lambda s, n=n, dst_re=dst_re: s.activation(
                    dst_re[:, n * 512:(n + 1) * 512], psum[n][:], AF.Copy),
                    waits=w)
            for n in range(NT):
                prog.op("scalar", lambda s, n=n, dst_im=dst_im: s.activation(
                    dst_im[:, n * 512:(n + 1) * 512], psum[4 + n][:], AF.Copy),
                    inc=scl.inc() if n == NT - 1 else None)
            return scl.n

        def load_mblks(names, c, dst1, dst2, waits0):
            for (nm, dstt) in ((names[0], dst1), (names[1], dst2)):
                for k in range(DCH):
                    w = [x for x in waits0 if x[1] > 0] \
                        if (k == 0 and nm == names[0]) else ()
                    prog.op("gpsimd", lambda g, nm=nm, k=k, c=c, dstt=dstt:
                            g.dma_start(
                                out=dstt[:, k * 128:(k + 1) * 128],
                                in_=mscr[nm][k * 128:(k + 1) * 128,
                                             c * 128:(c + 1) * 128]),
                            waits=w, inc=ld.inc(16))
            return ld.n

        def normalize(mm_evac_scl, vec_guard):
            w = [(scl, mm_evac_scl)]
            if vec_guard:
                w.append((vec, vec_guard))
            prog.op("scalar", lambda s: s.activation(wf1[:], At[:], AF.Square),
                    waits=w)
            prog.op("scalar", lambda s: s.activation(wf2[:], Bt[:], AF.Square),
                    inc=scl.inc())
            s_sq = scl.n
            prog.op("vector", lambda v: v.tensor_add(wf1[:], wf1[:], wf2[:]),
                    waits=[(scl, s_sq)], inc=vec.inc())
            v_add = vec.n
            prog.op("scalar", lambda s: s.activation(wf2[:], wf1[:], AF.Sqrt),
                    waits=[(vec, v_add)])
            prog.op("scalar", lambda s: s.activation(
                wf2[:], wf2[:], AF.Ln, bias=cb_t[:, 1:2]))
            prog.op("scalar", lambda s: s.activation(
                wf2[:], wf2[:], AF.Exp, scale=-1.0), inc=scl.inc())
            s_rn = scl.n
            def _nk(v):
                v.tensor_mul(At[:], At[:], wf2[:])
                return v.tensor_mul(Bt[:], Bt[:], wf2[:])
            prog.op("vector", _nk, waits=[(scl, s_rn)], inc=vec.inc())
            return vec.n

        vec_work_last = 0
        scl_evac_prev = evac_scl_prev
        sto_pr = [0, 0]
        mm_blk_last = 0
        for c in range(FCH):
            par = c % 2
            ld_k = load_mblks(("kre", "kim"), c, mk1, mk2,
                              [(sto, sto_mscr), (mms, mm_blk_last)])
            mm_k = proj_mms("k", c, mk1, mk2, scl_evac_prev,
                            extra_waits=[(ld, ld_k)])
            mm_blk_last = mm_k
            scl_evac_prev = evac8(At, Bt, mm_k,
                                  extra=[(vec, vec_work_last)])
            ld_v = load_mblks(("vre", "vim"), c, mk1, mk2,
                              [(mms, mm_k)])
            mm_v = proj_mms("v", c, mk1, mk2, scl_evac_prev,
                            extra_waits=[(ld, ld_v)])
            mm_blk_last = mm_v
            scl_evac_prev = evac8(Ct, Dt, mm_v)
            v_nk = normalize(scl_evac_prev, vec_work_last)
            def _prods(v, c=c, par=par):
                v.tensor_mul(scr1[:], At[:], Ct[:])
                v.tensor_mul(scr2[:], At[:], Dt[:])
                v.tensor_mul(Ct[:], Bt[:], Ct[:])
                v.tensor_mul(Dt[:], Bt[:], Dt[:])
                v.scalar_tensor_tensor(pre[par][:], scr1[:], 1.0, Dt[:],
                                       ALU.mult, ALU.subtract,
                                       accum_out=ptot[:, 2 * c:2 * c + 1])
                return v.scalar_tensor_tensor(
                    pim[par][:], scr2[:], 1.0, Ct[:], ALU.mult, ALU.add,
                    accum_out=ptot[:, 2 * c + 1:2 * c + 2])
            w = [(vec, v_nk)]
            if sto_pr[par]:
                w.append((sto, sto_pr[par]))
            prog.op("vector", _prods, waits=w, inc=vec.inc())
            vec_work_last = vec.n
            prog.op("sync", lambda s, c=c, par=par: s.dma_start(
                out=pr_out["re"][c * 128:(c + 1) * 128, :], in_=pre[par][:]),
                waits=[(vec, vec_work_last)], inc=sto.inc(16))
            prog.op("sync", lambda s, c=c, par=par: s.dma_start(
                out=pr_out["im"][c * 128:(c + 1) * 128, :], in_=pim[par][:]),
                inc=sto.inc(16))
            sto_pr[par] = sto.n

        for c in range(FCH):
            par = c % 2
            ld_q = load_mblks(("qre", "qim"), c, mk1, mk2,
                              [(mms, mm_blk_last)])
            mm_q = proj_mms("q", c, mk1, mk2, scl_evac_prev,
                            extra_waits=[(ld, ld_q)])
            mm_blk_last = mm_q
            scl_evac_prev = evac8(At, Bt, mm_q,
                                  extra=[(vec, vec_work_last)])
            v_nq = normalize(scl_evac_prev, vec_work_last)
            def _cast(v, par=par):
                v.tensor_copy(pre[par][:], At[:])
                return v.tensor_copy(pim[par][:], Bt[:])
            w = [(vec, v_nq)]
            if sto_pr[par]:
                w.append((sto, sto_pr[par]))
            prog.op("vector", _cast, waits=w, inc=vec.inc())
            vec_work_last = vec.n
            prog.op("sync", lambda s, c=c, par=par: s.dma_start(
                out=nq_out["re"][c * 128:(c + 1) * 128, :], in_=pre[par][:]),
                waits=[(vec, vec_work_last)], inc=sto.inc(16))
            prog.op("sync", lambda s, c=c, par=par: s.dma_start(
                out=nq_out["im"][c * 128:(c + 1) * 128, :], in_=pim[par][:]),
                inc=sto.inc(16))
            sto_pr[par] = sto.n

        prog.op("sync", lambda s: s.dma_start(out=ptot_out[:], in_=ptot[:]),
                waits=[(vec, vec.n)], inc=sto.inc(16))
        prog.op("sync", lambda s: None, waits=[(sto, sto.n)])
        p5.close()
        p4.close()
        prog.finalize()
    return nc



def build_launch_C():
    nc = bass.Bass("TRN2", target_bir_lowering=False)
    dram = lambda n, sh, dt=F32, kind="ExternalInput": nc.dram_tensor(
        n, sh, dt, kind=kind).ap()
    y1_in = dram("y1_in", [D, T])
    pr_in = {"re": dram("prre_in", [FPAD, T], BF16),
             "im": dram("prim_in", [FPAD, T], BF16)}
    nq_in = {"re": dram("nqre_in", [FPAD, T], BF16),
             "im": dram("nqim_in", [FPAD, T], BF16)}
    pta_in = dram("ptotall", [8, 1280])
    mask1_in = dram("mask1", [128, 80])
    cb_in = dram("cb_v", [128, 4])
    ic_in = dram("icmat", [FPAD, D], BF16)
    is_in = dram("ismat", [FPAD, D], BF16)
    w1_in = dram("w1T", [D, H1])
    w2_in = dram("w2T", [D, H1])
    h1_in = dram("h1T", [H1, H1])
    h2_in = dram("h2T", [H1, H1])
    w3_in = dram("w3T", [H1, D])
    s1_in = dram("s1_v", [128, 32])
    s2_in = dram("s2_v", [128, 32])
    h1s_in = dram("h1s_v", [128, 32])
    h2s_in = dram("h2s_v", [128, 32])
    w3s_in = dram("w3s_v", [128, 8])
    y_out = dram("y_out", [D, T], F32, "ExternalOutput")

    with contextlib.ExitStack() as stack:
        sb = lambda name, shape, dt=F32: stack.enter_context(
            nc.sbuf_tensor(name, shape, dt))
        ps = lambda name, shape: stack.enter_context(
            nc.psum_tensor(name, shape, F32))

        xb = sb("xb", [128, DCH * T], BF16)
        cb_t = sb("cb_t", [128, 4])
        s1_t = sb("s1_t", [128, 32]); s2_t = sb("s2_t", [128, 32])
        h1s_t = sb("h1s_t", [128, 32]); h2s_t = sb("h2s_t", [128, 32])
        w3s_t = sb("w3s_t", [128, 8])
        ones_col = sb("ones_col", [128, 1])
        ones_row = sb("ones_row", [1, 128])
        rowA = sb("rowA", [1, T]); rowB = sb("rowB", [1, T])
        wkS = [sb(f"wkS{i}", [128, T]) for i in range(2)]
        psum = [ps(f"pC{i}", [128, 512]) for i in range(8)]

        prog = Prog(nc, stack)
        ld = prog.sem(); vec = prog.sem(); scl = prog.sem()
        mms = prog.sem(); sto = prog.sem(); nsync = prog.sem()

        st = {"xb": xb, "wkS": wkS, "psum": psum, "rowA": rowA, "rowB": rowB,
              "ones_col": ones_col, "ones_row": ones_row,
              "cb_t": cb_t, "vec": vec, "scl": scl, "mms": mms, "nsync": nsync,
              "sto": sto}

        xf = stack.enter_context(nc.sbuf_tensor("xf", [128, DCH * T], F32))
        st["xf"] = xf

        p1 = contextlib.ExitStack()
        sb1 = lambda name, shape, dt=F32: p1.enter_context(
            nc.sbuf_tensor(name, shape, dt))
        agt1 = sb1("agt1", [128, 80])
        maskt1 = sb1("maskt1", [128, 80])
        carry1 = sb1("carry1", [128, 10])
        fkv = {"re": sb1("fkv_re", [128, FCH * T], BF16),
               "im": sb1("fkv_im", [128, FCH * T], BF16)}
        p2 = contextlib.ExitStack()
        sb2 = lambda name, shape, dt=F32: p2.enter_context(
            nc.sbuf_tensor(name, shape, dt))
        nqt = [sb2(f"nqt{i}", [128, T], BF16) for i in range(2)]
        sc1 = sb2("sc1", [128, T]); sc2 = sb2("sc2", [128, T])
        sc3 = sb2("sc3", [128, T], BF16)

        for k in range(DCH):
            prog.op("sync", lambda s, k=k: s.dma_start(
                out=xf[:, k * T:(k + 1) * T], in_=y1_in[k * 128:(k + 1) * 128, :]),
                inc=ld.inc(16))
        for (dst, srcd) in [(cb_t, cb_in), (maskt1, mask1_in), (s1_t, s1_in),
                            (s2_t, s2_in), (h1s_t, h1s_in), (h2s_t, h2s_in),
                            (w3s_t, w3s_in)]:
            prog.op("sync", lambda s, dst=dst, srcd=srcd: s.dma_start(
                out=dst[:], in_=srcd[:]), inc=ld.inc(16))
        lded0 = ld.n
        prog.op("vector", lambda v: v.memset(ones_col[:], 1.0))
        prog.op("vector", lambda v: v.memset(ones_row[:], 1.0))
        prog.op("vector", lambda v: v.memset(wkS[1][:], 0.0))
        cb1_vec = emit_carry_combine(prog, "gpsimd", agt1, maskt1, carry1, nsync,
                                     pta_in, 10, ld, vec, waits=[(ld, lded0)])

        vec_fkvq = [0] * FCH
        for c in range(FCH):
            par = c % 2
            for (part, dstc) in (("re", 0), ("im", 1)):
                prog.op("gpsimd", lambda g, c=c, part=part: g.dma_start(
                    out=fkv[part][:, c * T:(c + 1) * T],
                    in_=pr_in[part][c * 128:(c + 1) * 128, :]), inc=ld.inc(16))
            prog.op("gpsimd", lambda g, c=c: g.dma_start(
                out=nqt[0][:, :], in_=nq_in["re"][c * 128:(c + 1) * 128, :]),
                waits=[(vec, vec_fkvq[c - 1])] if c >= 1 else (), inc=ld.inc(16))
            prog.op("gpsimd", lambda g, c=c: g.dma_start(
                out=nqt[1][:, :], in_=nq_in["im"][c * 128:(c + 1) * 128, :]),
                inc=ld.inc(16))
            ld_c = ld.n
            def _scanq(v, c=c):
                fre = fkv["re"][:, c * T:(c + 1) * T]
                fim = fkv["im"][:, c * T:(c + 1) * T]
                v.tensor_tensor_scan(fre, fre, wkS[1][:, 0:T],
                                     carry1[:, 2 * c:2 * c + 1],
                                     ALU.add, ALU.add)
                v.tensor_tensor_scan(fim, fim, wkS[1][:, 0:T],
                                     carry1[:, 2 * c + 1:2 * c + 2],
                                     ALU.add, ALU.add)
                v.tensor_mul(sc1[:], fre, nqt[0][:, :])
                v.tensor_mul(sc2[:], fim, nqt[1][:, :])
                v.tensor_mul(sc3[:], fre, nqt[1][:, :])
                v.tensor_tensor(fre, sc1[:], sc2[:], ALU.add)
                v.tensor_mul(sc1[:], fim, nqt[0][:, :])
                return v.tensor_tensor(fim, sc1[:], sc3[:], ALU.subtract)
            w = [(ld, ld_c), (vec, cb1_vec)]
            prog.op("vector", _scanq, waits=w, inc=vec.inc())
            vec_fkvq[c] = vec.n
        vec_fkvq_all = vec.n

        p2.close()
        icb = sb1("icb", [128, FCH * D], BF16)
        isb = sb1("isb", [128, FCH * D], BF16)
        for c in range(FCH):
            prog.op("sync", lambda s, c=c: s.dma_start(
                out=icb[:, c * D:(c + 1) * D], in_=ic_in[c * 128:(c + 1) * 128, :]),
                waits=[(vec, vec_fkvq_all)] if c == 0 else (),
                inc=ld.inc(16))
            prog.op("sync", lambda s, c=c: s.dma_start(
                out=isb[:, c * D:(c + 1) * D], in_=is_in[c * 128:(c + 1) * 128, :]),
                inc=ld.inc(16))
        ld_ic = ld.n

        vec_z2 = 0
        for n in range(NT):
            w = [(ld, ld_ic), (vec, vec_fkvq_all)]
            if n > 0:
                w.append((vec, vec_z2))
            def _irf(t, n=n):
                last = None
                for c in range(FCH):
                    for (blkt, part) in ((icb, "re"), (isb, "im")):
                        for m in range(DCH):
                            last = t.matmul(
                                psum[m][:],
                                blkt[:, c * D + m * 128:c * D + (m + 1) * 128],
                                fkv[part][:, c * T + n * 512:c * T + (n + 1) * 512],
                                start=(c == 0 and part == "re"),
                                stop=(c == FCH - 1 and part == "im"))
                return last
            prog.op("tensor", _irf, waits=w, inc=mms.inc())
            mm_ir = mms.n

            def _z2(v, n=n):
                last = None
                for m in range(DCH):
                    sl = slice(m * T + n * 512, m * T + (n + 1) * 512)
                    last = v.scalar_tensor_tensor(
                        xf[:, sl], xf[:, sl], 2.0, psum[m][:],
                        ALU.mult, ALU.add)
                return last
            prog.op("vector", _z2, waits=[(mms, mm_ir)], inc=vec.inc())
            vec_z2 = vec.n

        ms2 = emit_layernorm(prog, st, 2, {"z_ready_vec": vec_z2,
                                           "banks_free_vec": vec_z2})
        p1.close()

        p5 = contextlib.ExitStack()
        sb5 = lambda name, shape, dt=F32: p5.enter_context(
            nc.sbuf_tensor(name, shape, dt))
        habf = sb5("habf", [128, 32 * 512], BF16)
        habf2 = sb5("habf2", [128, 32 * 512], BF16)
        wcb = sb5("wcb", [128, 4 * 1024], BF16)

        bfv = ms2["banks_busy_until_vec"]
        bfs = ms2["scl_shadow"]
        mm_wcb_free = ms2["mms_bcast"]

        def load_w4(win, row0, hb):
            for k4 in range(4):
                w = [(mms, mm_wcb_free)] if k4 == 0 else []
                prog.op("gpsimd", lambda g, k4=k4, hb=hb, win=win, row0=row0:
                        g.dma_start(
                            out=wcb[:, k4 * 1024:(k4 + 1) * 1024],
                            in_=win[(row0 + k4) * 128:(row0 + k4 + 1) * 128,
                                    hb * 1024:(hb + 1) * 1024]),
                        waits=[x for x in w if x[1] > 0], inc=ld.inc(16))
            return ld.n

        def bank_waits(extra=()):
            w = list(extra)
            if bfv:
                w.append((vec, bfv))
            if bfs:
                w.append((scl, bfs))
            return w

        for qt in range(4):
            tk = qt * 512
            for hb in range(4):
                mm_1 = 0
                for ks in range(2):
                    ld_w = load_w4(w1_in, ks * 4, hb)
                    w = [(ld, ld_w)]
                    if ks == 0:
                        w = bank_waits(w)
                    def _am(t, ks=ks, tk=tk):
                        last = None
                        for k4 in range(4):
                            k8 = ks * 4 + k4
                            for mh in range(8):
                                last = t.matmul(
                                    psum[mh][:],
                                    wcb[:, k4 * 1024 + mh * 128:
                                        k4 * 1024 + (mh + 1) * 128],
                                    xb[:, k8 * T + tk:k8 * T + tk + 512],
                                    start=(k8 == 0), stop=(k8 == 7))
                        return last
                    prog.op("tensor", _am, waits=w, inc=mms.inc())
                    mm_wcb_free = mms.n
                    mm_1 = mms.n
                for mh in range(8):
                    prog.op("scalar", lambda s, mh=mh, hb=hb: s.activation(
                        habf[:, (hb * 8 + mh) * 512:(hb * 8 + mh + 1) * 512],
                        psum[mh][:],
                        AF.Silu, scale=s1_t[:, hb * 8 + mh:hb * 8 + mh + 1]),
                        waits=[(mms, mm_1)] if mh == 0 else (),
                        inc=scl.inc() if mh == 7 else None)
                bfs = scl.n
                bfv = 0
                mm_2 = 0
                for ks in range(2):
                    ld_w = load_w4(w2_in, ks * 4, hb)
                    w = [(ld, ld_w)]
                    if ks == 0:
                        w = bank_waits(w)
                    def _bm(t, ks=ks, tk=tk):
                        last = None
                        for k4 in range(4):
                            k8 = ks * 4 + k4
                            for mh in range(8):
                                last = t.matmul(
                                    psum[mh][:],
                                    wcb[:, k4 * 1024 + mh * 128:
                                        k4 * 1024 + (mh + 1) * 128],
                                    xb[:, k8 * T + tk:k8 * T + tk + 512],
                                    start=(k8 == 0), stop=(k8 == 7))
                        return last
                    prog.op("tensor", _bm, waits=w, inc=mms.inc())
                    mm_wcb_free = mms.n
                    mm_2 = mms.n
                def _ha(v, hb=hb):
                    last = None
                    for mh in range(8):
                        sl = slice((hb * 8 + mh) * 512, (hb * 8 + mh + 1) * 512)
                        last = v.scalar_tensor_tensor(
                            habf[:, sl], psum[mh][:],
                            s2_t[:, hb * 8 + mh:hb * 8 + mh + 1],
                            habf[:, sl], ALU.mult, ALU.mult)
                    return last
                prog.op("vector", _ha, waits=[(mms, mm_2), (scl, bfs)],
                        inc=vec.inc())
                bfv = vec.n
                bfs = 0

            for (win, sclt, src_a, dst_a) in ((h1_in, h1s_t, habf, habf2),
                                              (h2_in, h2s_t, habf2, habf)):
                for hb in range(4):
                    mm_last = 0
                    for ks in range(8):
                        ld_w = load_w4(win, ks * 4, hb)
                        w = [(ld, ld_w)]
                        if ks == 0:
                            w = bank_waits(w)
                        def _hm(t, ks=ks, src_a=src_a):
                            last = None
                            for k4 in range(4):
                                kk = ks * 4 + k4
                                for mh in range(8):
                                    last = t.matmul(
                                        psum[mh][:],
                                        wcb[:, k4 * 1024 + mh * 128:
                                            k4 * 1024 + (mh + 1) * 128],
                                        src_a[:, kk * 512:(kk + 1) * 512],
                                        start=(kk == 0), stop=(kk == 31))
                            return last
                        prog.op("tensor", _hm, waits=w, inc=mms.inc())
                        mm_wcb_free = mms.n
                        mm_last = mms.n
                    for mh in range(8):
                        w_ev = [(mms, mm_last)] if mh == 0 else []
                        prog.op("scalar", lambda s, mh=mh, hb=hb, sclt=sclt,
                                dst_a=dst_a: s.activation(
                                    dst_a[:, (hb * 8 + mh) * 512:
                                          (hb * 8 + mh + 1) * 512],
                                    psum[mh][:], AF.Gelu,
                                    scale=sclt[:, hb * 8 + mh:hb * 8 + mh + 1]),
                                waits=w_ev,
                                inc=scl.inc() if mh == 7 else None)
                    bfs = scl.n
                    bfv = 0

            mm_wd_last = 0
            for ks in range(8):
                ld_w = load_w4(w3_in, ks * 4, 0)
                w = [(ld, ld_w)]
                if ks == 0:
                    w = bank_waits(w)
                def _dm(t, ks=ks):
                    last = None
                    for k4 in range(4):
                        kk = ks * 4 + k4
                        for m in range(8):
                            last = t.matmul(
                                psum[m][:],
                                wcb[:, k4 * 1024 + m * 128:
                                    k4 * 1024 + (m + 1) * 128],
                                habf[:, kk * 512:(kk + 1) * 512],
                                start=(kk == 0), stop=(kk == 31))
                    return last
                prog.op("tensor", _dm, waits=w, inc=mms.inc())
                mm_wcb_free = mms.n
                mm_wd_last = mms.n
            def _z3(v, tk=tk):
                last = None
                for m in range(8):
                    sl = slice(m * T + tk, m * T + tk + 512)
                    last = v.scalar_tensor_tensor(
                        xf[:, sl], psum[m][:], w3s_t[:, m:m + 1], xf[:, sl],
                        ALU.mult, ALU.add)
                return last
            prog.op("vector", _z3, waits=[(mms, mm_wd_last)], inc=vec.inc())
            bfv = vec.n
            bfs = 0
        vec_z3_all = vec.n

        ms3 = emit_layernorm(prog, st, 2, {"z_ready_vec": vec_z3_all,
                                           "banks_free_vec": vec_z3_all})
        for m in range(DCH):
            prog.op("sync", lambda s, m=m: s.dma_start(
                out=y_out[m * 128:(m + 1) * 128, :],
                in_=xf[:, m * T:(m + 1) * T]),
                waits=[(vec, ms3["vec_apply"])] if m == 0 else (),
                inc=sto.inc(16))
        prog.op("sync", lambda s: None, waits=[(sto, sto.n)])

        p5.close()
        prog.finalize()
    return nc



def build_fused():
    nc = bass.Bass("TRN2", target_bir_lowering=False)
    dram = lambda n, sh, dt=F32, kind="ExternalInput": nc.dram_tensor(
        n, sh, dt, kind=kind).ap()
    x_in = dram("x_fm", [D, T])
    gwT_in = dram("gwT", [D, D])
    a_in = dram("a_v", [128, 8])
    b_in = dram("b_v", [128, 8])
    gb_in = dram("gb_v", [128, 8])
    csel_in = dram("csel_v", [128, 4])
    cb_in = dram("cb_v", [128, 4])
    mask0_in = dram("mask0", [128, 64])
    mask1_in = dram("mask1", [128, 80])
    c_in = dram("c_v", [128, 8])
    d_in = dram("d_v", [128, 8])
    wg_in = dram("wgT_i", [D, H0], I32)
    wu_in = dram("wuT_i", [D, H0], I32)
    wd_in = dram("wdT_i", [H0, D], I32)
    w_qkv_in = {"q": dram("wq_r", [D, D]), "k": dram("wk_r", [D, D]),
                "v": dram("wv_r", [D, D])}
    cmat_in = dram("cmat", [D, FPAD], BF16)
    smat_in = dram("smat", [D, FPAD], BF16)
    ic_in = dram("icmat", [FPAD, D], BF16)
    is_in = dram("ismat", [FPAD, D], BF16)
    w1_in = dram("w1T", [D, H1])
    w2_in = dram("w2T", [D, H1])
    h1_in = dram("h1T", [H1, H1])
    h2_in = dram("h2T", [H1, H1])
    w3_in = dram("w3T", [H1, D])
    s1_in = dram("s1_v", [128, 32])
    s2_in = dram("s2_v", [128, 32])
    h1s_in = dram("h1s_v", [128, 32])
    h2s_in = dram("h2s_v", [128, 32])
    w3s_in = dram("w3s_v", [128, 8])
    y_out = dram("y_out", [D, T], F32, "ExternalOutput")
    intl = lambda n, sh, dt=F32: nc.dram_tensor(n, sh, dt, kind="Internal").ap()
    v_out = intl("v_scr", [D, T])
    pows_out = intl("pows_scr", [D, T])
    v_in = v_out
    pows_in = pows_out
    y1_out = intl("y1_scr", [D, T])
    y1_in = y1_out
    pr_out = {"re": intl("prre_scr", [FPAD, T], BF16),
              "im": intl("prim_scr", [FPAD, T], BF16)}
    nq_out = {"re": intl("nqre_scr", [FPAD, T], BF16),
              "im": intl("nqim_scr", [FPAD, T], BF16)}
    pr_in = pr_out
    nq_in = nq_out
    ccin0 = intl("ccin0", [1, D])
    ccout0 = intl("ccout0", [8, D])
    ccin1 = intl("ccin1", [1, 1280])
    ccout1 = intl("ccout1", [8, 1280])
    vta_in = ccout0
    pta_in = ccout1
    mscr = {}
    for nm in ["qre", "qim", "kre", "kim", "vre", "vim"]:
        mscr[nm] = nc.dram_tensor("mscr_" + nm, [D, FPAD], BF16,
                                  kind="Internal").ap()
    wbf = {
        "w1": nc.dram_tensor("w1bf", [4, 8, 128, 1024], BF16, kind="Internal").ap(),
        "w2": nc.dram_tensor("w2bf", [4, 8, 128, 1024], BF16, kind="Internal").ap(),
        "h1": nc.dram_tensor("h1bf", [4, 32, 128, 1024], BF16, kind="Internal").ap(),
        "h2": nc.dram_tensor("h2bf", [4, 32, 128, 1024], BF16, kind="Internal").ap(),
        "w3": nc.dram_tensor("w3bf", [1, 32, 128, 1024], BF16, kind="Internal").ap(),
    }
    wgbf = nc.dram_tensor("wgbf", [3, 8, 128, 1024], BF16, kind="Internal").ap()
    wubf = nc.dram_tensor("wubf", [3, 8, 128, 1024], BF16, kind="Internal").ap()
    wdbf = nc.dram_tensor("wdbf", [24, 128, 1024], BF16, kind="Internal").ap()

    with contextlib.ExitStack() as stack:
        psb = lambda name, shape, dt=F32: stack.enter_context(
            nc.sbuf_tensor(name, shape, dt))
        ps = lambda name, shape: stack.enter_context(
            nc.psum_tensor(name, shape, F32))
        xb = psb("xb", [128, DCH * T], BF16)
        cb_t = psb("cb_t", [128, 4])
        ones_col = psb("ones_col", [128, 1])
        ones_row = psb("ones_row", [1, 128])
        rowA = psb("rowA", [1, T])
        rowB = psb("rowB", [1, T])
        wkS = [psb(f"wkS{i}", [128, T]) for i in range(2)]
        c_t = psb("c_t", [128, 8])
        d1_t = psb("d1_t", [128, 8])
        a_t = psb("a_t", [128, 8])
        b_t = psb("b_t", [128, 8])
        gb_t = psb("gb_t", [128, 8])
        csel_t = psb("csel_t", [128, 4])
        scA = {n: psb("sc_" + n, [128, 8]) for n in TANH_SCRATCH}
        mskA = psb("msk", [128, 8], I32)
        a_raw = psb("a_raw", [128, 8])
        asafe = psb("asafe", [128, 8])
        carry = psb("carry", [128, 8])
        vtot = psb("vtot", [128, 8])
        psum = [ps(f"pF{i}", [128, 512]) for i in range(8)]

        prog = Prog(nc, stack)
        ld = prog.sem(); vec = prog.sem(); scl = prog.sem()
        mms = prog.sem(); sto = prog.sem(); nsync = prog.sem()
        ccs = prog.sem()
        ccs1 = prog.sem()
        wconv = prog.sem()

        conv_ops = []
        for (key, win_, nhb, nk) in (("w1", w1_in, 4, 8), ("w2", w2_in, 4, 8),
                                     ("h1", h1_in, 4, 32), ("h2", h2_in, 4, 32),
                                     ("w3", w3_in, 1, 32)):
            for hb_ in range(nhb):
                for k4_ in range(0, nk, 4):
                    def _cv(g, key=key, win_=win_, hb_=hb_, k4_=k4_):
                        return g.dma_start(
                            out=wbf[key][hb_, k4_:k4_ + 4].rearrange(
                                "k p c -> (k p) c"),
                            in_=win_[k4_ * 128:(k4_ + 4) * 128,
                                     hb_ * 1024:(hb_ + 1) * 1024])
                    conv_ops.append(_cv)
        N_CONV = len(conv_ops)

        def drip(n):
            for _ in range(min(n, len(conv_ops))):
                prog.op("gpsimd", conv_ops.pop(0), inc=wconv.inc(16))

        st = {"xb": xb, "wkS": wkS, "psum": psum, "rowA": rowA, "rowB": rowB,
              "ones_col": ones_col, "ones_row": ones_row,
              "cb_t": cb_t, "vec": vec, "scl": scl, "mms": mms, "nsync": nsync,
              "sto": sto}

        p_xf = contextlib.ExitStack()
        xf = p_xf.enter_context(nc.sbuf_tensor("xf", [128, DCH * T], F32))
        st["xf"] = xf

        for k in range(DCH):
            prog.op("sync", lambda s, k=k: s.dma_start(
                out=xf[:, k * T:(k + 1) * T], in_=x_in[k * 128:(k + 1) * 128, :]),
                inc=ld.inc(16))
        prog.op("sync", lambda s: s.dma_start(out=cb_t[:], in_=cb_in[:]),
                inc=ld.inc(16))

        pA = contextlib.ExitStack()
        sb = lambda name, shape, dt=F32: pA.enter_context(
            nc.sbuf_tensor(name, shape, dt))
        gw = sb("gw", [128, DCH * D], BF16)
        sc = scA
        msk = mskA
        zeros = sb("zeros", [128, T])
        _pw0 = sb("powstA", [128, T])
        _gs0 = sb("gstripA", [128, T])
        _vs0 = sb("vstripA", [128, T])
        powst = [_pw0, _pw0]
        gstrip = [_gs0, _gs0]
        vstrip = [_vs0, _vs0]
        wk1 = sb("wk1A", [128, T])
        wk2 = sb("wk2A", [128, T])

        for k in range(DCH):
            prog.op("gpsimd", lambda g, k=k: g.dma_start(
                out=xb[:, k * T:(k + 1) * T], in_=x_in[k * 128:(k + 1) * 128, :]),
                inc=ld.inc(16))
            prog.op("gpsimd", lambda g, k=k: g.dma_start(
                out=gw[:, k * D:(k + 1) * D], in_=gwT_in[k * 128:(k + 1) * 128, :]),
                inc=ld.inc(16))
        for (dst, src) in [(a_t, a_in), (b_t, b_in), (gb_t, gb_in),
                           (csel_t, csel_in)]:
            prog.op("sync", lambda s, dst=dst, src=src: s.dma_start(
                out=dst[:], in_=src[:]), inc=ld.inc(16))
        lded = ld.n

        wcv_g = [0] * 3
        wcv_u = [0] * 3
        for blk in range(3):
            for (win_, dst, rec) in ((wg_in, wgbf, wcv_g),
                                     (wu_in, wubf, wcv_u)):
                for k4 in range(0, DCH, 4):
                    prog.op("gpsimd", lambda g, win_=win_, dst=dst, blk=blk,
                            k4=k4: g.dma_start(
                                out=dst[blk, k4:k4 + 4].rearrange(
                                    "k p c -> (k p) c"),
                                in_=win_[k4 * 128:(k4 + 4) * 128,
                                         blk * 1024:(blk + 1) * 1024]),
                            inc=wconv.inc(16))
                rec[blk] = wconv.n
        for kh4 in range(0, 24, 4):
            prog.op("gpsimd", lambda g, kh4=kh4: g.dma_start(
                out=wdbf[kh4:kh4 + 4].rearrange("k p c -> (k p) c"),
                in_=wd_in[kh4 * 128:(kh4 + 4) * 128, :]),
                inc=wconv.inc(16))
        wcv_d = wconv.n

        prog.op("vector", lambda v: v.memset(zeros[:], 0.0))
        def _tanh(v):
            emit_tanh(SyncedEngine(v, nsync), sc, a_t, a_raw, msk)
            return v.engine_nop()
        prog.op("vector", _tanh, waits=[(ld, lded)])

        def _asafe(v0):
            v = SyncedEngine(v0, nsync)
            t1, t2 = sc["tmp"], sc["s"]
            v.tensor_scalar(t1[:], a_raw[:], -1.0, None, ALU.mult)
            v.tensor_max(t1[:], t1[:], a_raw[:])
            v.tensor_scalar(t1[:], t1[:], 1e-4, None, ALU.max)
            v.tensor_single_scalar(t2[:], a_raw[:], 0.0, ALU.is_lt)
            v.tensor_scalar(t2[:], t2[:], -2.0, 1.0, ALU.mult, ALU.add)
            v.tensor_mul(asafe[:], t1[:], t2[:])
            return v0.engine_nop()
        prog.op("vector", _asafe)

        def _carries(v0):
            v = SyncedEngine(v0, nsync)
            p2, p4, p6, t1 = sc["p"], sc["q"], sc["y"], sc["tmp"]
            v.tensor_copy(p2[:], asafe[:])
            for _ in range(11):
                v.tensor_mul(p2[:], p2[:], p2[:])
            v.tensor_mul(p4[:], p2[:], p2[:])
            v.tensor_mul(p6[:], p4[:], p2[:])
            v.tensor_scalar(carry[:], zeros[:, 0:8], csel_t[:, 0:1], None, ALU.add)
            v.scalar_tensor_tensor(t1[:], p2[:], csel_t[:, 1:2], carry[:],
                                   ALU.mult, ALU.add)
            v.scalar_tensor_tensor(carry[:], p4[:], csel_t[:, 2:3], t1[:],
                                   ALU.mult, ALU.add)
            v.scalar_tensor_tensor(t1[:], p6[:], csel_t[:, 3:4], carry[:],
                                   ALU.mult, ALU.add)
            v.tensor_copy(carry[:], t1[:])
            return v0.engine_nop()
        prog.op("vector", _carries, inc=vec.inc())
        vec_init = vec.n

        vec_pow = [0] * DCH; vec_v = [0] * DCH
        scl_evac = [0] * DCH; scl_rch = [0] * DCH
        mm_done = [0] * DCH
        sto_pow = [0] * DCH; sto_v = [0] * DCH

        for m in range(DCH):
            par = m % 2
            pw, gs, vs = powst[par], gstrip[par], vstrip[par]
            xf_m = xf[:, m * T:(m + 1) * T]

            w = [(vec, vec_init)]
            if m >= 1:
                w += [(scl, scl_rch[m - 1]), (sto, sto_pow[m - 1])]
            def _powscan(v, pw=pw, m=m):
                v.tensor_scalar(wk1[:], zeros[:], asafe[:, m:m + 1], None, ALU.add)
                v.tensor_copy(pw[:, 0:1], carry[:, m:m + 1])
                return v.tensor_tensor_scan(
                    pw[:, 1:T], wk1[:, 1:T], zeros[:, 1:T],
                    carry[:, m:m + 1], ALU.mult, ALU.add)
            prog.op("vector", _powscan, waits=w, inc=vec.inc())
            vec_pow[m] = vec.n

            w = [(ld, lded)]
            if m >= 1:
                w.append((scl, scl_evac[m - 1]))
                w.append((vec, vec_v[m - 1]))
            def _gatemm(t, m=m, par=par):
                last = None
                for n in range(NT):
                    bank = psum[4 * par + n]
                    for k in range(DCH):
                        last = t.matmul(
                            bank[:],
                            gw[:, k * D + m * 128: k * D + (m + 1) * 128],
                            xb[:, k * T + n * 512: k * T + (n + 1) * 512],
                            start=(k == 0), stop=(k == DCH - 1))
                return last
            prog.op("tensor", _gatemm, waits=w, inc=mms.inc())
            mm_done[m] = mms.n

            for n in range(NT):
                w = []
                if n == 0:
                    w.append((mms, mm_done[m]))
                    if m >= 1:
                        w.append((vec, vec_v[m - 1]))
                prog.op("scalar", lambda s, n=n, par=par, gs=gs, m=m: s.activation(
                    gs[:, n * 512:(n + 1) * 512], psum[4 * par + n][:],
                    AF.Sigmoid, bias=gb_t[:, m:m + 1]),
                    waits=w, inc=scl.inc() if n == NT - 1 else None)
            scl_evac[m] = scl.n

            def _rchain(s, pw=pw):
                s.activation(wk2[:], pw[:], AF.Abs, bias=cb_t[:, 0:1])
                s.activation(wk2[:], wk2[:], AF.Ln)
                s.activation(wk2[:], wk2[:], AF.Exp, scale=-1.0)
                return s.sign(wk1[:], pw[:], bias=cb_t[:, 0:1])
            prog.op("scalar", _rchain, waits=[(vec, vec_pow[m])], inc=scl.inc())
            scl_rch[m] = scl.n

            w = [(scl, scl_rch[m])]
            if m >= 1:
                w.append((sto, sto_v[m - 1]))
            def _vchain(v, gs=gs, vs=vs, m=m, xf_m=xf_m):
                v.tensor_mul(wk2[:], wk2[:], wk1[:])
                v.scalar_tensor_tensor(wk1[:], gs[:], b_t[:, m:m + 1], xf_m,
                                       ALU.mult, ALU.mult)
                return v.scalar_tensor_tensor(
                    vs[:], wk1[:], 1.0, wk2[:], ALU.mult, ALU.mult,
                    accum_out=vtot[:, m:m + 1])
            prog.op("vector", _vchain, waits=w, inc=vec.inc())
            vec_v[m] = vec.n

            prog.op("sync", lambda s, pw=pw, m=m: s.dma_start(
                out=pows_out[m * 128:(m + 1) * 128, :], in_=pw[:]),
                waits=[(vec, vec_pow[m])], inc=sto.inc(16))
            sto_pow[m] = sto.n
            prog.op("sync", lambda s, vs=vs, m=m: s.dma_start(
                out=v_out[m * 128:(m + 1) * 128, :], in_=vs[:]),
                waits=[(vec, vec_v[m])], inc=sto.inc(16))
            sto_v[m] = sto.n

        A_vec_end = vec.n
        prog.op("gpsimd", lambda g: g.dma_start(
            out=ccin0.rearrange("one (k p) -> p (one k)", p=128), in_=vtot[:],
            allow_slow_non_contiguous=True),
            waits=[(vec, A_vec_end)], inc=ld.inc(16))
        ld_cc0 = ld.n
        prog.op("gpsimd", lambda g: g.collective_compute(
            "AllGather", ALU.bypass,
            replica_groups=[[0, 1, 2, 3, 4, 5, 6, 7]],
            ins=[ccin0.opt()], outs=[ccout0.opt()]),
            waits=[(ld, ld_cc0)], inc=ccs.inc(None))
        A_sto_end = sto.n
        A_mm_end = mms.n
        A_scl_end = scl.n
        A_ld_end = ld.n
        prog.set_barrier([(ld, ld.n), (vec, vec.n), (scl, scl.n),
                          (mms, mms.n), (sto, sto.n)])
        prog.set_barrier([(ld, ld.n), (vec, vec.n), (scl, scl.n),
                          (mms, mms.n), (sto, sto.n), (ccs, 1)])

        pA.close()

        p1 = contextlib.ExitStack()
        sb1 = lambda name, shape, dt=F32: p1.enter_context(
            nc.sbuf_tensor(name, shape, dt))
        vst = [sb1(f"vst{i}", [128, T]) for i in range(2)]
        pwst = [sb1(f"pwst{i}", [128, T]) for i in range(2)]
        agt = sb1("agt", [128, 64])
        maskt = sb1("maskt", [128, 64])
        carry0 = sb1("carry0", [128, 8])

        first_small = [True]
        for (dst, srcd) in [(c_t, c_in), (d1_t, d_in),
                            (maskt, mask0_in)]:
            w = [(vec, A_vec_end)] if first_small[0] else []
            first_small[0] = False
            prog.op("sync", lambda s, dst=dst, srcd=srcd: s.dma_start(
                out=dst[:], in_=srcd[:]), waits=w, inc=ld.inc(16))
        lded0 = ld.n

        prog.op("vector", lambda v: v.memset(ones_col[:], 1.0))
        prog.op("vector", lambda v: v.memset(ones_row[:], 1.0))
        prog.op("vector", lambda v: v.memset(wkS[1][:], 0.0))

        def _d1(v0):
            v = SyncedEngine(v0, nsync)
            v.tensor_scalar(d1_t[:], d1_t[:], 1.0, None, ALU.add)
            return v0.engine_nop()
        prog.op("vector", _d1, waits=[(ld, lded0)], inc=vec.inc())

        cb_vec = emit_carry_combine(prog, "gpsimd", agt, maskt, carry0, nsync,
                                    vta_in, 8, ld, vec,
                                    waits=[(ld, lded0), (ccs, 1),
                                           (ld, ld_cc0)])

        vec_z0 = [0] * DCH
        for m in range(DCH):
            par = m % 2
            vs, pw = vst[par], pwst[par]
            w = []
            if m >= 2:
                w.append((vec, vec_z0[m - 2]))
            if m == 0:
                w += [(vec, A_vec_end), (sto, A_sto_end), (ld, ld_cc0),
                      (scl, A_scl_end), (mms, A_mm_end)]
            prog.op("gpsimd", lambda g, m=m, vs=vs: g.dma_start(
                out=vs[:], in_=v_in[m * 128:(m + 1) * 128, :]),
                waits=w, inc=ld.inc(16))
            prog.op("gpsimd", lambda g, m=m, pw=pw: g.dma_start(
                out=pw[:], in_=pows_in[m * 128:(m + 1) * 128, :]),
                inc=ld.inc(16))
            ld_m = ld.n

            def _scan_z0(v, m=m, vs=vs, pw=pw):
                v.tensor_tensor_scan(vs[:], vs[:], wkS[1][:, 0:T],
                                     carry0[:, m:m + 1], ALU.add, ALU.add)
                v.scalar_tensor_tensor(wkS[0][:], vs[:], c_t[:, m:m + 1], pw[:],
                                       ALU.mult, ALU.mult)
                return v.scalar_tensor_tensor(
                    xf[:, m * T:(m + 1) * T], xf[:, m * T:(m + 1) * T],
                    d1_t[:, m:m + 1], wkS[0][:], ALU.mult, ALU.add)
            prog.op("vector", _scan_z0, waits=[(ld, ld_m), (vec, cb_vec)],
                    inc=vec.inc())
            vec_z0[m] = vec.n

        prog.set_barrier([(ld, ld.n), (vec, vec.n), (scl, scl.n),
                          (mms, mms.n), (sto, sto.n)])
        ms0 = emit_layernorm(prog, st, 2, {"z_ready_vec": vec_z0[DCH - 1]})
        p1.close()

        prog.set_barrier([(ld, ld.n), (vec, vec.n), (scl, scl.n),
                          (mms, mms.n), (sto, sto.n)])
        p3 = contextlib.ExitStack()
        sb3 = lambda name, shape, dt=F32: p3.enter_context(
            nc.sbuf_tensor(name, shape, dt))
        wblkb = [sb3(f"wblk{i}", [128, DCH * 1024], BF16) for i in range(2)]
        gsil = sb3("gsil", [128, 8 * 512], BF16)
        hh = sb3("hh", [128, 24 * 512], BF16)
        wdb = [sb3(f"wdb{i}", [128, 1024], BF16) for i in range(2)]

        vec_banks_free = ms0["banks_busy_until_vec"]
        scl_entry = ms0["scl_shadow"]
        vec_z1_all = 0
        mm_step = {}
        ld_step = {}
        prev_mm_d = 0
        for qt in range(4):
            tk = qt * 512
            vec_hh = 0
            for blk in range(3):
                s_g = (qt * 3 + blk) * 2
                s_u = s_g + 1
                wbg = wblkb[s_g % 2]
                wbu = wblkb[s_u % 2]
                for k in range(DCH):
                    w = []
                    if k == 0:
                        if s_g >= 1:
                            w.append((mms, mm_step[s_g - 1]))
                        if qt == 0:
                            w.append((wconv, wcv_g[blk]))
                    prog.op("gpsimd", lambda g, k=k, blk=blk, wbg=wbg:
                            g.dma_start(
                                out=wbg[:, k * 1024:(k + 1) * 1024],
                                in_=wgbf[blk, k]),
                            waits=w, inc=ld.inc(16))
                ld_step[s_g] = ld.n

                w = [(ld, ld_step[s_g]), (vec, vec_banks_free)]
                if qt == 0 and blk == 0:
                    w.append((scl, scl_entry))

                def _gmm(t, tk=tk, wbg=wbg):
                    last = None
                    for k in range(DCH):
                        for mh in range(8):
                            last = t.matmul(
                                psum[mh][:],
                                wbg[:, k * 1024 + mh * 128:
                                    k * 1024 + (mh + 1) * 128],
                                xb[:, k * T + tk:k * T + tk + 512],
                                start=(k == 0), stop=(k == DCH - 1))
                    return last
                prog.op("tensor", _gmm, waits=w, inc=mms.inc())
                mm_step[s_g] = mms.n
                mm_g = mms.n
                for mh in range(8):
                    prog.op("scalar", lambda s, mh=mh: s.activation(
                        gsil[:, mh * 512:(mh + 1) * 512],
                        psum[mh][:], AF.Silu, scale=1.0 / 127.0),
                        waits=[(mms, mm_g)] if mh == 0 else (),
                        inc=scl.inc() if mh == 7 else None)
                scl_gate = scl.n
                for k in range(DCH):
                    w = []
                    if k == 0:
                        w.append((mms, mm_step[s_u - 1]))
                        if qt == 0:
                            w.append((wconv, wcv_u[blk]))
                    prog.op("gpsimd", lambda g, k=k, blk=blk, wbu=wbu:
                            g.dma_start(
                                out=wbu[:, k * 1024:(k + 1) * 1024],
                                in_=wubf[blk, k]),
                            waits=w, inc=ld.inc(16))
                ld_step[s_u] = ld.n

                def _umm(t, tk=tk, wbu=wbu):
                    last = None
                    for k in range(DCH):
                        for mh in range(8):
                            last = t.matmul(
                                psum[mh][:],
                                wbu[:, k * 1024 + mh * 128:
                                    k * 1024 + (mh + 1) * 128],
                                xb[:, k * T + tk:k * T + tk + 512],
                                start=(k == 0), stop=(k == DCH - 1))
                    return last
                prog.op("tensor", _umm,
                        waits=[(ld, ld_step[s_u]), (scl, scl_gate)],
                        inc=mms.inc())
                mm_step[s_u] = mms.n
                mm_u = mms.n

                def _hh(v, blk=blk):
                    last = None
                    for mh in range(8):
                        last = v.scalar_tensor_tensor(
                            hh[:, (blk * 8 + mh) * 512:(blk * 8 + mh + 1) * 512],
                            psum[mh][:], 1.0 / 127.0,
                            gsil[:, mh * 512:(mh + 1) * 512],
                            ALU.mult, ALU.mult)
                    return last
                prog.op("vector", _hh, waits=[(mms, mm_u)], inc=vec.inc())
                vec_hh = vec.n
                vec_banks_free = vec.n

            mm_wd = [0] * 24
            for kh in range(24):
                wdt = wdb[kh % 2]
                if kh >= 2:
                    w = [(mms, mm_wd[kh - 2])]
                elif qt > 0:
                    w = [(mms, mm_step[s_u])]
                else:
                    w = [(wconv, wcv_d)] if kh == 0 else []
                prog.op("gpsimd", lambda g, kh=kh, wdt=wdt: g.dma_start(
                    out=wdt[:], in_=wdbf[kh]),
                    waits=[x for x in w if x[1] > 0], inc=ld.inc(16))
                ld_kh = ld.n
                w = [(ld, ld_kh)]
                if kh == 0:
                    w.append((vec, vec_hh))

                def _dk(t, kh=kh, wdt=wdt):
                    last = None
                    for m in range(DCH):
                        last = t.matmul(
                            psum[m][:],
                            wdt[:, m * 128:(m + 1) * 128],
                            hh[:, kh * 512:(kh + 1) * 512],
                            start=(kh == 0), stop=(kh == 23))
                    return last
                prog.op("tensor", _dk, waits=w, inc=mms.inc())
                mm_wd[kh] = mms.n
            mm_d = mms.n

            def _z1(v, tk=tk):
                last = None
                for m in range(DCH):
                    sl = slice(m * T + tk, m * T + tk + 512)
                    last = v.scalar_tensor_tensor(
                        xf[:, sl], psum[m][:], 1.0 / 127.0, xf[:, sl],
                        ALU.mult, ALU.add)
                return last
            prog.op("vector", _z1, waits=[(mms, mm_d)], inc=vec.inc())
            vec_banks_free = vec.n
            prev_mm_d = mm_d
            drip(22)
        vec_z1_all = vec.n
        prog.set_barrier([(ld, ld.n), (vec, vec.n), (scl, scl.n),
                          (mms, mms.n), (sto, sto.n)])
        ms1 = emit_layernorm(prog, st, 2, {"z_ready_vec": vec_z1_all,
                                           "banks_free_vec": vec_z1_all})
        sto_y1 = 0
        for m in range(DCH):
            prog.op("sync", lambda s, m=m: s.dma_start(
                out=y1_out[m * 128:(m + 1) * 128, :],
                in_=xf[:, m * T:(m + 1) * T]),
                waits=[(vec, ms1["vec_apply"])] if m == 0 else (),
                inc=sto.inc(16))
        sto_y1 = sto.n
        p3.close()
        p_xf.close()

        prog.set_barrier([(ld, ld.n), (vec, vec.n), (scl, scl.n),
                          (mms, mms.n), (sto, sto.n)])
        p4 = contextlib.ExitStack()
        sb4 = lambda name, shape, dt=F32: p4.enter_context(
            nc.sbuf_tensor(name, shape, dt))
        wfull = sb4("wfull", [128, DCH * 1024], BF16)
        cfull = sb4("cfull", [128, DCH * FPAD], BF16)
        sfull = sb4("sfull", [128, DCH * FPAD], BF16)
        mstag = sb4("mstag", [128, DCH * FPAD], BF16)

        for e in range(DCH):
            prog.op("sync", lambda s, e=e: s.dma_start(
                out=cfull[:, e * FPAD:(e + 1) * FPAD],
                in_=cmat_in[e * 128:(e + 1) * 128, :]),
                waits=[(sto, sto_y1)] if e == 0 else (), inc=ld.inc(16))
            prog.op("sync", lambda s, e=e: s.dma_start(
                out=sfull[:, e * FPAD:(e + 1) * FPAD],
                in_=smat_in[e * 128:(e + 1) * 128, :]), inc=ld.inc(16))
        ld_cs = ld.n

        evac_scl_prev = ms1["scl_shadow"]
        bank_free_vec = ms1["banks_busy_until_vec"]
        sto_mscr = 0
        for mat in ("k", "v", "q"):
            for e in range(DCH):
                prog.op("gpsimd", lambda g, e=e, mat=mat: g.dma_start(
                    out=wfull[:, e * 1024:(e + 1) * 1024],
                    in_=w_qkv_in[mat][e * 128:(e + 1) * 128, :]),
                    waits=[(mms, mms.n), (sto, sto_y1)] if e == 0 else (),
                    inc=ld.inc(16))
            ld_w = ld.n
            drip(8)
            for (part, csrc) in (("re", cfull), ("im", sfull)):
                scr = mscr[mat + part]
                for (ftw, ft0) in ((512, 0), (128, 512)):
                    w = [(ld, ld_w), (ld, ld_cs), (scl, evac_scl_prev),
                         (vec, bank_free_vec)]
                    def _mmm(t, csrc=csrc, ftw=ftw, ft0=ft0):
                        last = None
                        for e in range(DCH):
                            for m in range(DCH):
                                last = t.matmul(
                                    psum[m][0:128, 0:ftw],
                                    wfull[:, e * 1024 + m * 128:
                                          e * 1024 + (m + 1) * 128],
                                    csrc[:, e * FPAD + ft0:e * FPAD + ft0 + ftw],
                                    start=(e == 0), stop=(e == DCH - 1))
                        return last
                    prog.op("tensor", _mmm, waits=w, inc=mms.inc())
                    mm_m = mms.n
                    for m in range(DCH):
                        w = []
                        if m == 0:
                            w = [(mms, mm_m)]
                            if sto_mscr:
                                w.append((sto, sto_mscr))
                        prog.op("scalar", lambda s, m=m, ft0=ft0, ftw=ftw:
                                s.activation(
                                    mstag[:, m * FPAD + ft0:m * FPAD + ft0 + ftw],
                                    psum[m][0:128, 0:ftw], AF.Copy),
                                waits=w,
                                inc=scl.inc() if m == DCH - 1 else None)
                    evac_scl_prev = scl.n
                    bank_free_vec = 0
                for m in range(DCH):
                    prog.op("sync", lambda s, m=m, scr=scr: s.dma_start(
                        out=scr[m * 128:(m + 1) * 128, :],
                        in_=mstag[:, m * FPAD:(m + 1) * FPAD]),
                        waits=[(scl, evac_scl_prev)] if m == 0 else (),
                        inc=sto.inc(16))
                sto_mscr = sto.n
        p4.close()

        prog.set_barrier([(ld, ld.n), (vec, vec.n), (scl, scl.n),
                          (mms, mms.n), (sto, sto.n)])
        p5 = contextlib.ExitStack()
        sb5 = lambda name, shape, dt=F32: p5.enter_context(
            nc.sbuf_tensor(name, shape, dt))
        At = sb5("At", [128, T]); Bt = sb5("Bt", [128, T])
        Ct = sb5("Ct", [128, T]); Dt = sb5("Dt", [128, T])
        wf1 = sb5("wf1", [128, T]); wf2 = sb5("wf2", [128, T])
        scr1 = sb5("scr1", [128, T]); scr2 = sb5("scr2", [128, T])
        pre = [sb5(f"pre{i}", [128, T], BF16) for i in range(2)]
        pim = [sb5(f"pim{i}", [128, T], BF16) for i in range(2)]
        mk1 = sb5("mk1", [128, DCH * 128], BF16)
        mk2 = sb5("mk2", [128, DCH * 128], BF16)
        ptot = sb5("ptot", [128, 10])

        def proj_mms(tag, c, blk_re, blk_im, bank0_wait_scl, extra_waits=()):
            w = list(extra_waits) + [(scl, bank0_wait_scl)]
            def _p(t, c=c, blk_re=blk_re, blk_im=blk_im):
                last = None
                for n in range(NT):
                    for k in range(DCH):
                        last = t.matmul(
                            psum[n][:],
                            blk_re[:, k * 128:(k + 1) * 128],
                            xb[:, k * T + n * 512:k * T + (n + 1) * 512],
                            start=(k == 0), stop=(k == DCH - 1))
                for n in range(NT):
                    for k in range(DCH):
                        last = t.matmul(
                            psum[4 + n][:],
                            blk_im[:, k * 128:(k + 1) * 128],
                            xb[:, k * T + n * 512:k * T + (n + 1) * 512],
                            start=(k == 0), stop=(k == DCH - 1))
                return last
            prog.op("tensor", _p, waits=[x for x in w if x[1] > 0], inc=mms.inc())
            return mms.n

        def evac8(dst_re, dst_im, mm_wait, extra=()):
            for n in range(NT):
                w = ([(mms, mm_wait)] + [x for x in extra if x[1] > 0]) \
                    if n == 0 else ()
                prog.op("scalar", lambda s, n=n, dst_re=dst_re: s.activation(
                    dst_re[:, n * 512:(n + 1) * 512], psum[n][:], AF.Copy),
                    waits=w)
            for n in range(NT):
                prog.op("scalar", lambda s, n=n, dst_im=dst_im: s.activation(
                    dst_im[:, n * 512:(n + 1) * 512], psum[4 + n][:], AF.Copy),
                    inc=scl.inc() if n == NT - 1 else None)
            return scl.n

        def load_mblks(names, c, dst1, dst2, waits0):
            for (nm, dstt) in ((names[0], dst1), (names[1], dst2)):
                for k in range(DCH):
                    w = [x for x in waits0 if x[1] > 0] \
                        if (k == 0 and nm == names[0]) else ()
                    prog.op("gpsimd", lambda g, nm=nm, k=k, c=c, dstt=dstt:
                            g.dma_start(
                                out=dstt[:, k * 128:(k + 1) * 128],
                                in_=mscr[nm][k * 128:(k + 1) * 128,
                                             c * 128:(c + 1) * 128]),
                            waits=w, inc=ld.inc(16))
            return ld.n

        def normalize(mm_evac_scl, vec_guard):
            w = [(scl, mm_evac_scl)]
            if vec_guard:
                w.append((vec, vec_guard))
            prog.op("scalar", lambda s: s.activation(wf1[:], At[:], AF.Square),
                    waits=w)
            prog.op("scalar", lambda s: s.activation(wf2[:], Bt[:], AF.Square),
                    inc=scl.inc())
            s_sq = scl.n
            prog.op("vector", lambda v: v.tensor_add(wf1[:], wf1[:], wf2[:]),
                    waits=[(scl, s_sq)], inc=vec.inc())
            v_add = vec.n
            prog.op("scalar", lambda s: s.activation(wf2[:], wf1[:], AF.Sqrt),
                    waits=[(vec, v_add)])
            prog.op("scalar", lambda s: s.activation(
                wf2[:], wf2[:], AF.Ln, bias=cb_t[:, 1:2]))
            prog.op("scalar", lambda s: s.activation(
                wf2[:], wf2[:], AF.Exp, scale=-1.0), inc=scl.inc())
            s_rn = scl.n
            def _nk(v):
                v.tensor_mul(At[:], At[:], wf2[:])
                return v.tensor_mul(Bt[:], Bt[:], wf2[:])
            prog.op("vector", _nk, waits=[(scl, s_rn)], inc=vec.inc())
            return vec.n

        vec_work_last = 0
        scl_evac_prev = evac_scl_prev
        sto_pr = [0, 0]
        mm_blk_last = 0
        for c in range(FCH):
            par = c % 2
            ld_k = load_mblks(("kre", "kim"), c, mk1, mk2,
                              [(sto, sto_mscr), (mms, mm_blk_last)])
            mm_k = proj_mms("k", c, mk1, mk2, scl_evac_prev,
                            extra_waits=[(ld, ld_k)])
            mm_blk_last = mm_k
            scl_evac_prev = evac8(At, Bt, mm_k,
                                  extra=[(vec, vec_work_last)])
            ld_v = load_mblks(("vre", "vim"), c, mk1, mk2,
                              [(mms, mm_k)])
            mm_v = proj_mms("v", c, mk1, mk2, scl_evac_prev,
                            extra_waits=[(ld, ld_v)])
            mm_blk_last = mm_v
            scl_evac_prev = evac8(Ct, Dt, mm_v)
            v_nk = normalize(scl_evac_prev, vec_work_last)
            def _prods(v, c=c, par=par):
                v.tensor_mul(scr1[:], At[:], Ct[:])
                v.tensor_mul(scr2[:], At[:], Dt[:])
                v.tensor_mul(Ct[:], Bt[:], Ct[:])
                v.tensor_mul(Dt[:], Bt[:], Dt[:])
                v.scalar_tensor_tensor(pre[par][:], scr1[:], 1.0, Dt[:],
                                       ALU.mult, ALU.subtract,
                                       accum_out=ptot[:, 2 * c:2 * c + 1])
                return v.scalar_tensor_tensor(
                    pim[par][:], scr2[:], 1.0, Ct[:], ALU.mult, ALU.add,
                    accum_out=ptot[:, 2 * c + 1:2 * c + 2])
            w = [(vec, v_nk)]
            if sto_pr[par]:
                w.append((sto, sto_pr[par]))
            prog.op("vector", _prods, waits=w, inc=vec.inc())
            vec_work_last = vec.n
            prog.op("sync", lambda s, c=c, par=par: s.dma_start(
                out=pr_out["re"][c * 128:(c + 1) * 128, :], in_=pre[par][:]),
                waits=[(vec, vec_work_last)], inc=sto.inc(16))
            prog.op("sync", lambda s, c=c, par=par: s.dma_start(
                out=pr_out["im"][c * 128:(c + 1) * 128, :], in_=pim[par][:]),
                inc=sto.inc(16))
            sto_pr[par] = sto.n

        for c in range(FCH):
            par = c % 2
            ld_q = load_mblks(("qre", "qim"), c, mk1, mk2,
                              [(mms, mm_blk_last)])
            mm_q = proj_mms("q", c, mk1, mk2, scl_evac_prev,
                            extra_waits=[(ld, ld_q)])
            mm_blk_last = mm_q
            scl_evac_prev = evac8(At, Bt, mm_q,
                                  extra=[(vec, vec_work_last)])
            v_nq = normalize(scl_evac_prev, vec_work_last)
            def _cast(v, par=par):
                v.tensor_copy(pre[par][:], At[:])
                return v.tensor_copy(pim[par][:], Bt[:])
            w = [(vec, v_nq)]
            if sto_pr[par]:
                w.append((sto, sto_pr[par]))
            prog.op("vector", _cast, waits=w, inc=vec.inc())
            vec_work_last = vec.n
            prog.op("sync", lambda s, c=c, par=par: s.dma_start(
                out=nq_out["re"][c * 128:(c + 1) * 128, :], in_=pre[par][:]),
                waits=[(vec, vec_work_last)], inc=sto.inc(16))
            prog.op("sync", lambda s, c=c, par=par: s.dma_start(
                out=nq_out["im"][c * 128:(c + 1) * 128, :], in_=pim[par][:]),
                inc=sto.inc(16))
            sto_pr[par] = sto.n

        B_vec_end = vec.n
        prog.op("gpsimd", lambda g: g.dma_start(
            out=ccin1.rearrange("one (k p) -> p (one k)", p=128), in_=ptot[:],
            allow_slow_non_contiguous=True),
            waits=[(vec, B_vec_end)], inc=ld.inc(16))
        ld_cc1 = ld.n
        prog.op("gpsimd", lambda g: g.collective_compute(
            "AllGather", ALU.bypass,
            replica_groups=[[0, 1, 2, 3, 4, 5, 6, 7]],
            ins=[ccin1.opt()], outs=[ccout1.opt()]),
            waits=[(ld, ld_cc1)], inc=ccs1.inc(None))
        sto_after_B = sto.n
        mm_after_B = mms.n
        p5.close()
        p4.close()

        sb = lambda name, shape, dt=F32: stack.enter_context(
            nc.sbuf_tensor(name, shape, dt))
        s1_t = sb("s1_t", [128, 32]); s2_t = sb("s2_t", [128, 32])
        h1s_t = sb("h1s_t", [128, 32]); h2s_t = sb("h2s_t", [128, 32])
        w3s_t = sb("w3s_t", [128, 8])

        xfc = stack.enter_context(nc.sbuf_tensor("xf2", [128, DCH * T], F32))
        st["xf"] = xfc

        prog.set_barrier([(ld, ld.n), (vec, vec.n), (scl, scl.n),
                          (mms, mms.n), (sto, sto.n), (ccs1, 1)])
        p1 = contextlib.ExitStack()
        sb1 = lambda name, shape, dt=F32: p1.enter_context(
            nc.sbuf_tensor(name, shape, dt))
        agt1 = sb1("agt1", [128, 80])
        maskt1 = sb1("maskt1", [128, 80])
        carry1 = sb1("carry1", [128, 10])
        fkv = {"re": sb1("fkv_re", [128, FCH * T], BF16),
               "im": sb1("fkv_im", [128, FCH * T], BF16)}
        p2 = contextlib.ExitStack()
        sb2 = lambda name, shape, dt=F32: p2.enter_context(
            nc.sbuf_tensor(name, shape, dt))
        nqt = [sb2(f"nqt{i}", [128, T], BF16) for i in range(2)]
        sc1 = sb2("sc1", [128, T]); sc2 = sb2("sc2", [128, T])
        sc3 = sb2("sc3", [128, T], BF16)

        for k in range(DCH):
            prog.op("sync", lambda s, k=k: s.dma_start(
                out=xfc[:, k * T:(k + 1) * T], in_=y1_in[k * 128:(k + 1) * 128, :]),
                waits=[(sto, sto_y1), (mms, mm_after_B)] if k == 0 else (),
                inc=ld.inc(16))
        for (dst, srcd) in [(maskt1, mask1_in), (s1_t, s1_in),
                            (s2_t, s2_in), (h1s_t, h1s_in), (h2s_t, h2s_in),
                            (w3s_t, w3s_in)]:
            prog.op("sync", lambda s, dst=dst, srcd=srcd: s.dma_start(
                out=dst[:], in_=srcd[:]), inc=ld.inc(16))
        lded0 = ld.n
        prog.op("vector", lambda v: v.memset(wkS[1][:], 0.0),
                waits=[(mms, mm_after_B)])
        cb1_vec = emit_carry_combine(prog, "gpsimd", agt1, maskt1, carry1, nsync,
                                     pta_in, 10, ld, vec,
                                     waits=[(ld, lded0), (ccs1, 1),
                                            (ld, ld_cc1)])

        vec_fkvq = [0] * FCH
        for c in range(FCH):
            par = c % 2
            for (part, dstc) in (("re", 0), ("im", 1)):
                prog.op("gpsimd", lambda g, c=c, part=part: g.dma_start(
                    out=fkv[part][:, c * T:(c + 1) * T],
                    in_=pr_in[part][c * 128:(c + 1) * 128, :]),
                    waits=[(sto, sto_after_B)] if (c == 0 and part == "re")
                    else (), inc=ld.inc(16))
            prog.op("gpsimd", lambda g, c=c: g.dma_start(
                out=nqt[0][:, :], in_=nq_in["re"][c * 128:(c + 1) * 128, :]),
                waits=[(vec, vec_fkvq[c - 1])] if c >= 1 else (), inc=ld.inc(16))
            prog.op("gpsimd", lambda g, c=c: g.dma_start(
                out=nqt[1][:, :], in_=nq_in["im"][c * 128:(c + 1) * 128, :]),
                inc=ld.inc(16))
            ld_c = ld.n
            def _scanq(v, c=c):
                fre = fkv["re"][:, c * T:(c + 1) * T]
                fim = fkv["im"][:, c * T:(c + 1) * T]
                v.tensor_tensor_scan(fre, fre, wkS[1][:, 0:T],
                                     carry1[:, 2 * c:2 * c + 1],
                                     ALU.add, ALU.add)
                v.tensor_tensor_scan(fim, fim, wkS[1][:, 0:T],
                                     carry1[:, 2 * c + 1:2 * c + 2],
                                     ALU.add, ALU.add)
                v.tensor_mul(sc1[:], fre, nqt[0][:, :])
                v.tensor_mul(sc2[:], fim, nqt[1][:, :])
                v.tensor_mul(sc3[:], fre, nqt[1][:, :])
                v.tensor_tensor(fre, sc1[:], sc2[:], ALU.add)
                v.tensor_mul(sc1[:], fim, nqt[0][:, :])
                return v.tensor_tensor(fim, sc1[:], sc3[:], ALU.subtract)
            w = [(ld, ld_c), (vec, cb1_vec)]
            prog.op("vector", _scanq, waits=w, inc=vec.inc())
            vec_fkvq[c] = vec.n
        vec_fkvq_all = vec.n

        prog.set_barrier([(ld, ld.n), (vec, vec.n), (scl, scl.n),
                          (mms, mms.n), (sto, sto.n)])
        p2.close()
        icb = sb1("icb", [128, FCH * D], BF16)
        isb = sb1("isb", [128, FCH * D], BF16)
        for c in range(FCH):
            prog.op("sync", lambda s, c=c: s.dma_start(
                out=icb[:, c * D:(c + 1) * D], in_=ic_in[c * 128:(c + 1) * 128, :]),
                waits=[(vec, vec_fkvq_all)] if c == 0 else (),
                inc=ld.inc(16))
            prog.op("sync", lambda s, c=c: s.dma_start(
                out=isb[:, c * D:(c + 1) * D], in_=is_in[c * 128:(c + 1) * 128, :]),
                inc=ld.inc(16))
        ld_ic = ld.n

        vec_z2 = 0
        for n in range(NT):
            w = [(ld, ld_ic), (vec, vec_fkvq_all)]
            if n > 0:
                w.append((vec, vec_z2))
            def _irf(t, n=n):
                last = None
                for c in range(FCH):
                    for (blkt, part) in ((icb, "re"), (isb, "im")):
                        for m in range(DCH):
                            last = t.matmul(
                                psum[m][:],
                                blkt[:, c * D + m * 128:c * D + (m + 1) * 128],
                                fkv[part][:, c * T + n * 512:c * T + (n + 1) * 512],
                                start=(c == 0 and part == "re"),
                                stop=(c == FCH - 1 and part == "im"))
                return last
            prog.op("tensor", _irf, waits=w, inc=mms.inc())
            mm_ir = mms.n

            def _z2(v, n=n):
                last = None
                for m in range(DCH):
                    sl = slice(m * T + n * 512, m * T + (n + 1) * 512)
                    last = v.scalar_tensor_tensor(
                        xfc[:, sl], xfc[:, sl], 2.0, psum[m][:],
                        ALU.mult, ALU.add)
                return last
            prog.op("vector", _z2, waits=[(mms, mm_ir)], inc=vec.inc())
            vec_z2 = vec.n

        prog.set_barrier([(ld, ld.n), (vec, vec.n), (scl, scl.n),
                          (mms, mms.n), (sto, sto.n)])
        ms2 = emit_layernorm(prog, st, 2, {"z_ready_vec": vec_z2,
                                           "banks_free_vec": vec_z2})
        p1.close()

        prog.set_barrier([(ld, ld.n), (vec, vec.n), (scl, scl.n),
                          (mms, mms.n), (sto, sto.n)])
        p5 = contextlib.ExitStack()
        sb5 = lambda name, shape, dt=F32: p5.enter_context(
            nc.sbuf_tensor(name, shape, dt))
        habf = sb5("habf", [128, 32 * 512], BF16)
        habf2 = sb5("habf2", [128, 32 * 512], BF16)
        wcb2 = [sb5("wcbA", [128, 2048], BF16), sb5("wcbB", [128, 2048], BF16)]

        drip(99)
        bfv = ms2["banks_busy_until_vec"]
        bfs = ms2["scl_shadow"]
        wcb_state = {"par": 0,
                     "guard": [ms2["mms_bcast"], ms2["mms_bcast"]],
                     "first": True}

        def bank_waits(extra=()):
            w = list(extra)
            if bfv:
                w.append((vec, bfv))
            if bfs:
                w.append((scl, bfs))
            return w

        def ffn1_stage(key, hb, in_ch, src_fn, entry=False):
            mm_last = 0
            for g2 in range(in_ch // 2):
                par = wcb_state["par"] % 2
                wcb_state["par"] += 1
                w = [(mms, wcb_state["guard"][par])]
                if wcb_state["first"]:
                    w.append((wconv, wconv.n))
                    wcb_state["first"] = False
                for j in range(2):
                    prog.op("gpsimd", lambda g, key=key, hb=hb, g2=g2, j=j,
                            par=par: g.dma_start(
                                out=wcb2[par][:, j * 1024:(j + 1) * 1024],
                                in_=wbf[key][hb, g2 * 2 + j]),
                            waits=[x for x in w if x[1] > 0] if j == 0 else (),
                            inc=ld.inc(16))
                ld_g = ld.n
                w = [(ld, ld_g)]
                if g2 == 0 and entry:
                    w = bank_waits(w)

                def _mm(t, g2=g2, par=par, in_ch=in_ch, src_fn=src_fn):
                    last = None
                    for j in range(2):
                        kk = g2 * 2 + j
                        for mh in range(8):
                            last = t.matmul(
                                psum[mh][:],
                                wcb2[par][:, j * 1024 + mh * 128:
                                          j * 1024 + (mh + 1) * 128],
                                src_fn(kk),
                                start=(kk == 0), stop=(kk == in_ch - 1))
                    return last
                prog.op("tensor", _mm, waits=w, inc=mms.inc())
                wcb_state["guard"][par] = mms.n
                mm_last = mms.n
            return mm_last

        for qt in range(4):
            tk = qt * 512
            for hb in range(4):
                mm_1 = ffn1_stage("w1", hb, 8,
                                  lambda kk, tk=tk: xb[:, kk * T + tk:
                                                       kk * T + tk + 512],
                                  entry=True)
                for mh in range(8):
                    prog.op("scalar", lambda s, mh=mh, hb=hb: s.activation(
                        habf[:, (hb * 8 + mh) * 512:(hb * 8 + mh + 1) * 512],
                        psum[mh][:],
                        AF.Silu, scale=s1_t[:, hb * 8 + mh:hb * 8 + mh + 1]),
                        waits=[(mms, mm_1)] if mh == 0 else (),
                        inc=scl.inc() if mh == 7 else None)
                bfs = scl.n
                bfv = 0
                mm_2 = ffn1_stage("w2", hb, 8,
                                  lambda kk, tk=tk: xb[:, kk * T + tk:
                                                       kk * T + tk + 512],
                                  entry=True)

                def _ha(v, hb=hb):
                    last = None
                    for mh in range(8):
                        sl = slice((hb * 8 + mh) * 512, (hb * 8 + mh + 1) * 512)
                        last = v.scalar_tensor_tensor(
                            habf[:, sl], psum[mh][:],
                            s2_t[:, hb * 8 + mh:hb * 8 + mh + 1],
                            habf[:, sl], ALU.mult, ALU.mult)
                    return last
                prog.op("vector", _ha, waits=[(mms, mm_2), (scl, bfs)],
                        inc=vec.inc())
                bfv = vec.n
                bfs = 0

            for (key, sclt, src_a, dst_a) in (("h1", h1s_t, habf, habf2),
                                              ("h2", h2s_t, habf2, habf)):
                for hb in range(4):
                    mm_last = ffn1_stage(key, hb, 32,
                                         lambda kk, src_a=src_a: src_a[
                                             :, kk * 512:(kk + 1) * 512],
                                         entry=True)
                    for mh in range(8):
                        prog.op("scalar", lambda s, mh=mh, hb=hb, sclt=sclt,
                                dst_a=dst_a: s.activation(
                                    dst_a[:, (hb * 8 + mh) * 512:
                                          (hb * 8 + mh + 1) * 512],
                                    psum[mh][:], AF.Gelu,
                                    scale=sclt[:, hb * 8 + mh:hb * 8 + mh + 1]),
                                waits=[(mms, mm_last)] if mh == 0 else (),
                                inc=scl.inc() if mh == 7 else None)
                    bfs = scl.n
                    bfv = 0

            mm_wd_last = ffn1_stage("w3", 0, 32,
                                    lambda kk: habf[:, kk * 512:(kk + 1) * 512],
                                    entry=True)

            def _z3(v, tk=tk):
                last = None
                for m in range(8):
                    sl = slice(m * T + tk, m * T + tk + 512)
                    last = v.scalar_tensor_tensor(
                        xfc[:, sl], psum[m][:], w3s_t[:, m:m + 1], xfc[:, sl],
                        ALU.mult, ALU.add)
                return last
            prog.op("vector", _z3, waits=[(mms, mm_wd_last)], inc=vec.inc())
            bfv = vec.n
            bfs = 0
        vec_z3_all = vec.n

        prog.set_barrier([(ld, ld.n), (vec, vec.n), (scl, scl.n),
                          (mms, mms.n), (sto, sto.n)])
        ms3 = emit_layernorm(prog, st, 2, {"z_ready_vec": vec_z3_all,
                                           "banks_free_vec": vec_z3_all})
        for m in range(DCH):
            prog.op("sync", lambda s, m=m: s.dma_start(
                out=y_out[m * 128:(m + 1) * 128, :],
                in_=xfc[:, m * T:(m + 1) * T]),
                waits=[(vec, ms3["vec_apply"])] if m == 0 else (),
                inc=sto.inc(16))
        prog.op("sync", lambda s: None, waits=[(sto, sto.n)])

        p5.close()

        prog.finalize()
    return nc



def _pack128(vec):
    return np.ascontiguousarray(np.asarray(vec).reshape(-1, 128).T).astype(np.float32)


def _dft_consts():
    e = np.arange(D, dtype=np.float64)[:, None]
    f = np.arange(FPAD, dtype=np.float64)[None, :]
    ang = 2.0 * np.pi * e * f / D
    C = np.cos(ang)
    S = -np.sin(ang)
    C[:, 513:] = 0.0
    S[:, 513:] = 0.0
    return (C.astype(ml_dtypes.bfloat16), S.astype(ml_dtypes.bfloat16))


def _idft_consts():
    f = np.arange(FPAD, dtype=np.float64)[:, None]
    d = np.arange(D, dtype=np.float64)[None, :]
    w = np.where((f == 0) | (f == 512), 1.0, 2.0)
    w[f >= 513] = 0.0
    ang = 2.0 * np.pi * f * d / D
    IC = w * np.cos(ang) / D
    IS = -w * np.sin(ang) / D
    return (IC.astype(ml_dtypes.bfloat16), IS.astype(ml_dtypes.bfloat16))


def _prep_inputs(inputs):
    x = inputs["L0_attn_x"]
    cmat, smat = _dft_consts()
    icmat, ismat = _idft_consts()
    shared = {
        "gwT": np.ascontiguousarray(inputs["L0_attn_gate_W"].T),
        "a_v": _pack128(inputs["L0_attn_a_logit"]),
        "b_v": _pack128(inputs["L0_attn_b"]),
        "c_v": _pack128(inputs["L0_attn_c"]),
        "d_v": _pack128(inputs["L0_attn_d"]),
        "gb_v": _pack128(inputs["L0_attn_gate_b"]),
        "cb_v": np.ascontiguousarray(
            np.broadcast_to(np.array([1e-12, 1e-8, 1e-5, 0.0], np.float32),
                            (128, 4))),
        "wgT_i": np.ascontiguousarray(inputs["L0_ffn_w_gate"].T),
        "wuT_i": np.ascontiguousarray(inputs["L0_ffn_w_up"].T),
        "wdT_i": np.ascontiguousarray(inputs["L0_ffn_w_down"].T),
        "wq_r": np.ascontiguousarray(inputs["L1_attn_Wq"]),
        "wk_r": np.ascontiguousarray(inputs["L1_attn_Wk"]),
        "wv_r": np.ascontiguousarray(inputs["L1_attn_Wv"]),
        "cmat": cmat,
        "smat": smat,
        "icmat": icmat,
        "ismat": ismat,
        "w1T": np.ascontiguousarray(inputs["L1_ffn_w1_w"].T),
        "w2T": np.ascontiguousarray(inputs["L1_ffn_w2_w"].T),
        "h1T": np.ascontiguousarray(inputs["L1_ffn_h1_w"].T),
        "h2T": np.ascontiguousarray(inputs["L1_ffn_h2_w"].T),
        "w3T": np.ascontiguousarray(inputs["L1_ffn_w3_w"].T),
        "s1_v": _pack128(inputs["L1_ffn_w1_s"]),
        "s2_v": _pack128(inputs["L1_ffn_w2_s"]),
        "h1s_v": _pack128(inputs["L1_ffn_h1_s"]),
        "h2s_v": _pack128(inputs["L1_ffn_h2_s"]),
        "w3s_v": _pack128(inputs["L1_ffn_w3_s"]),
    }
    per_core = []
    for c in range(NCORES):
        bidx, t0 = c // 4, 2048 * (c % 4)
        csel = np.zeros((128, 4), np.float32)
        csel[:, c % 4] = 1.0
        mask8 = np.zeros(8, np.float32)
        mask8[bidx * 4:c] = 1.0
        per_core.append({
            "x_fm": np.ascontiguousarray(x[bidx, t0:t0 + T, :].T),
            "csel_v": csel,
            "mask0": np.ascontiguousarray(
                np.broadcast_to(np.tile(mask8, 8), (128, 64))).astype(np.float32),
            "mask1": np.ascontiguousarray(
                np.broadcast_to(np.tile(mask8, 10), (128, 80))).astype(np.float32),
        })
    return shared, per_core


_A_KEYS = ["x_fm", "gwT", "a_v", "b_v", "gb_v", "csel_v", "cb_v"]
_B_KEYS = ["x_fm", "c_v", "d_v", "cb_v", "mask0", "wgT_i", "wuT_i", "wdT_i",
           "wq_r", "wk_r", "wv_r", "cmat", "smat"]
_C_KEYS = ["cb_v", "mask1", "icmat", "ismat", "w1T", "w2T", "h1T", "h2T", "w3T",
           "s1_v", "s2_v", "h1s_v", "h2s_v", "w3s_v"]


def _maps(shared, per_core, keys):
    out = []
    for c in range(NCORES):
        m = {}
        for k in keys:
            m[k] = shared[k] if k in shared else per_core[c][k]
        out.append(m)
    return out


_F_KEYS = ["x_fm", "gwT", "a_v", "b_v", "gb_v", "csel_v", "cb_v", "mask0",
           "mask1", "c_v", "d_v", "wgT_i", "wuT_i", "wdT_i", "wq_r", "wk_r",
           "wv_r", "cmat", "smat", "icmat", "ismat", "w1T", "w2T", "h1T",
           "h2T", "w3T", "s1_v", "s2_v", "h1s_v", "h2s_v", "w3s_v"]

FUSED = True


def kernel(**inputs):
    shared, per_core = _prep_inputs(inputs)
    if FUSED:
        ncF = build_fused()
        mapsF = _maps(shared, per_core, _F_KEYS)
        resF = run_bass_kernel_spmd(ncF, mapsF, core_ids=list(range(NCORES)))
        out = np.empty((2, 8192, D), np.float32)
        for c in range(NCORES):
            bidx, t0 = c // 4, 2048 * (c % 4)
            out[bidx, t0:t0 + T, :] = np.asarray(resF.results[c]["y_out"]).T
        return out
    ncA = build_launch_A()
    mapsA = _maps(shared, per_core, _A_KEYS)
    resA = run_bass_kernel_spmd(ncA, mapsA, core_ids=list(range(NCORES)))
    vta = np.stack([np.asarray(resA.results[c]["vtot_out"]).T.reshape(-1)
                    for c in range(NCORES)])
    ncB = build_launch_B()
    mapsB = _maps(shared, per_core, _B_KEYS)
    for c in range(NCORES):
        mapsB[c]["vtotall"] = vta
        mapsB[c]["v_in"] = np.asarray(resA.results[c]["v_out"])
        mapsB[c]["pows_in"] = np.asarray(resA.results[c]["pows_out"])
    resB = run_bass_kernel_spmd(ncB, mapsB, core_ids=list(range(NCORES)))
    pta = np.stack([np.asarray(resB.results[c]["ptot_out"]).T.reshape(-1)
                    for c in range(NCORES)])
    ncC = build_launch_C()
    mapsC = _maps(shared, per_core, _C_KEYS)
    for c in range(NCORES):
        mapsC[c]["ptotall"] = pta
        mapsC[c]["y1_in"] = np.asarray(resB.results[c]["y1_out"])
        for (kc, kb) in [("prre_in", "prre_out"), ("prim_in", "prim_out"),
                         ("nqre_in", "nqre_out"), ("nqim_in", "nqim_out")]:
            arr = np.asarray(resB.results[c][kb])
            if arr.dtype != ml_dtypes.bfloat16:
                arr = arr.view(ml_dtypes.bfloat16)
            mapsC[c][kc] = arr
    resC = run_bass_kernel_spmd(ncC, mapsC, core_ids=list(range(NCORES)))
    out = np.empty((2, 8192, D), np.float32)
    for c in range(NCORES):
        bidx, t0 = c // 4, 2048 * (c % 4)
        out[bidx, t0:t0 + T, :] = np.asarray(resC.results[c]["y_out"]).T
    return out
